# version 05
# baseline (speedup 1.0000x reference)
import contextlib
import numpy as np
import ml_dtypes
import concourse.bass as bass
import concourse.mybir as mybir
from concourse.bass_utils import run_bass_kernel_spmd

F32 = mybir.dt.float32
BF16 = mybir.dt.bfloat16
AF = mybir.ActivationFunctionType
ALU = mybir.AluOpType
AX = mybir.AxisListType

TOK = 2048
NCT = 16
DM = 1024
NKH = 8
NH = 16
HD = 64
E = 8
FF = 4096
SPAN = 512
NSP = FF // SPAN
NFJ = SPAN // 128
NSPANS = E * NSP
NCH = 2
CTOK = 1024
NTT = 8
EPS_LN = 1e-5
EPS_DEN = 1e-6


class Sem:
    def __init__(self, es, nc, name):
        self.h = es.enter_context(nc.semaphore(name))
        self.n = 0


class B:

    def __init__(self):
        self.ops = {k: [] for k in ("pe", "act", "dve", "sync", "gp")}

    def em(self, eng, fn):
        self.ops[eng].append(fn)

    def w(self, eng, sem, val):
        if val > 0:
            self.ops[eng].append(lambda e, s=sem.h, v=val: e.wait_ge(s, v))


def build_nc():
    nc = bass.Bass("TRN2", target_bir_lowering=False)
    xs = nc.dram_tensor("xs", [TOK, DM], F32, kind="ExternalInput").ap()
    w_attn = nc.dram_tensor("w_attn", [3 * DM, DM], F32, kind="ExternalInput").ap()
    w_proj = nc.dram_tensor("w_proj", [DM, DM], F32, kind="ExternalInput").ap()
    gate_w = nc.dram_tensor("gate_w", [E, DM], F32, kind="ExternalInput").ap()
    w1 = nc.dram_tensor("w1", [E, DM, FF], F32, kind="ExternalInput").ap()
    w3 = nc.dram_tensor("w3", [E, DM, FF], F32, kind="ExternalInput").ap()
    w2 = nc.dram_tensor("w2", [E, FF, DM], F32, kind="ExternalInput").ap()
    ident = nc.dram_tensor("ident", [128, 128], BF16, kind="ExternalInput").ap()
    maskT = nc.dram_tensor("maskT", [128, 128], BF16, kind="ExternalInput").ap()
    onesc = nc.dram_tensor("onesc", [128, 1], BF16, kind="ExternalInput").ap()
    wsel = nc.dram_tensor("wsel", [64, 8], F32, kind="ExternalInput").ap()
    yout = nc.dram_tensor("yout", [TOK, DM], F32, kind="ExternalOutput").ap()
    wattn_bf = nc.dram_tensor("wattn_bf", [3 * DM, DM], BF16, kind="Internal").ap()
    w1bf = nc.dram_tensor("w1bf", [E, DM, FF], BF16, kind="Internal").ap()
    w3bf = nc.dram_tensor("w3bf", [E, DM, FF], BF16, kind="Internal").ap()
    w2bf = nc.dram_tensor("w2bf", [E, FF, DM], BF16, kind="Internal").ap()
    wp_bf = nc.dram_tensor("wp_bf", [DM, DM], BF16, kind="Internal").ap()
    cc_in = nc.dram_tensor("cc_in", [64, NH * 65], F32, kind="Internal").ap()
    cc_out = nc.dram_tensor("cc_out", [8 * 64, NH * 65], F32, kind="Internal").ap()

    es = contextlib.ExitStack()
    sb = lambda n, s, d: es.enter_context(nc.sbuf_tensor(n, s, d))
    ps = lambda n, s, d: es.enter_context(nc.psum_tensor(n, s, d))
    with es:
        xn_fm = sb("xn_fm", [128, NKH * TOK], BF16)
        wreg = sb("wreg", [128, 32768], BF16)
        y_acc = sb("y_acc", [128, NTT * DM], F32)
        combs = sb("combs", [128, NCT * E], F32)
        saug = sb("saug", [64, NH * 65], F32)
        saugb = sb("saugb", [64, NH * 65], BF16)
        spart = sb("spart", [64, NH * 65], F32)
        sgath = sb("sgath", [64, NH * 65], F32)
        wselb = sb("wselb", [64, 8], F32)
        xsb = sb("xsb", [128, 2 * DM], F32)
        xn_tm = sb("xn_tm", [128, DM], BF16)
        lns = sb("lns", [128, 24], F32)
        gtmp = sb("gtmp", [128, 32], F32)
        gw8 = sb("gw8", [16, DM], BF16)
        gwT = sb("gwT", [128, NKH * E], BF16)
        identb = sb("identb", [128, 128], BF16)
        maskb = sb("maskb", [128, 128], BF16)
        onesb = sb("onesb", [128, 1], BF16)
        kp_tm = sb("kp_tm", [128, DM], BF16)
        v_tm = sb("v_tm", [128, DM], BF16)
        qp_tm = sb("qp_tm", [128, DM], BF16)
        qp_fm = sb("qp_fm", [64, NH * 128], BF16)
        kp_fm = sb("kp_fm", [64, NH * 128], BF16)
        y_tm = sb("y_tm", [128, DM], BF16)
        y_fm = sb("y_fm", [128, DM], BF16)
        tmin = sb("tmin", [128, 2 * 512], F32)
        texp = sb("texp", [128, 2 * 512], F32)
        msc = sb("msc", [128, 2 * 128], BF16)
        sil = sb("sil", [128, 2 * 512], BF16)
        gbuf = sb("gbuf", [128, 2 * 8 * 512], BF16)

        def w1b(slot, kh, fcol0, n):
            off = slot * (NKH * SPAN) + kh * SPAN + fcol0
            return wreg[:, off:off + n]

        def w3b(slot, kh, fcol0, n):
            off = 8192 + slot * (NKH * SPAN) + kh * SPAN + fcol0
            return wreg[:, off:off + n]

        def w2b(slot, kf, hcol0, n):
            off = 16384 + slot * (NFJ * DM) + kf * DM + hcol0
            return wreg[:, off:off + n]

        def wqT(kh, c0, n):
            return wreg[:, kh * 3072 + c0:kh * 3072 + c0 + n]

        def wqTA(kh, c0, n):
            return wreg[:, 16384 + kh * 2048 + (c0 - 1024):16384 + kh * 2048 + (c0 - 1024) + n]

        def wpT(kh, c0, n):
            return wreg[:, 24576 + kh * DM + c0:24576 + kh * DM + c0 + n]

        b0 = ps("b0", [128, 512], F32)
        b1 = ps("b1", [128, 512], F32)
        py = ps("py", [128, 1024], F32)
        b2 = py[:, 0:512]
        b3 = py[:, 512:1024]
        b4 = ps("b4", [128, 512], F32)
        b5 = ps("b5", [128, 512], F32)
        b6 = ps("b6", [128, 1024], BF16)
        b7 = ps("b7", [128, 512], F32)

        d_x = [Sem(es, nc, "d_x0"), Sem(es, nc, "d_x1")]
        d_yx = Sem(es, nc, "d_yx")
        d_w = [Sem(es, nc, "d_w0"), Sem(es, nc, "d_w1")]
        d_w2 = [Sem(es, nc, "d_w2a"), Sem(es, nc, "d_w2b")]
        d_wq = Sem(es, nc, "d_wq")
        d_ms = Sem(es, nc, "d_ms")
        d_cast = Sem(es, nc, "d_cast")
        d_ct = Sem(es, nc, "d_ct")
        d_cs = Sem(es, nc, "d_cs")
        s_cc = Sem(es, nc, "s_cc")
        s_pe = Sem(es, nc, "s_pe")
        s_ac = Sem(es, nc, "s_ac")
        s_dv = Sem(es, nc, "s_dv")
        d_st = Sem(es, nc, "d_st")

        P = B()

        def pe_inc():
            s_pe.n += 1
            return s_pe.n

        def ac_inc():
            s_ac.n += 1
            return s_ac.n

        def dv_inc():
            s_dv.n += 1
            return s_dv.n

        def mm_group(bank_ap_fn, pairs, extra_first_wait=None):
            n = len(pairs)
            for i, (lf, rf) in enumerate(pairs):
                def mk(t, lf=lf, rf=rf, st=(i == 0), sp=(i == n - 1), bf=bank_ap_fn):
                    ins = t.matmul(bf(), lf(), rf(), start=st, stop=sp)
                    if sp:
                        ins.then_inc(s_pe.h, 1)
                P.em("pe", mk)
            return pe_inc()

        P.em("sync", lambda s: s.dma_start(out=identb[:], in_=ident).then_inc(d_ct.h, 16)); d_ct.n += 16
        P.em("sync", lambda s: s.dma_start(out=maskb[:], in_=maskT).then_inc(d_ct.h, 16)); d_ct.n += 16
        P.em("sync", lambda s: s.dma_start(out=onesb[:], in_=onesc).then_inc(d_ct.h, 16)); d_ct.n += 16
        P.em("sync", lambda s: s.dma_start(out=wselb[:], in_=wsel).then_inc(d_ct.h, 16)); d_ct.n += 16
        consts_rdy = d_ct.n

        P.em("gp", lambda g: g.dma_start(out=wattn_bf[:], in_=w_attn).then_inc(d_ms.h, 16)); d_ms.n += 16
        P.em("gp", lambda g: g.dma_start(out=gw8[0:8, :], in_=gate_w).then_inc(d_ms.h, 16)); d_ms.n += 16
        P.em("gp", lambda g: g.dma_start(out=wp_bf[:], in_=w_proj).then_inc(d_ms.h, 16)); d_ms.n += 16
        p0_gp = d_ms.n

        cast_done = {}

        def cast_expert(e_):
            if e_ in cast_done or e_ >= E:
                return
            def mk(g, e_=e_):
                g.dma_start(out=w1bf[e_], in_=w1[e_]).then_inc(d_cast.h, 16)
                g.dma_start(out=w3bf[e_], in_=w3[e_]).then_inc(d_cast.h, 16)
                g.dma_start(out=w2bf[e_], in_=w2[e_]).then_inc(d_cast.h, 16)
            P.em("gp", mk)
            d_cast.n += 48
            cast_done[e_] = d_cast.n

        cast_expert(0)
        cast_expert(1)

        P.em("dve", lambda v: v.memset(saug[:], 0.0).then_inc(s_dv.h, 1)); dv_inc()
        P.em("dve", lambda v: v.memset(spart[:], 0.0).then_inc(s_dv.h, 1)); spart_zero = dv_inc()

        P.w("pe", d_ms, p0_gp)
        P.w("pe", d_ct, consts_rdy)
        for kh in range(NKH):
            def mk(t, kh=kh, sp=(kh == NKH - 1)):
                ins = t.transpose(b6[:, kh * 8:kh * 8 + 8], gw8[0:8, kh * 128:(kh + 1) * 128], identb[0:8, 0:8])
                if sp:
                    ins.then_inc(s_pe.h, 1)
            P.em("pe", mk)
        gwt_tr = pe_inc()
        P.w("dve", s_pe, gwt_tr)
        P.em("dve", lambda v: v.tensor_copy(gwT[:], b6[:, 0:NKH * E]).then_inc(s_dv.h, 1))
        gwt_rdy = dv_inc()

        P.w("sync", d_ms, p0_gp)
        for kh in range(NKH):
            def mk(s, kh=kh):
                s.dma_start(out=wqTA(kh, 1024, 2048), in_=wattn_bf[1024:3072, kh * 128:(kh + 1) * 128],
                            transpose=True).then_inc(d_wq.h, 16)
            P.em("sync", mk)
            d_wq.n += 16
        wq_kv_rdy = d_wq.n

        SX, SSQ, M_, EX2, T1_, VAR, LNV, RSTD, NMR = range(9)
        D_, ED, DEN0, DEN1, REC0, REC1, S1_, S2_ = range(9, 17)

        def col(i):
            return lns[:, i:i + 1]

        last_dve = 0
        exp_free = [0, 0]
        xn_evac = [0] * NCT
        xnT_done = [0] * NCT
        xn_consumed_act = [0] * NCT
        b6_evac = 0
        kv_bank_free = [0, 0]
        part_mm = 0
        kv_war = 0

        def dve_op(fn, *waits, edge=False):
            nonlocal last_dve
            for sem, val in waits:
                P.w("dve", sem, val)
            if edge:
                P.w("dve", s_dv, last_dve)
            def mk(v, fn=fn):
                fn(v).then_inc(s_dv.h, 1)
            P.em("dve", mk)
            last_dve = dv_inc()
            return last_dve

        for ct in range(NCT):
            par = ct % 2
            xt = xsb[:, par * DM:(par + 1) * DM]
            if ct >= 2:
                P.w("sync", s_ac, xn_consumed_act[ct - 2])
            def mkx(s, ct=ct, xt=xt, dx=d_x[par]):
                s.dma_start(out=xt, in_=xs[ct * 128:(ct + 1) * 128, :]).then_inc(dx.h, 16)
            P.em("sync", mkx)
            d_x[par].n += 16
            x_rdy = d_x[par].n

            sum_done = dve_op(lambda v, xt=xt: v.reduce_sum(col(SX), xt, axis=AX.X), (d_x[par], x_rdy))
            if ct >= 1:
                P.w("dve", s_pe, xnT_done[ct - 1])
            ssq_done = dve_op(lambda v, xt=xt: v.scalar_tensor_tensor(
                xn_tm[:], xt, 1.0, xt, op0=ALU.mult, op1=ALU.mult, accum_out=col(SSQ)))
            dve_op(lambda v: v.tensor_scalar_mul(col(M_), col(SX), 1.0 / DM), edge=True)
            dve_op(lambda v: v.tensor_scalar_mul(col(EX2), col(SSQ), 1.0 / DM), edge=True)
            dve_op(lambda v: v.tensor_tensor(col(T1_), col(M_), col(M_), op=ALU.mult), edge=True)
            dve_op(lambda v: v.tensor_tensor(col(VAR), col(EX2), col(T1_), op=ALU.subtract), edge=True)
            var_done = dve_op(lambda v: v.tensor_scalar_add(col(VAR), col(VAR), EPS_LN), edge=True)
            P.w("act", s_dv, var_done)
            P.em("act", lambda a: a.activation(col(LNV), col(VAR), AF.Ln).then_inc(s_ac.h, 1))
            lnv_done = ac_inc()
            P.w("act", s_ac, lnv_done)
            P.em("act", lambda a: a.activation(col(RSTD), col(LNV), AF.Exp, scale=-0.5).then_inc(s_ac.h, 1))
            rstd_done = ac_inc()
            nmr_done = dve_op(lambda v: v.scalar_tensor_tensor(
                col(NMR), col(M_), -1.0, col(RSTD), op0=ALU.mult, op1=ALU.mult), (s_ac, rstd_done), edge=True)
            P.w("act", s_dv, nmr_done)
            P.w("act", s_ac, rstd_done)
            def mkxn(a, xt=xt):
                a.activation(xn_tm[:], xt, AF.Identity, bias=col(NMR), scale=col(RSTD)).then_inc(s_ac.h, 1)
            P.em("act", mkxn)
            xn_done = ac_inc()
            xn_consumed_act[ct] = xn_done

            P.w("pe", s_ac, xn_done)
            P.w("pe", s_dv, b6_evac)
            for kh in range(NKH):
                def mk(t, kh=kh, sp=(kh == NKH - 1)):
                    ins = t.transpose(b6[:, kh * 128:(kh + 1) * 128], xn_tm[:, kh * 128:(kh + 1) * 128], identb[:])
                    if sp:
                        ins.then_inc(s_pe.h, 1)
                P.em("pe", mk)
            xnT_done[ct] = pe_inc()
            xn_view = xn_fm[:].rearrange("p (kh t) -> p kh t", t=TOK)[:, :, ct * 128:(ct + 1) * 128]
            b6v = b6[:, 0:1024].rearrange("p (kh t) -> p kh t", t=128)
            xn_evac[ct] = dve_op(lambda v, xv=xn_view, bv=b6v: v.tensor_copy(xv, bv), (s_pe, xnT_done[ct]))
            b6_evac = xn_evac[ct]

            pairs = []
            for kh in range(NKH):
                pairs.append((
                    (lambda kh=kh, ct=ct: xn_fm[:, kh * TOK + ct * 128:kh * TOK + ct * 128 + 128]),
                    (lambda kh=kh: gwT[:, kh * 8:(kh + 1) * 8]),
                ))
            P.w("pe", s_dv, xn_evac[ct])
            P.w("pe", s_dv, gwt_rdy)
            gate_mm = mm_group(lambda: b3[:, 0:8], pairs)
            gsc = gtmp[:, 0:8]
            eq1 = gtmp[:, 8:16]
            scm = gtmp[:, 16:24]
            eq2 = gtmp[:, 24:32]
            dve_op(lambda v: v.tensor_copy(gsc, b3[:, 0:8]), (s_pe, gate_mm), edge=True)
            gsc_done = last_dve
            dve_op(lambda v: v.reduce_max(col(M_), gsc, axis=AX.X), edge=True)
            dve_op(lambda v: v.tensor_scalar(eq1, gsc, col(M_), None, op0=ALU.is_equal), edge=True)
            dve_op(lambda v: v.scalar_tensor_tensor(scm, eq1, -1e30, gsc, op0=ALU.mult, op1=ALU.add), edge=True)
            dve_op(lambda v: v.reduce_max(col(T1_), scm, axis=AX.X), edge=True)
            dve_op(lambda v: v.tensor_scalar(eq2, scm, col(T1_), None, op0=ALU.is_equal), edge=True)
            d_done = dve_op(lambda v: v.tensor_tensor(col(D_), col(T1_), col(M_), op=ALU.subtract), edge=True)
            P.w("act", s_dv, d_done)
            P.em("act", lambda a: a.activation(col(ED), col(D_), AF.Exp).then_inc(s_ac.h, 1))
            ed_done = ac_inc()
            dve_op(lambda v: v.tensor_scalar_add(col(DEN0), col(ED), 1.0), (s_ac, ed_done), edge=True)
            dve_op(lambda v: v.reciprocal(col(S1_), col(DEN0)), edge=True)
            dve_op(lambda v: v.tensor_tensor(col(S2_), col(ED), col(S1_), op=ALU.mult), edge=True)
            combv = combs[:, ct * E:(ct + 1) * E]
            dve_op(lambda v, cv=combv: v.tensor_scalar(cv, eq1, col(S1_), None, op0=ALU.mult), edge=True)
            comb_done = dve_op(lambda v, cv=combv: v.scalar_tensor_tensor(
                cv, eq2, col(S2_), cv, op0=ALU.mult, op1=ALU.add), edge=True)

            kv_banks = [b0, b1]
            for sl in range(4):
                bank = kv_banks[sl % 2]
                pairs = []
                for kh in range(NKH):
                    pairs.append((
                        (lambda kh=kh, ct=ct: xn_fm[:, kh * TOK + ct * 128:kh * TOK + ct * 128 + 128]),
                        (lambda kh=kh, sl=sl: wqTA(kh, 1024 + sl * 512, 512)),
                    ))
                P.w("pe", d_wq, wq_kv_rdy)
                P.w("pe", s_dv, kv_bank_free[sl % 2])
                kv_mm = mm_group(lambda b=bank: b[:], pairs)
                if sl < 2:
                    tpar = sl % 2
                    tmn = tmin[:, tpar * 512:(tpar + 1) * 512]
                    tex = texp[:, tpar * 512:(tpar + 1) * 512]
                    mn_done = dve_op(lambda v, b=bank, t=tmn: v.tensor_scalar_min(t, b[:], 0.0),
                                     (s_pe, kv_mm), (s_ac, exp_free[tpar]))
                    P.w("act", s_dv, mn_done)
                    def mke(a, tmn=tmn, tex=tex):
                        a.activation(tex, tmn, AF.Exp).then_inc(s_ac.h, 1)
                    P.em("act", mke)
                    exp_done = ac_inc()
                    exp_free[tpar] = exp_done
                    kslice = kp_tm[:, sl * 512:(sl + 1) * 512]
                    kv_bank_free[sl % 2] = dve_op(
                        lambda v, b=bank, tex=tex, ks=kslice: v.scalar_tensor_tensor(
                            ks, b[:], 0.0, tex, op0=ALU.max, op1=ALU.add),
                        (s_ac, exp_done), (s_pe, kv_war), edge=True)
                else:
                    vslice = v_tm[:, (sl - 2) * 512:(sl - 1) * 512]
                    kv_bank_free[sl % 2] = dve_op(lambda v, b=bank, vs=vslice: v.tensor_copy(vs, b[:]),
                                                  (s_pe, kv_mm), (s_pe, kv_war))
            kv_done = last_dve
            for h in range(NH):
                P.w("pe", s_dv, kv_done)
                if part_mm > 0:
                    P.w("pe", s_dv, part_add)
                def mk1(t, h=h):
                    t.matmul(b3[0:64, 64:128], kp_tm[:, h * 64:(h + 1) * 64], v_tm[:, h * 64:(h + 1) * 64],
                             start=True, stop=True)
                P.em("pe", mk1)
                def mk2(t, h=h):
                    t.matmul(b3[0:64, 128:129], kp_tm[:, h * 64:(h + 1) * 64], onesb[:],
                             start=True, stop=True).then_inc(s_pe.h, 1)
                P.em("pe", mk2)
                part_mm = pe_inc()
                spv = spart[:, h * 65:(h + 1) * 65]
                part_add = dve_op(lambda v, spv=spv: v.tensor_tensor(spv, spv, b3[0:64, 64:129], op=ALU.add),
                                  (s_pe, part_mm), (s_dv, spart_zero))
            kv_war = part_mm
        p1_pe_end = s_pe.n
        partials_done = last_dve

        P.w("sync", s_dv, partials_done)
        P.em("sync", lambda s: s.dma_start(out=cc_in[:], in_=spart[:]).then_inc(d_cs.h, 16)); d_cs.n += 16
        P.w("gp", d_cs, d_cs.n)
        P.em("gp", lambda g: g.collective_compute(
            "AllGather", ALU.bypass, replica_groups=[[0, 1, 2, 3, 4, 5, 6, 7]],
            ins=[cc_in[:]], outs=[cc_out[:]]).then_inc(s_cc.h, 1)); s_cc.n += 1
        P.w("gp", s_cc, s_cc.n)
        P.w("dve", d_ct, consts_rdy)
        for r in range(8):
            def mkg(g, r=r):
                g.dma_start(out=sgath[:], in_=cc_out[r * 64:(r + 1) * 64, :]).then_inc(d_ms.h, 16)
            P.em("gp", mkg)
            d_ms.n += 16
            gath_rdy = d_ms.n
            acc_done = dve_op(lambda v, r=r: v.scalar_tensor_tensor(
                saug[:], sgath[:], wselb[:, r:r + 1], saug[:],
                op0=ALU.mult, op1=ALU.add), (d_ms, gath_rdy), edge=True)
            if r < 7:
                P.w("gp", s_dv, acc_done)
        saug_bf_rdy = dve_op(lambda v: v.tensor_copy(saugb[:], saug[:]), edge=True)
        saug_rdy_h = [saug_bf_rdy] * NH

        ffn_state = {
            "sil_par": 0,
            "g_pe_done": [0, 0],
            "y_evac": 0,
            "g_hist": [0, 0],
            "sil_free": [0, 0],
            "bH_free": [0, 0],
            "bU_free": [0, 0],
            "fjg": 0,
            "last_g_pe": 0,
            "w13_loaded": {},
            "w2_loaded": {},
            "span_pe_done": {},
        }

        def load_w13(k, wait_pe=0):
            e_, sp_ = divmod(k, NSP)
            slot = k % 2
            dw = d_w[slot]
            if wait_pe:
                P.w("sync", s_pe, wait_pe)
            P.w("sync", d_cast, cast_done[e_])
            def mk(s, e_=e_, sp_=sp_, slot=slot, dw=dw):
                src1 = w1bf[e_].rearrange("(kh p) f -> p kh f", p=128)[:, :, sp_ * SPAN:(sp_ + 1) * SPAN]
                dst1 = wreg[:, slot * NKH * SPAN:(slot + 1) * NKH * SPAN].rearrange("p (kh f) -> p kh f", f=SPAN)
                s.dma_start(out=dst1, in_=src1).then_inc(dw.h, 16)
                src3 = w3bf[e_].rearrange("(kh p) f -> p kh f", p=128)[:, :, sp_ * SPAN:(sp_ + 1) * SPAN]
                dst3 = wreg[:, 8192 + slot * NKH * SPAN:8192 + (slot + 1) * NKH * SPAN].rearrange(
                    "p (kh f) -> p kh f", f=SPAN)
                s.dma_start(out=dst3, in_=src3).then_inc(dw.h, 16)
            P.em("sync", mk)
            dw.n += 32
            ffn_state["w13_loaded"][k] = dw.n

        def load_w2s(k, wait_pe=0):
            e_, sp_ = divmod(k, NSP)
            slot = k % 2
            dw = d_w2[slot]
            if wait_pe:
                P.w("sync", s_pe, wait_pe)
            P.w("sync", d_cast, cast_done[e_])
            def mk2(s, e_=e_, sp_=sp_, slot=slot, dw=dw):
                src2 = w2bf[e_].rearrange("(kf p) n -> p kf n", p=128)[:, sp_ * NFJ:(sp_ + 1) * NFJ, :]
                dst2 = wreg[:, 16384 + slot * NFJ * DM:16384 + (slot + 1) * NFJ * DM].rearrange(
                    "p (kf n) -> p kf n", n=DM)
                s.dma_start(out=dst2, in_=src2).then_inc(dw.h, 16)
            P.em("sync", mk2)
            dw.n += 16
            ffn_state["w2_loaded"][k] = dw.n

        def ffn_chunk(tc, pe_gate, pre_dv=(), pre_ac=()):
            st = ffn_state
            if tc == 1:
                P.w("sync", d_st, 16)
            xsrc = xs.rearrange("(tt p) h -> p tt h", p=128)[:, tc * NTT:(tc + 1) * NTT, :]
            P.em("sync", lambda s, xsrc=xsrc: s.dma_start(
                out=y_acc[:].rearrange("p (tt h) -> p tt h", h=DM), in_=xsrc).then_inc(d_yx.h, 16))
            d_yx.n += 16
            yx_rdy = d_yx.n

            st["w13_loaded"] = {}
            st["w2_loaded"] = {}
            st["span_pe_done"] = {}
            if tc == 0:
                load_w13(0)
                load_w13(1)
                load_w2s(0, wait_pe=pe_gate)
                load_w2s(1)
            else:
                load_w13(0, wait_pe=pe_gate)
                load_w13(1)
                load_w2s(0)
                load_w2s(1)

            bH = [b0, b4]
            bU = [b1, b5]
            wsel_ = [w1b, w3b]
            for kp in range(0, NSPANS, 2):
                e_ = kp // NSP
                P.w("pe", d_w[0], st["w13_loaded"][kp])
                P.w("pe", d_w[1], st["w13_loaded"][kp + 1])
                if kp > 0:
                    P.w("pe", d_w2[0], st["w2_loaded"][kp])
                    P.w("pe", d_w2[1], st["w2_loaded"][kp + 1])
                if kp == 0:
                    P.w("pe", s_dv, xn_evac[NTT * (tc + 1) - 1])
                    for v_ in pre_dv:
                        P.w("pe", s_dv, v_)
                    for v_ in pre_ac:
                        P.w("pe", s_ac, v_)
                for tg in range(2):
                    gpar = tg % 2
                    tof = tc * CTOK + tg * 512
                    for half in range(2):
                        slot = half
                        sp_ = (kp + half) % NSP
                        for fj in range(NFJ):
                            spar = st["sil_par"]
                            bp = st["fjg"] % 2
                            P.w("pe", s_dv, st["g_hist"][bp])
                            pairs = [((lambda kh=kh, slot=slot, fj=fj: w1b(slot, kh, fj * 128, 128)),
                                      (lambda kh=kh, tof=tof: xn_fm[:, kh * TOK + tof:kh * TOK + tof + 512]))
                                     for kh in range(NKH)]
                            h_done = mm_group(lambda bp=bp: bH[bp][:], pairs)
                            P.w("act", s_pe, h_done)
                            P.w("act", s_dv, st["sil_free"][spar])
                            def mks(a, spar=spar, bp=bp):
                                a.activation(sil[:, spar * 512:(spar + 1) * 512], bH[bp][:], AF.Silu).then_inc(s_ac.h, 1)
                            P.em("act", mks)
                            st["bH_free"][bp] = ac_inc()
                            sil_done = s_ac.n
                            pairs = [((lambda kh=kh, slot=slot, fj=fj: w3b(slot, kh, fj * 128, 128)),
                                      (lambda kh=kh, tof=tof: xn_fm[:, kh * TOK + tof:kh * TOK + tof + 512]))
                                     for kh in range(NKH)]
                            u_done = mm_group(lambda bp=bp: bU[bp][:], pairs)
                            gd = gbuf[:, (gpar * 8 + half * NFJ + fj) * 512:(gpar * 8 + half * NFJ + fj) * 512 + 512]
                            g_done = dve_op(
                                lambda v, gd=gd, spar=spar, bp=bp: v.tensor_tensor(
                                    gd, sil[:, spar * 512:(spar + 1) * 512], bU[bp][:], op=ALU.mult),
                                (s_ac, sil_done), (s_pe, u_done), (s_pe, st["g_pe_done"][gpar]))
                            st["g_hist"][bp] = g_done
                            st["bU_free"][bp] = g_done
                            st["sil_free"][spar] = g_done
                            st["sil_par"] ^= 1
                            st["fjg"] += 1
                    g_rdy = s_dv.n
                    if tg == 1:
                        st["last_g_pe"] = s_pe.n
                    P.w("pe", s_dv, g_rdy)
                    for tt in range(4):
                        ttg = tg * 4 + tt
                        ct_g = tc * NTT + ttg
                        if kp == 0 and tg == 0 and tt == 0:
                            P.w("pe", d_w2[0], st["w2_loaded"][kp])
                            P.w("pe", d_w2[1], st["w2_loaded"][kp + 1])
                        P.w("pe", s_dv, st["y_evac"])
                        for ht in range(2):
                            pairs = []
                            for half in range(2):
                                for fj in range(NFJ):
                                    pairs.append((
                                        (lambda fj=fj, gpar=gpar, tt=tt, half=half: gbuf[
                                            :, (gpar * 8 + half * NFJ + fj) * 512 + tt * 128:
                                               (gpar * 8 + half * NFJ + fj) * 512 + tt * 128 + 128]),
                                        (lambda fj=fj, half=half, ht=ht: w2b(half, fj, ht * 512, 512)),
                                    ))
                            ymm = mm_group(lambda ht=ht: py[:, ht * 512:(ht + 1) * 512], pairs)
                        dst = y_acc[:, ttg * DM:ttg * DM + DM]
                        cslice = combs[:, ct_g * E + e_:ct_g * E + e_ + 1]
                        st["y_evac"] = dve_op(
                            lambda v, dst=dst, cs=cslice: v.scalar_tensor_tensor(
                                dst, py[:], cs, dst, op0=ALU.mult, op1=ALU.add),
                            (s_pe, ymm), (d_yx, yx_rdy), (s_dv, comb_done))
                    st["g_pe_done"][gpar] = s_pe.n
                g_end = s_pe.n
                st["span_pe_done"][kp] = g_end
                st["span_pe_done"][kp + 1] = g_end
                if kp + 2 < NSPANS:
                    load_w13(kp + 2, wait_pe=st["last_g_pe"])
                    load_w13(kp + 3)
                    load_w2s(kp + 2, wait_pe=g_end)
                    load_w2s(kp + 3)
                if tc == 0 and kp % NSP == NSP - 2:
                    cast_expert(kp // NSP + 2)
            return s_pe.n

        scan_state = {
            "msc_free": [0, 0],
            "rec_free": [0, 0],
            "b2_free": 0,
            "b4_free": 0,
            "b5_free": 0,
            "b3_free": 0,
            "qkv_bank_free": [0, 0],
            "exp_free": [0, 0],
            "head_par": 0,
            "qp_war": 0,
            "kv_war": 0,
        }

        def scan_chunk(tc, wq_rdy, pre_ac=0, pre_dv=0):
            nonlocal b6_evac, saug_bf_rdy
            ss = scan_state
            proj_evac = 0
            for ctl in range(8):
                ct = tc * 8 + ctl
                for sl in range(2):
                    bank = [b0, b1][sl % 2]
                    pairs = [((lambda kh=kh, ct=ct: xn_fm[:, kh * TOK + ct * 128:kh * TOK + ct * 128 + 128]),
                              (lambda kh=kh, sl=sl: wqT(kh, sl * 512, 512)))
                             for kh in range(NKH)]
                    P.w("pe", d_wq, wq_rdy)
                    P.w("pe", s_dv, ss["qkv_bank_free"][sl % 2])
                    if ctl == 0 and sl == 0:
                        P.w("pe", s_ac, pre_ac)
                        P.w("pe", s_dv, pre_dv)
                    q_mm = mm_group(lambda b=bank: b[:], pairs)
                    tpar = sl % 2
                    tmn = tmin[:, tpar * 512:(tpar + 1) * 512]
                    tex = texp[:, tpar * 512:(tpar + 1) * 512]
                    mn_done = dve_op(lambda v, b=bank, t=tmn: v.tensor_scalar_min(t, b[:], 0.0),
                                     (s_pe, q_mm), (s_ac, ss["exp_free"][tpar]))
                    P.w("act", s_dv, mn_done)
                    def mke(a, tmn=tmn, tex=tex):
                        a.activation(tex, tmn, AF.Exp).then_inc(s_ac.h, 1)
                    P.em("act", mke)
                    exp_done = ac_inc()
                    ss["exp_free"][tpar] = exp_done
                    qslice = qp_tm[:, sl * 512:(sl + 1) * 512]
                    ss["qkv_bank_free"][sl % 2] = dve_op(
                        lambda v, b=bank, tex=tex, qs=qslice: v.scalar_tensor_tensor(
                            qs, b[:], 0.0, tex, op0=ALU.max, op1=ALU.add),
                        (s_ac, exp_done), (s_pe, ss["qp_war"]), edge=True)
                for sl in range(4):
                    bank = [b0, b1][sl % 2]
                    pairs = [((lambda kh=kh, ct=ct: xn_fm[:, kh * TOK + ct * 128:kh * TOK + ct * 128 + 128]),
                              (lambda kh=kh, sl=sl: wqT(kh, 1024 + sl * 512, 512)))
                             for kh in range(NKH)]
                    P.w("pe", s_dv, ss["qkv_bank_free"][sl % 2])
                    kv_mm = mm_group(lambda b=bank: b[:], pairs)
                    if sl < 2:
                        tpar = sl % 2
                        tmn = tmin[:, tpar * 512:(tpar + 1) * 512]
                        tex = texp[:, tpar * 512:(tpar + 1) * 512]
                        mn_done = dve_op(lambda v, b=bank, t=tmn: v.tensor_scalar_min(t, b[:], 0.0),
                                         (s_pe, kv_mm), (s_ac, ss["exp_free"][tpar]))
                        P.w("act", s_dv, mn_done)
                        def mke(a, tmn=tmn, tex=tex):
                            a.activation(tex, tmn, AF.Exp).then_inc(s_ac.h, 1)
                        P.em("act", mke)
                        exp_done = ac_inc()
                        ss["exp_free"][tpar] = exp_done
                        kslice = kp_tm[:, sl * 512:(sl + 1) * 512]
                        ss["qkv_bank_free"][sl % 2] = dve_op(
                            lambda v, b=bank, tex=tex, ks=kslice: v.scalar_tensor_tensor(
                                ks, b[:], 0.0, tex, op0=ALU.max, op1=ALU.add),
                            (s_ac, exp_done), (s_pe, ss["kv_war"]), edge=True)
                    else:
                        vslice = v_tm[:, (sl - 2) * 512:(sl - 1) * 512]
                        ss["qkv_bank_free"][sl % 2] = dve_op(
                            lambda v, b=bank, vs=vslice: v.tensor_copy(vs, b[:]), (s_pe, kv_mm), (s_pe, ss["kv_war"]))
                qkv_rdy = last_dve

                for (srct, dstv) in ((qp_tm, qp_fm), (kp_tm, kp_fm)):
                    for half in range(2):
                        P.w("pe", s_dv, qkv_rdy)
                        P.w("pe", s_dv, b6_evac)
                        for hh in range(8):
                            h = half * 8 + hh
                            def mk(t, srct=srct, h=h, hh=hh, sp=(hh == 7)):
                                ins = t.transpose(b6[0:64, hh * 128:(hh + 1) * 128],
                                                  srct[:, h * 64:(h + 1) * 64], identb[:])
                                if sp:
                                    ins.then_inc(s_pe.h, 1)
                            P.em("pe", mk)
                        trd = pe_inc()
                        b6_evac = dve_op(lambda v, dstv=dstv, half=half: v.tensor_copy(
                            dstv[:, half * 1024:(half + 1) * 1024], b6[0:64, 0:1024]), (s_pe, trd))
                fm_rdy = b6_evac
                ss["qp_war"] = s_pe.n

                for h in range(NH):
                    hpar = ss["head_par"]
                    qpf = qp_fm[:, h * 128:(h + 1) * 128]
                    kpf = kp_fm[:, h * 128:(h + 1) * 128]
                    P.w("pe", s_dv, fm_rdy)
                    P.w("pe", s_dv, ss["b2_free"])
                    def mksc(t, kpf=kpf, qpf=qpf):
                        t.matmul(b2[:, 0:128], kpf, qpf, start=True, stop=True).then_inc(s_pe.h, 1)
                    P.em("pe", mksc)
                    sc_mm = pe_inc()
                    mslice = msc[:, hpar * 128:(hpar + 1) * 128]
                    P.w("dve", s_pe, ss["msc_free"][hpar])
                    ss["b2_free"] = dve_op(
                        lambda v, ms=mslice: v.tensor_tensor(ms, b2[:, 0:128], maskb[:], op=ALU.mult),
                        (s_pe, sc_mm), (d_ct, consts_rdy))
                    mask_done = ss["b2_free"]
                    P.w("pe", s_dv, mask_done)
                    P.w("pe", s_ac, ss["b4_free"])
                    P.w("pe", s_dv, saug_rdy_h[h])
                    def mkn1(t, ms=mslice, h=h):
                        t.matmul(b4[:, 0:64], ms, v_tm[:, h * 64:(h + 1) * 64], start=True, stop=False)
                    P.em("pe", mkn1)
                    def mkn2(t, qpf=qpf, h=h):
                        t.matmul(b4[:, 0:64], qpf, saugb[:, h * 65:h * 65 + 64],
                                 start=False, stop=True).then_inc(s_pe.h, 1)
                    P.em("pe", mkn2)
                    num_mm = pe_inc()
                    P.w("pe", s_dv, ss["b5_free"])
                    def mkd1(t, ms=mslice):
                        t.matmul(b5[:, 0:1], ms, onesb[:], start=True, stop=False)
                    P.em("pe", mkd1)
                    def mkd2(t, qpf=qpf, h=h):
                        t.matmul(b5[:, 0:1], qpf, saugb[:, h * 65 + 64:h * 65 + 65],
                                 start=False, stop=True).then_inc(s_pe.h, 1)
                    P.em("pe", mkd2)
                    den_mm = pe_inc()
                    ss["msc_free"][hpar] = den_mm
                    P.w("pe", s_dv, ss["b3_free"])
                    def mks1(t, h=h):
                        t.matmul(b3[0:64, 64:128], kp_tm[:, h * 64:(h + 1) * 64], v_tm[:, h * 64:(h + 1) * 64],
                                 start=True, stop=True)
                    P.em("pe", mks1)
                    def mks2(t, h=h):
                        t.matmul(b3[0:64, 128:129], kp_tm[:, h * 64:(h + 1) * 64], onesb[:],
                                 start=True, stop=True).then_inc(s_pe.h, 1)
                    P.em("pe", mks2)
                    ds_mm = pe_inc()
                    rc = REC0 if hpar == 0 else REC1
                    dn = DEN0 if hpar == 0 else DEN1
                    P.w("dve", s_ac, ss["rec_free"][hpar])
                    dve_op(lambda v, dn=dn: v.tensor_scalar_add(col(dn), b5[:, 0:1], EPS_DEN), (s_pe, den_mm))
                    ss["b5_free"] = last_dve
                    rec_done = dve_op(lambda v, rc=rc, dn=dn: v.reciprocal(col(rc), col(dn)), edge=True)
                    P.w("act", s_pe, num_mm)
                    P.w("act", s_dv, rec_done)
                    def mky(a, h=h, rc=rc):
                        a.activation(y_tm[:, h * 64:(h + 1) * 64], b4[:, 0:64], AF.Identity,
                                     scale=col(rc)).then_inc(s_ac.h, 1)
                    P.em("act", mky)
                    ss["b4_free"] = ac_inc()
                    ss["rec_free"][hpar] = ss["b4_free"]
                    yh_done = ss["b4_free"]
                    sgv = saug[:, h * 65:(h + 1) * 65]
                    dve_op(lambda v, sgv=sgv: v.tensor_tensor(sgv, sgv, b3[0:64, 64:129], op=ALU.add),
                           (s_pe, ds_mm), (s_dv, saug_rdy_h[h]))
                    ss["b3_free"] = last_dve
                    sgbv = saugb[:, h * 65:(h + 1) * 65]
                    saug_rdy_h[h] = dve_op(lambda v, sgv=sgv, sgbv=sgbv: v.tensor_copy(sgbv, sgv), edge=True)
                    ss["kv_war"] = ds_mm
                    ss["head_par"] ^= 1
                y_all = s_ac.n

                P.w("pe", s_ac, y_all)
                P.w("pe", s_dv, b6_evac)
                for j in range(NKH):
                    def mk(t, j=j, sp=(j == NKH - 1)):
                        ins = t.transpose(b6[:, j * 128:(j + 1) * 128], y_tm[:, j * 128:(j + 1) * 128], identb[:])
                        if sp:
                            ins.then_inc(s_pe.h, 1)
                    P.em("pe", mk)
                ytr = pe_inc()
                b6_evac = dve_op(lambda v: v.tensor_copy(y_fm[:], b6[:, 0:1024]), (s_pe, ytr))
                yfm_rdy = b6_evac
                for cs in range(2):
                    bank = b7 if cs == 0 else b2
                    bid = "b7" if cs == 0 else "b2"
                    pairs = [((lambda j=j: y_fm[:, j * 128:(j + 1) * 128]),
                              (lambda j=j, cs=cs: wpT(j, cs * 512, 512)))
                             for j in range(NKH)]
                    P.w("pe", s_dv, yfm_rdy)
                    if cs == 1:
                        P.w("pe", s_dv, ss["b2_free"])
                        P.w("pe", s_dv, proj_evac)
                    else:
                        P.w("pe", s_dv, proj_evac)
                    pj_mm = mm_group(lambda b=bank: b[:], pairs)
                    dst = y_acc[:, ctl * DM + cs * 512:ctl * DM + cs * 512 + 512]
                    proj_evac = dve_op(lambda v, dst=dst, b=bank: v.scalar_tensor_tensor(
                        dst, b[:], 1.0, dst, op0=ALU.mult, op1=ALU.add), (s_pe, pj_mm))
                    if cs == 1:
                        ss["b2_free"] = proj_evac
            return proj_evac

        ffn0_pe_end = ffn_chunk(0, pe_gate=p1_pe_end, pre_dv=(kv_bank_free[0], kv_bank_free[1]))
        P.w("sync", s_pe, ffn0_pe_end)
        for kh in range(NKH):
            def mk(s, kh=kh):
                s.dma_start(out=wqT(kh, 0, 3072), in_=wattn_bf[:, kh * 128:(kh + 1) * 128],
                            transpose=True).then_inc(d_wq.h, 16)
            P.em("sync", mk)
            d_wq.n += 16
        for kh in range(NKH):
            def mk(s, kh=kh):
                s.dma_start(out=wpT(kh, 0, 1024), in_=wp_bf[:, kh * 128:(kh + 1) * 128],
                            transpose=True).then_inc(d_wq.h, 16)
            P.em("sync", mk)
            d_wq.n += 16
        wq_full0 = d_wq.n
        scan_state["kv_war"] = p1_pe_end
        scan_state["b5_free"] = max(ffn_state["bU_free"])
        scan0_end = scan_chunk(0, wq_full0, pre_ac=max(ffn_state["bH_free"]), pre_dv=max(ffn_state["bU_free"]))
        P.w("sync", s_dv, scan0_end)
        yo0 = yout.rearrange("(tt p) h -> p tt h", p=128)[:, 0:NTT, :]
        P.em("sync", lambda s, yo0=yo0: s.dma_start(
            out=yo0, in_=y_acc[:].rearrange("p (tt h) -> p tt h", h=DM)).then_inc(d_st.h, 16))
        d_st.n += 16

        scan0_pe_end = s_pe.n
        ffn1_pe_end = ffn_chunk(1, pe_gate=scan0_pe_end,
                                pre_dv=(scan_state["qkv_bank_free"][0], scan_state["qkv_bank_free"][1],
                                        scan_state["b5_free"], scan_state["b3_free"], scan_state["b2_free"]),
                                pre_ac=(scan_state["b4_free"], scan_state["rec_free"][0], scan_state["rec_free"][1]))
        P.w("sync", s_pe, ffn1_pe_end)
        for kh in range(NKH):
            def mk(s, kh=kh):
                s.dma_start(out=wqT(kh, 0, 3072), in_=wattn_bf[:, kh * 128:(kh + 1) * 128],
                            transpose=True).then_inc(d_wq.h, 16)
            P.em("sync", mk)
            d_wq.n += 16
        for kh in range(NKH):
            def mk(s, kh=kh):
                s.dma_start(out=wpT(kh, 0, 1024), in_=wp_bf[:, kh * 128:(kh + 1) * 128],
                            transpose=True).then_inc(d_wq.h, 16)
            P.em("sync", mk)
            d_wq.n += 16
        wq_full1 = d_wq.n
        scan_state["b5_free"] = max(scan_state["b5_free"], max(ffn_state["bU_free"]))
        scan1_end = scan_chunk(1, wq_full1, pre_ac=max(ffn_state["bH_free"]), pre_dv=max(ffn_state["bU_free"]))
        P.w("sync", s_dv, scan1_end)
        yo1 = yout.rearrange("(tt p) h -> p tt h", p=128)[:, NTT:2 * NTT, :]
        P.em("sync", lambda s, yo1=yo1: s.dma_start(
            out=yo1, in_=y_acc[:].rearrange("p (tt h) -> p tt h", h=DM)).then_inc(d_st.h, 16))
        d_st.n += 16
        P.w("sync", d_st, d_st.n)

        with nc.Block() as block:
            @block.tensor
            def _(t):
                for f in P.ops["pe"]:
                    f(t)

            @block.scalar
            def _(a):
                for f in P.ops["act"]:
                    f(a)

            @block.vector
            def _(v):
                for f in P.ops["dve"]:
                    f(v)

            @block.sync
            def _(s):
                for f in P.ops["sync"]:
                    f(s)

            @block.gpsimd
            def _(g):
                for f in P.ops["gp"]:
                    f(g)

    return nc


_NC_CACHE = {}


def kernel(L0_attn_x, L0_attn_w_attn, L0_attn_w_proj, L0_ffn_gate_w, L0_ffn_w1, L0_ffn_w3, L0_ffn_w2):
    x = np.asarray(L0_attn_x)
    w_attn = np.ascontiguousarray(np.asarray(L0_attn_w_attn), dtype=np.float32)
    w_proj = np.ascontiguousarray(np.asarray(L0_attn_w_proj), dtype=np.float32)
    gate_w = np.ascontiguousarray(np.asarray(L0_ffn_gate_w), dtype=np.float32)
    w1 = np.ascontiguousarray(np.asarray(L0_ffn_w1), dtype=np.float32)
    w3 = np.ascontiguousarray(np.asarray(L0_ffn_w3), dtype=np.float32)
    w2 = np.ascontiguousarray(np.asarray(L0_ffn_w2), dtype=np.float32)

    ident = np.eye(128, dtype=ml_dtypes.bfloat16)
    maskT = np.triu(np.ones((128, 128))).astype(ml_dtypes.bfloat16)
    onesc = np.ones((128, 1), dtype=ml_dtypes.bfloat16)

    if "nc" not in _NC_CACHE:
        _NC_CACHE["nc"] = build_nc()
    nc = _NC_CACHE["nc"]

    in_maps = []
    for c in range(8):
        b, seg = divmod(c, 4)
        xs = np.ascontiguousarray(x[b, seg * TOK:(seg + 1) * TOK, :], dtype=np.float32)
        wsel = np.zeros((64, 8), np.float32)
        for r in range(8):
            rb, rseg = divmod(r, 4)
            if rb == b and rseg < seg:
                wsel[:, r] = 1.0
        in_maps.append({
            "xs": xs, "w_attn": w_attn, "w_proj": w_proj, "gate_w": gate_w,
            "w1": w1, "w3": w3, "w2": w2,
            "ident": ident, "maskT": maskT, "onesc": onesc, "wsel": wsel,
        })

    res = run_bass_kernel_spmd(nc, in_maps, core_ids=list(range(8)))
    out = np.empty((2, 8192, 1024), np.float32)
    for c in range(8):
        b, seg = divmod(c, 4)
        out[b, seg * TOK:(seg + 1) * TOK, :] = np.asarray(res.results[c]["yout"])
    return out


# version 06
# speedup vs baseline: 1.2558x; 1.2558x over previous
import contextlib
import numpy as np
import ml_dtypes
import concourse.bass as bass
import concourse.mybir as mybir
from concourse.bass_utils import run_bass_kernel_spmd

F32 = mybir.dt.float32
BF16 = mybir.dt.bfloat16
AF = mybir.ActivationFunctionType
ALU = mybir.AluOpType
AX = mybir.AxisListType

TOK = 2048
NCT = 16
DM = 1024
NKH = 8
NH = 16
HD = 64
E = 8
FF = 4096
SPAN = 512
NSP = FF // SPAN
NFJ = SPAN // 128
NSPANS = E * NSP
NCH = 2
CTOK = 1024
NTT = 8
EPS_LN = 1e-5
EPS_DEN = 1e-6


class Sem:
    def __init__(self, es, nc, name):
        self.h = es.enter_context(nc.semaphore(name))
        self.n = 0


class B:

    def __init__(self):
        self.ops = {k: [] for k in ("pe", "act", "dve", "sync", "gp")}

    def em(self, eng, fn):
        self.ops[eng].append(fn)

    def w(self, eng, sem, val):
        if val > 0:
            self.ops[eng].append(lambda e, s=sem.h, v=val: e.wait_ge(s, v))


def build_nc():
    nc = bass.Bass("TRN2", target_bir_lowering=False)
    xs = nc.dram_tensor("xs", [TOK, DM], F32, kind="ExternalInput").ap()
    w_attn = nc.dram_tensor("w_attn", [3 * DM, DM], F32, kind="ExternalInput").ap()
    w_proj = nc.dram_tensor("w_proj", [DM, DM], F32, kind="ExternalInput").ap()
    gate_w = nc.dram_tensor("gate_w", [E, DM], F32, kind="ExternalInput").ap()
    w1 = nc.dram_tensor("w1", [E, DM, FF], F32, kind="ExternalInput").ap()
    w3 = nc.dram_tensor("w3", [E, DM, FF], F32, kind="ExternalInput").ap()
    w2 = nc.dram_tensor("w2", [E, FF, DM], F32, kind="ExternalInput").ap()
    ident = nc.dram_tensor("ident", [128, 128], BF16, kind="ExternalInput").ap()
    maskT = nc.dram_tensor("maskT", [128, 128], BF16, kind="ExternalInput").ap()
    onesc = nc.dram_tensor("onesc", [128, 1], BF16, kind="ExternalInput").ap()
    wsel = nc.dram_tensor("wsel", [64, 8], F32, kind="ExternalInput").ap()
    yout = nc.dram_tensor("yout", [TOK, DM], F32, kind="ExternalOutput").ap()
    wattn_bf = nc.dram_tensor("wattn_bf", [3 * DM, DM], BF16, kind="Internal").ap()
    w1bf = nc.dram_tensor("w1bf", [E, DM, FF], BF16, kind="Internal").ap()
    w3bf = nc.dram_tensor("w3bf", [E, DM, FF], BF16, kind="Internal").ap()
    w2bf = nc.dram_tensor("w2bf", [E, FF, DM], BF16, kind="Internal").ap()
    wp_bf = nc.dram_tensor("wp_bf", [DM, DM], BF16, kind="Internal").ap()
    cc_in = nc.dram_tensor("cc_in", [64, NH * 65], F32, kind="Internal").ap()
    cc_out = nc.dram_tensor("cc_out", [8 * 64, NH * 65], F32, kind="Internal").ap()

    es = contextlib.ExitStack()
    sb = lambda n, s, d: es.enter_context(nc.sbuf_tensor(n, s, d))
    ps = lambda n, s, d: es.enter_context(nc.psum_tensor(n, s, d))
    with es:
        xn_fm = sb("xn_fm", [128, NKH * TOK], BF16)
        wreg = sb("wreg", [128, 32768], BF16)
        y_acc = sb("y_acc", [128, NTT * DM], F32)
        combs = sb("combs", [128, NCT * E], F32)
        saug = sb("saug", [64, NH * 65], F32)
        saugb = sb("saugb", [64, NH * 65], BF16)
        spart = sb("spart", [64, NH * 65], F32)
        sgath = sb("sgath", [64, NH * 65], F32)
        wselb = sb("wselb", [64, 8], F32)
        xsb = sb("xsb", [128, 2 * DM], F32)
        xn_tm = sb("xn_tm", [128, DM], BF16)
        lns = sb("lns", [128, 24], F32)
        gtmp = sb("gtmp", [128, 32], F32)
        gw8 = sb("gw8", [16, DM], BF16)
        gwT = sb("gwT", [128, NKH * E], BF16)
        identb = sb("identb", [128, 128], BF16)
        maskb = sb("maskb", [128, 128], BF16)
        onesb = sb("onesb", [128, 1], BF16)
        kp_tm = sb("kp_tm", [128, DM], BF16)
        v_tm = sb("v_tm", [128, DM], BF16)
        qp_tm = sb("qp_tm", [128, DM], BF16)
        qp_fm = sb("qp_fm", [64, NH * 128], BF16)
        kp_fm = sb("kp_fm", [64, NH * 128], BF16)
        y_tm = sb("y_tm", [128, DM], BF16)
        y_fm = sb("y_fm", [128, DM], BF16)
        tmin = sb("tmin", [128, 2 * 512], F32)
        texp = sb("texp", [128, 2 * 512], F32)
        msc = sb("msc", [128, 2 * 128], BF16)
        sil = sb("sil", [128, 2 * 512], BF16)
        gbuf = sb("gbuf", [128, 2 * 8 * 512], BF16)

        def w1b(slot, kh, fcol0, n):
            off = slot * (NKH * SPAN) + kh * SPAN + fcol0
            return wreg[:, off:off + n]

        def w3b(slot, kh, fcol0, n):
            off = 8192 + slot * (NKH * SPAN) + kh * SPAN + fcol0
            return wreg[:, off:off + n]

        def w2b(slot, kf, hcol0, n):
            off = 16384 + slot * (NFJ * DM) + kf * DM + hcol0
            return wreg[:, off:off + n]

        def wqT(kh, c0, n):
            return wreg[:, kh * 3072 + c0:kh * 3072 + c0 + n]

        def wqTA(kh, c0, n):
            return wreg[:, 16384 + kh * 2048 + (c0 - 1024):16384 + kh * 2048 + (c0 - 1024) + n]

        def wpT(kh, c0, n):
            return wreg[:, 24576 + kh * DM + c0:24576 + kh * DM + c0 + n]

        b0 = ps("b0", [128, 512], F32)
        b1 = ps("b1", [128, 512], F32)
        py = ps("py", [128, 1024], F32)
        b2 = py[:, 0:512]
        b3 = py[:, 512:1024]
        b4 = ps("b4", [128, 512], F32)
        b5 = ps("b5", [128, 512], F32)
        b6 = ps("b6", [128, 1024], BF16)
        b7 = ps("b7", [128, 512], F32)

        d_x = [Sem(es, nc, "d_x0"), Sem(es, nc, "d_x1")]
        d_yx = Sem(es, nc, "d_yx")
        d_w = [Sem(es, nc, "d_w0"), Sem(es, nc, "d_w1")]
        d_w2 = [Sem(es, nc, "d_w2a"), Sem(es, nc, "d_w2b")]
        d_wq = Sem(es, nc, "d_wq")
        d_ms = Sem(es, nc, "d_ms")
        d_cast = Sem(es, nc, "d_cast")
        d_ct = Sem(es, nc, "d_ct")
        d_cs = Sem(es, nc, "d_cs")
        s_cc = Sem(es, nc, "s_cc")
        s_pe = Sem(es, nc, "s_pe")
        s_ac = Sem(es, nc, "s_ac")
        s_dv = Sem(es, nc, "s_dv")
        d_st = Sem(es, nc, "d_st")

        P = B()

        def pe_inc():
            s_pe.n += 1
            return s_pe.n

        def ac_inc():
            s_ac.n += 1
            return s_ac.n

        def dv_inc():
            s_dv.n += 1
            return s_dv.n

        def mm_group(bank_ap_fn, pairs, extra_first_wait=None):
            n = len(pairs)
            for i, (lf, rf) in enumerate(pairs):
                def mk(t, lf=lf, rf=rf, st=(i == 0), sp=(i == n - 1), bf=bank_ap_fn):
                    ins = t.matmul(bf(), lf(), rf(), start=st, stop=sp)
                    if sp:
                        ins.then_inc(s_pe.h, 1)
                P.em("pe", mk)
            return pe_inc()

        P.em("sync", lambda s: s.dma_start(out=identb[:], in_=ident).then_inc(d_ct.h, 16)); d_ct.n += 16
        P.em("sync", lambda s: s.dma_start(out=maskb[:], in_=maskT).then_inc(d_ct.h, 16)); d_ct.n += 16
        P.em("sync", lambda s: s.dma_start(out=onesb[:], in_=onesc).then_inc(d_ct.h, 16)); d_ct.n += 16
        P.em("sync", lambda s: s.dma_start(out=wselb[:], in_=wsel).then_inc(d_ct.h, 16)); d_ct.n += 16
        consts_rdy = d_ct.n

        P.em("gp", lambda g: g.dma_start(out=wattn_bf[:], in_=w_attn).then_inc(d_ms.h, 16)); d_ms.n += 16
        P.em("gp", lambda g: g.dma_start(out=gw8[0:8, :], in_=gate_w).then_inc(d_ms.h, 16)); d_ms.n += 16
        P.em("gp", lambda g: g.dma_start(out=wp_bf[:], in_=w_proj).then_inc(d_ms.h, 16)); d_ms.n += 16
        p0_gp = d_ms.n

        P.em("dve", lambda v: v.memset(saug[:], 0.0).then_inc(s_dv.h, 1)); dv_inc()
        P.em("dve", lambda v: v.memset(spart[:], 0.0).then_inc(s_dv.h, 1)); spart_zero = dv_inc()

        P.w("pe", d_ms, p0_gp)
        P.w("pe", d_ct, consts_rdy)
        for kh in range(NKH):
            def mk(t, kh=kh, sp=(kh == NKH - 1)):
                ins = t.transpose(b6[:, kh * 8:kh * 8 + 8], gw8[0:8, kh * 128:(kh + 1) * 128], identb[0:8, 0:8])
                if sp:
                    ins.then_inc(s_pe.h, 1)
            P.em("pe", mk)
        gwt_tr = pe_inc()
        P.w("dve", s_pe, gwt_tr)
        P.em("dve", lambda v: v.tensor_copy(gwT[:], b6[:, 0:NKH * E]).then_inc(s_dv.h, 1))
        gwt_rdy = dv_inc()

        P.w("sync", d_ms, p0_gp)
        for kh in range(NKH):
            def mk(s, kh=kh):
                s.dma_start(out=wqTA(kh, 1024, 2048), in_=wattn_bf[1024:3072, kh * 128:(kh + 1) * 128],
                            transpose=True).then_inc(d_wq.h, 16)
            P.em("sync", mk)
            d_wq.n += 16
        wq_kv_rdy = d_wq.n

        SX, SSQ, M_, EX2, T1_, VAR, LNV, RSTD, NMR = range(9)
        D_, ED, DEN0, DEN1, REC0, REC1, S1_, S2_ = range(9, 17)

        def col(i):
            return lns[:, i:i + 1]

        last_dve = 0
        exp_free = [0, 0]
        xn_evac = [0] * NCT
        xnT_done = [0] * NCT
        xn_consumed_act = [0] * NCT
        b6_evac = 0
        kv_bank_free = [0, 0]
        part_mm = 0
        kv_war = 0

        def dve_op(fn, *waits, edge=False):
            nonlocal last_dve
            for sem, val in waits:
                P.w("dve", sem, val)
            if edge:
                P.w("dve", s_dv, last_dve)
            def mk(v, fn=fn):
                fn(v).then_inc(s_dv.h, 1)
            P.em("dve", mk)
            last_dve = dv_inc()
            return last_dve

        for ct in range(NCT):
            par = ct % 2
            xt = xsb[:, par * DM:(par + 1) * DM]
            if ct >= 2:
                P.w("sync", s_ac, xn_consumed_act[ct - 2])
            def mkx(s, ct=ct, xt=xt, dx=d_x[par]):
                s.dma_start(out=xt, in_=xs[ct * 128:(ct + 1) * 128, :]).then_inc(dx.h, 16)
            P.em("sync", mkx)
            d_x[par].n += 16
            x_rdy = d_x[par].n

            sum_done = dve_op(lambda v, xt=xt: v.reduce_sum(col(SX), xt, axis=AX.X), (d_x[par], x_rdy))
            if ct >= 1:
                P.w("dve", s_pe, xnT_done[ct - 1])
            ssq_done = dve_op(lambda v, xt=xt: v.scalar_tensor_tensor(
                xn_tm[:], xt, 1.0, xt, op0=ALU.mult, op1=ALU.mult, accum_out=col(SSQ)))
            dve_op(lambda v: v.tensor_scalar_mul(col(M_), col(SX), 1.0 / DM), edge=True)
            dve_op(lambda v: v.tensor_scalar_mul(col(EX2), col(SSQ), 1.0 / DM), edge=True)
            dve_op(lambda v: v.tensor_tensor(col(T1_), col(M_), col(M_), op=ALU.mult), edge=True)
            dve_op(lambda v: v.tensor_tensor(col(VAR), col(EX2), col(T1_), op=ALU.subtract), edge=True)
            var_done = dve_op(lambda v: v.tensor_scalar_add(col(VAR), col(VAR), EPS_LN), edge=True)
            P.w("act", s_dv, var_done)
            P.em("act", lambda a: a.activation(col(LNV), col(VAR), AF.Ln).then_inc(s_ac.h, 1))
            lnv_done = ac_inc()
            P.w("act", s_ac, lnv_done)
            P.em("act", lambda a: a.activation(col(RSTD), col(LNV), AF.Exp, scale=-0.5).then_inc(s_ac.h, 1))
            rstd_done = ac_inc()
            nmr_done = dve_op(lambda v: v.scalar_tensor_tensor(
                col(NMR), col(M_), -1.0, col(RSTD), op0=ALU.mult, op1=ALU.mult), (s_ac, rstd_done), edge=True)
            P.w("act", s_dv, nmr_done)
            P.w("act", s_ac, rstd_done)
            def mkxn(a, xt=xt):
                a.activation(xn_tm[:], xt, AF.Identity, bias=col(NMR), scale=col(RSTD)).then_inc(s_ac.h, 1)
            P.em("act", mkxn)
            xn_done = ac_inc()
            xn_consumed_act[ct] = xn_done

            P.w("pe", s_ac, xn_done)
            P.w("pe", s_dv, b6_evac)
            for kh in range(NKH):
                def mk(t, kh=kh, sp=(kh == NKH - 1)):
                    ins = t.transpose(b6[:, kh * 128:(kh + 1) * 128], xn_tm[:, kh * 128:(kh + 1) * 128], identb[:])
                    if sp:
                        ins.then_inc(s_pe.h, 1)
                P.em("pe", mk)
            xnT_done[ct] = pe_inc()
            xn_view = xn_fm[:].rearrange("p (kh t) -> p kh t", t=TOK)[:, :, ct * 128:(ct + 1) * 128]
            b6v = b6[:, 0:1024].rearrange("p (kh t) -> p kh t", t=128)
            xn_evac[ct] = dve_op(lambda v, xv=xn_view, bv=b6v: v.tensor_copy(xv, bv), (s_pe, xnT_done[ct]))
            b6_evac = xn_evac[ct]

            pairs = []
            for kh in range(NKH):
                pairs.append((
                    (lambda kh=kh, ct=ct: xn_fm[:, kh * TOK + ct * 128:kh * TOK + ct * 128 + 128]),
                    (lambda kh=kh: gwT[:, kh * 8:(kh + 1) * 8]),
                ))
            P.w("pe", s_dv, xn_evac[ct])
            P.w("pe", s_dv, gwt_rdy)
            gate_mm = mm_group(lambda: b3[:, 0:8], pairs)
            gsc = gtmp[:, 0:8]
            eq1 = gtmp[:, 8:16]
            scm = gtmp[:, 16:24]
            eq2 = gtmp[:, 24:32]
            dve_op(lambda v: v.tensor_copy(gsc, b3[:, 0:8]), (s_pe, gate_mm), edge=True)
            gsc_done = last_dve
            dve_op(lambda v: v.reduce_max(col(M_), gsc, axis=AX.X), edge=True)
            dve_op(lambda v: v.tensor_scalar(eq1, gsc, col(M_), None, op0=ALU.is_equal), edge=True)
            dve_op(lambda v: v.scalar_tensor_tensor(scm, eq1, -1e30, gsc, op0=ALU.mult, op1=ALU.add), edge=True)
            dve_op(lambda v: v.reduce_max(col(T1_), scm, axis=AX.X), edge=True)
            dve_op(lambda v: v.tensor_scalar(eq2, scm, col(T1_), None, op0=ALU.is_equal), edge=True)
            d_done = dve_op(lambda v: v.tensor_tensor(col(D_), col(T1_), col(M_), op=ALU.subtract), edge=True)
            P.w("act", s_dv, d_done)
            P.em("act", lambda a: a.activation(col(ED), col(D_), AF.Exp).then_inc(s_ac.h, 1))
            ed_done = ac_inc()
            dve_op(lambda v: v.tensor_scalar_add(col(DEN0), col(ED), 1.0), (s_ac, ed_done), edge=True)
            dve_op(lambda v: v.reciprocal(col(S1_), col(DEN0)), edge=True)
            dve_op(lambda v: v.tensor_tensor(col(S2_), col(ED), col(S1_), op=ALU.mult), edge=True)
            combv = combs[:, ct * E:(ct + 1) * E]
            dve_op(lambda v, cv=combv: v.tensor_scalar(cv, eq1, col(S1_), None, op0=ALU.mult), edge=True)
            comb_done = dve_op(lambda v, cv=combv: v.scalar_tensor_tensor(
                cv, eq2, col(S2_), cv, op0=ALU.mult, op1=ALU.add), edge=True)

            kv_banks = [b0, b1]
            for sl in range(4):
                bank = kv_banks[sl % 2]
                pairs = []
                for kh in range(NKH):
                    pairs.append((
                        (lambda kh=kh, ct=ct: xn_fm[:, kh * TOK + ct * 128:kh * TOK + ct * 128 + 128]),
                        (lambda kh=kh, sl=sl: wqTA(kh, 1024 + sl * 512, 512)),
                    ))
                P.w("pe", d_wq, wq_kv_rdy)
                P.w("pe", s_dv, kv_bank_free[sl % 2])
                kv_mm = mm_group(lambda b=bank: b[:], pairs)
                if sl < 2:
                    tpar = sl % 2
                    tmn = tmin[:, tpar * 512:(tpar + 1) * 512]
                    tex = texp[:, tpar * 512:(tpar + 1) * 512]
                    mn_done = dve_op(lambda v, b=bank, t=tmn: v.tensor_scalar_min(t, b[:], 0.0),
                                     (s_pe, kv_mm), (s_ac, exp_free[tpar]))
                    P.w("act", s_dv, mn_done)
                    def mke(a, tmn=tmn, tex=tex):
                        a.activation(tex, tmn, AF.Exp).then_inc(s_ac.h, 1)
                    P.em("act", mke)
                    exp_done = ac_inc()
                    exp_free[tpar] = exp_done
                    kslice = kp_tm[:, sl * 512:(sl + 1) * 512]
                    kv_bank_free[sl % 2] = dve_op(
                        lambda v, b=bank, tex=tex, ks=kslice: v.scalar_tensor_tensor(
                            ks, b[:], 0.0, tex, op0=ALU.max, op1=ALU.add),
                        (s_ac, exp_done), (s_pe, kv_war), edge=True)
                else:
                    vslice = v_tm[:, (sl - 2) * 512:(sl - 1) * 512]
                    kv_bank_free[sl % 2] = dve_op(lambda v, b=bank, vs=vslice: v.tensor_copy(vs, b[:]),
                                                  (s_pe, kv_mm), (s_pe, kv_war))
            kv_done = last_dve
            for h in range(NH):
                P.w("pe", s_dv, kv_done)
                if part_mm > 0:
                    P.w("pe", s_dv, part_add)
                def mk1(t, h=h):
                    t.matmul(b3[0:64, 64:128], kp_tm[:, h * 64:(h + 1) * 64], v_tm[:, h * 64:(h + 1) * 64],
                             start=True, stop=True)
                P.em("pe", mk1)
                def mk2(t, h=h):
                    t.matmul(b3[0:64, 128:129], kp_tm[:, h * 64:(h + 1) * 64], onesb[:],
                             start=True, stop=True).then_inc(s_pe.h, 1)
                P.em("pe", mk2)
                part_mm = pe_inc()
                spv = spart[:, h * 65:(h + 1) * 65]
                part_add = dve_op(lambda v, spv=spv: v.tensor_tensor(spv, spv, b3[0:64, 64:129], op=ALU.add),
                                  (s_pe, part_mm), (s_dv, spart_zero))
            kv_war = part_mm
        p1_pe_end = s_pe.n
        partials_done = last_dve

        P.w("sync", s_dv, partials_done)
        P.em("sync", lambda s: s.dma_start(out=cc_in[:], in_=spart[:]).then_inc(d_cs.h, 16)); d_cs.n += 16
        P.w("gp", d_cs, d_cs.n)
        P.em("gp", lambda g: g.collective_compute(
            "AllGather", ALU.bypass, replica_groups=[[0, 1, 2, 3, 4, 5, 6, 7]],
            ins=[cc_in[:]], outs=[cc_out[:]]).then_inc(s_cc.h, 1)); s_cc.n += 1
        P.w("gp", s_cc, s_cc.n)
        P.w("dve", d_ct, consts_rdy)
        for r in range(8):
            def mkg(g, r=r):
                g.dma_start(out=sgath[:], in_=cc_out[r * 64:(r + 1) * 64, :]).then_inc(d_ms.h, 16)
            P.em("gp", mkg)
            d_ms.n += 16
            gath_rdy = d_ms.n
            acc_done = dve_op(lambda v, r=r: v.scalar_tensor_tensor(
                saug[:], sgath[:], wselb[:, r:r + 1], saug[:],
                op0=ALU.mult, op1=ALU.add), (d_ms, gath_rdy), edge=True)
            if r < 7:
                P.w("gp", s_dv, acc_done)
        saug_bf_rdy = dve_op(lambda v: v.tensor_copy(saugb[:], saug[:]), edge=True)
        saug_rdy_h = [saug_bf_rdy] * NH

        ffn_state = {
            "sil_par": 0,
            "g_pe_done": [0, 0],
            "y_evac": 0,
            "g_hist": [0, 0],
            "sil_free": [0, 0],
            "bH_free": [0, 0],
            "bU_free": [0, 0],
            "fjg": 0,
            "last_g_pe": 0,
            "w13_loaded": {},
            "w2_loaded": {},
            "span_pe_done": {},
        }

        def load_w13(k, wait_pe=0):
            e_, sp_ = divmod(k, NSP)
            slot = k % 2
            dw = d_w[slot]
            if wait_pe:
                P.w("gp", s_pe, wait_pe)
            def mk(g, e_=e_, sp_=sp_, slot=slot, dw=dw):
                src1 = w1[e_].rearrange("(kh p) f -> p kh f", p=128)[:, :, sp_ * SPAN:(sp_ + 1) * SPAN]
                dst1 = wreg[:, slot * NKH * SPAN:(slot + 1) * NKH * SPAN].rearrange("p (kh f) -> p kh f", f=SPAN)
                g.dma_start(out=dst1, in_=src1).then_inc(dw.h, 16)
                src3 = w3[e_].rearrange("(kh p) f -> p kh f", p=128)[:, :, sp_ * SPAN:(sp_ + 1) * SPAN]
                dst3 = wreg[:, 8192 + slot * NKH * SPAN:8192 + (slot + 1) * NKH * SPAN].rearrange(
                    "p (kh f) -> p kh f", f=SPAN)
                g.dma_start(out=dst3, in_=src3).then_inc(dw.h, 16)
            P.em("gp", mk)
            dw.n += 32
            ffn_state["w13_loaded"][k] = dw.n

        def load_w2s(k, wait_pe=0):
            e_, sp_ = divmod(k, NSP)
            slot = k % 2
            dw = d_w2[slot]
            if wait_pe:
                P.w("gp", s_pe, wait_pe)
            def mk2(g, e_=e_, sp_=sp_, slot=slot, dw=dw):
                src2 = w2[e_].rearrange("(kf p) n -> p kf n", p=128)[:, sp_ * NFJ:(sp_ + 1) * NFJ, :]
                dst2 = wreg[:, 16384 + slot * NFJ * DM:16384 + (slot + 1) * NFJ * DM].rearrange(
                    "p (kf n) -> p kf n", n=DM)
                g.dma_start(out=dst2, in_=src2).then_inc(dw.h, 16)
            P.em("gp", mk2)
            dw.n += 16
            ffn_state["w2_loaded"][k] = dw.n

        def ffn_chunk(tc, pe_gate, pre_dv=(), pre_ac=()):
            st = ffn_state
            if tc == 1:
                P.w("sync", d_st, 16)
            xsrc = xs.rearrange("(tt p) h -> p tt h", p=128)[:, tc * NTT:(tc + 1) * NTT, :]
            P.em("sync", lambda s, xsrc=xsrc: s.dma_start(
                out=y_acc[:].rearrange("p (tt h) -> p tt h", h=DM), in_=xsrc).then_inc(d_yx.h, 16))
            d_yx.n += 16
            yx_rdy = d_yx.n

            st["w13_loaded"] = {}
            st["w2_loaded"] = {}
            st["span_pe_done"] = {}
            if tc == 0:
                load_w13(0)
                load_w13(1)
                load_w2s(0, wait_pe=pe_gate)
                load_w2s(1)
            else:
                load_w13(0, wait_pe=pe_gate)
                load_w13(1)
                load_w2s(0)
                load_w2s(1)

            bH = [b0, b4]
            bU = [b1, b5]
            for k in range(NSPANS):
                e_, sp_ = divmod(k, NSP)
                slot = k % 2
                half = k % 2
                P.w("pe", d_w[slot], st["w13_loaded"][k])
                if k == 0:
                    P.w("pe", s_dv, xn_evac[NTT * (tc + 1) - 1])
                    for v_ in pre_dv:
                        P.w("pe", s_dv, v_)
                    for v_ in pre_ac:
                        P.w("pe", s_ac, v_)
                for tg in range(2):
                    gpar = tg % 2
                    tof = tc * CTOK + tg * 512
                    for fj in range(NFJ):
                        spar = st["sil_par"]
                        bp = st["fjg"] % 2
                        P.w("pe", s_dv, st["g_hist"][bp])
                        pairs = [((lambda kh=kh, slot=slot, fj=fj: w1b(slot, kh, fj * 128, 128)),
                                  (lambda kh=kh, tof=tof: xn_fm[:, kh * TOK + tof:kh * TOK + tof + 512]))
                                 for kh in range(NKH)]
                        h_done = mm_group(lambda bp=bp: bH[bp][:], pairs)
                        P.w("act", s_pe, h_done)
                        P.w("act", s_dv, st["sil_free"][spar])
                        def mks(a, spar=spar, bp=bp):
                            a.activation(sil[:, spar * 512:(spar + 1) * 512], bH[bp][:], AF.Silu).then_inc(s_ac.h, 1)
                        P.em("act", mks)
                        st["bH_free"][bp] = ac_inc()
                        sil_done = s_ac.n
                        pairs = [((lambda kh=kh, slot=slot, fj=fj: w3b(slot, kh, fj * 128, 128)),
                                  (lambda kh=kh, tof=tof: xn_fm[:, kh * TOK + tof:kh * TOK + tof + 512]))
                                 for kh in range(NKH)]
                        u_done = mm_group(lambda bp=bp: bU[bp][:], pairs)
                        gd = gbuf[:, (gpar * 8 + half * NFJ + fj) * 512:(gpar * 8 + half * NFJ + fj) * 512 + 512]
                        g_done = dve_op(
                            lambda v, gd=gd, spar=spar, bp=bp: v.tensor_tensor(
                                gd, sil[:, spar * 512:(spar + 1) * 512], bU[bp][:], op=ALU.mult),
                            (s_ac, sil_done), (s_pe, u_done), (s_pe, st["g_pe_done"][gpar]))
                        st["g_hist"][bp] = g_done
                        st["bU_free"][bp] = g_done
                        st["sil_free"][spar] = g_done
                        st["sil_par"] ^= 1
                        st["fjg"] += 1
                if k + 2 < NSPANS:
                    load_w13(k + 2, wait_pe=s_pe.n)
                if k % 2 == 0:
                    continue
                g_rdy = s_dv.n
                P.w("pe", d_w2[0], st["w2_loaded"][k - 1])
                P.w("pe", d_w2[1], st["w2_loaded"][k])
                P.w("pe", s_dv, g_rdy)
                for tg in range(2):
                    gpar = tg % 2
                    for tt in range(4):
                        ttg = tg * 4 + tt
                        ct_g = tc * NTT + ttg
                        P.w("pe", s_dv, st["y_evac"])
                        for ht in range(2):
                            pairs = []
                            for hf in range(2):
                                for fj in range(NFJ):
                                    pairs.append((
                                        (lambda fj=fj, gpar=gpar, tt=tt, hf=hf: gbuf[
                                            :, (gpar * 8 + hf * NFJ + fj) * 512 + tt * 128:
                                               (gpar * 8 + hf * NFJ + fj) * 512 + tt * 128 + 128]),
                                        (lambda fj=fj, hf=hf, ht=ht: w2b(hf, fj, ht * 512, 512)),
                                    ))
                            ymm = mm_group(lambda ht=ht: py[:, ht * 512:(ht + 1) * 512], pairs)
                        dst = y_acc[:, ttg * DM:ttg * DM + DM]
                        cslice = combs[:, ct_g * E + e_:ct_g * E + e_ + 1]
                        st["y_evac"] = dve_op(
                            lambda v, dst=dst, cs=cslice: v.scalar_tensor_tensor(
                                dst, py[:], cs, dst, op0=ALU.mult, op1=ALU.add),
                            (s_pe, ymm), (d_yx, yx_rdy), (s_dv, comb_done))
                    st["g_pe_done"][gpar] = s_pe.n
                g_end = s_pe.n
                st["span_pe_done"][k - 1] = g_end
                st["span_pe_done"][k] = g_end
                if k + 2 < NSPANS:
                    load_w2s(k + 1, wait_pe=g_end)
                    load_w2s(k + 2)
            return s_pe.n
            return s_pe.n

        scan_state = {
            "msc_free": [0, 0],
            "rec_free": [0, 0],
            "b2_free": 0,
            "b4_free": 0,
            "b5_free": 0,
            "b3_free": 0,
            "qkv_bank_free": [0, 0],
            "exp_free": [0, 0],
            "head_par": 0,
            "qp_war": 0,
            "kv_war": 0,
        }

        def scan_chunk(tc, wq_rdy, pre_ac=0, pre_dv=0):
            nonlocal b6_evac, saug_bf_rdy
            ss = scan_state
            proj_evac = 0
            for ctl in range(8):
                ct = tc * 8 + ctl
                for sl in range(2):
                    bank = [b0, b1][sl % 2]
                    pairs = [((lambda kh=kh, ct=ct: xn_fm[:, kh * TOK + ct * 128:kh * TOK + ct * 128 + 128]),
                              (lambda kh=kh, sl=sl: wqT(kh, sl * 512, 512)))
                             for kh in range(NKH)]
                    P.w("pe", d_wq, wq_rdy)
                    P.w("pe", s_dv, ss["qkv_bank_free"][sl % 2])
                    if ctl == 0 and sl == 0:
                        P.w("pe", s_ac, pre_ac)
                        P.w("pe", s_dv, pre_dv)
                    q_mm = mm_group(lambda b=bank: b[:], pairs)
                    tpar = sl % 2
                    tmn = tmin[:, tpar * 512:(tpar + 1) * 512]
                    tex = texp[:, tpar * 512:(tpar + 1) * 512]
                    mn_done = dve_op(lambda v, b=bank, t=tmn: v.tensor_scalar_min(t, b[:], 0.0),
                                     (s_pe, q_mm), (s_ac, ss["exp_free"][tpar]))
                    P.w("act", s_dv, mn_done)
                    def mke(a, tmn=tmn, tex=tex):
                        a.activation(tex, tmn, AF.Exp).then_inc(s_ac.h, 1)
                    P.em("act", mke)
                    exp_done = ac_inc()
                    ss["exp_free"][tpar] = exp_done
                    qslice = qp_tm[:, sl * 512:(sl + 1) * 512]
                    ss["qkv_bank_free"][sl % 2] = dve_op(
                        lambda v, b=bank, tex=tex, qs=qslice: v.scalar_tensor_tensor(
                            qs, b[:], 0.0, tex, op0=ALU.max, op1=ALU.add),
                        (s_ac, exp_done), (s_pe, ss["qp_war"]), edge=True)
                for sl in range(4):
                    bank = [b0, b1][sl % 2]
                    pairs = [((lambda kh=kh, ct=ct: xn_fm[:, kh * TOK + ct * 128:kh * TOK + ct * 128 + 128]),
                              (lambda kh=kh, sl=sl: wqT(kh, 1024 + sl * 512, 512)))
                             for kh in range(NKH)]
                    P.w("pe", s_dv, ss["qkv_bank_free"][sl % 2])
                    kv_mm = mm_group(lambda b=bank: b[:], pairs)
                    if sl < 2:
                        tpar = sl % 2
                        tmn = tmin[:, tpar * 512:(tpar + 1) * 512]
                        tex = texp[:, tpar * 512:(tpar + 1) * 512]
                        mn_done = dve_op(lambda v, b=bank, t=tmn: v.tensor_scalar_min(t, b[:], 0.0),
                                         (s_pe, kv_mm), (s_ac, ss["exp_free"][tpar]))
                        P.w("act", s_dv, mn_done)
                        def mke(a, tmn=tmn, tex=tex):
                            a.activation(tex, tmn, AF.Exp).then_inc(s_ac.h, 1)
                        P.em("act", mke)
                        exp_done = ac_inc()
                        ss["exp_free"][tpar] = exp_done
                        kslice = kp_tm[:, sl * 512:(sl + 1) * 512]
                        ss["qkv_bank_free"][sl % 2] = dve_op(
                            lambda v, b=bank, tex=tex, ks=kslice: v.scalar_tensor_tensor(
                                ks, b[:], 0.0, tex, op0=ALU.max, op1=ALU.add),
                            (s_ac, exp_done), (s_pe, ss["kv_war"]), edge=True)
                    else:
                        vslice = v_tm[:, (sl - 2) * 512:(sl - 1) * 512]
                        ss["qkv_bank_free"][sl % 2] = dve_op(
                            lambda v, b=bank, vs=vslice: v.tensor_copy(vs, b[:]), (s_pe, kv_mm), (s_pe, ss["kv_war"]))
                qkv_rdy = last_dve

                for (srct, dstv) in ((qp_tm, qp_fm), (kp_tm, kp_fm)):
                    for half in range(2):
                        P.w("pe", s_dv, qkv_rdy)
                        P.w("pe", s_dv, b6_evac)
                        for hh in range(8):
                            h = half * 8 + hh
                            def mk(t, srct=srct, h=h, hh=hh, sp=(hh == 7)):
                                ins = t.transpose(b6[0:64, hh * 128:(hh + 1) * 128],
                                                  srct[:, h * 64:(h + 1) * 64], identb[:])
                                if sp:
                                    ins.then_inc(s_pe.h, 1)
                            P.em("pe", mk)
                        trd = pe_inc()
                        b6_evac = dve_op(lambda v, dstv=dstv, half=half: v.tensor_copy(
                            dstv[:, half * 1024:(half + 1) * 1024], b6[0:64, 0:1024]), (s_pe, trd))
                fm_rdy = b6_evac
                ss["qp_war"] = s_pe.n

                for h in range(NH):
                    hpar = ss["head_par"]
                    qpf = qp_fm[:, h * 128:(h + 1) * 128]
                    kpf = kp_fm[:, h * 128:(h + 1) * 128]
                    P.w("pe", s_dv, fm_rdy)
                    P.w("pe", s_dv, ss["b2_free"])
                    def mksc(t, kpf=kpf, qpf=qpf):
                        t.matmul(b2[:, 0:128], kpf, qpf, start=True, stop=True).then_inc(s_pe.h, 1)
                    P.em("pe", mksc)
                    sc_mm = pe_inc()
                    mslice = msc[:, hpar * 128:(hpar + 1) * 128]
                    P.w("dve", s_pe, ss["msc_free"][hpar])
                    ss["b2_free"] = dve_op(
                        lambda v, ms=mslice: v.tensor_tensor(ms, b2[:, 0:128], maskb[:], op=ALU.mult),
                        (s_pe, sc_mm), (d_ct, consts_rdy))
                    mask_done = ss["b2_free"]
                    P.w("pe", s_dv, mask_done)
                    P.w("pe", s_ac, ss["b4_free"])
                    P.w("pe", s_dv, saug_rdy_h[h])
                    def mkn1(t, ms=mslice, h=h):
                        t.matmul(b4[:, 0:64], ms, v_tm[:, h * 64:(h + 1) * 64], start=True, stop=False)
                    P.em("pe", mkn1)
                    def mkn2(t, qpf=qpf, h=h):
                        t.matmul(b4[:, 0:64], qpf, saugb[:, h * 65:h * 65 + 64],
                                 start=False, stop=True).then_inc(s_pe.h, 1)
                    P.em("pe", mkn2)
                    num_mm = pe_inc()
                    P.w("pe", s_dv, ss["b5_free"])
                    def mkd1(t, ms=mslice):
                        t.matmul(b5[:, 0:1], ms, onesb[:], start=True, stop=False)
                    P.em("pe", mkd1)
                    def mkd2(t, qpf=qpf, h=h):
                        t.matmul(b5[:, 0:1], qpf, saugb[:, h * 65 + 64:h * 65 + 65],
                                 start=False, stop=True).then_inc(s_pe.h, 1)
                    P.em("pe", mkd2)
                    den_mm = pe_inc()
                    ss["msc_free"][hpar] = den_mm
                    P.w("pe", s_dv, ss["b3_free"])
                    def mks1(t, h=h):
                        t.matmul(b3[0:64, 64:128], kp_tm[:, h * 64:(h + 1) * 64], v_tm[:, h * 64:(h + 1) * 64],
                                 start=True, stop=True)
                    P.em("pe", mks1)
                    def mks2(t, h=h):
                        t.matmul(b3[0:64, 128:129], kp_tm[:, h * 64:(h + 1) * 64], onesb[:],
                                 start=True, stop=True).then_inc(s_pe.h, 1)
                    P.em("pe", mks2)
                    ds_mm = pe_inc()
                    rc = REC0 if hpar == 0 else REC1
                    dn = DEN0 if hpar == 0 else DEN1
                    P.w("dve", s_ac, ss["rec_free"][hpar])
                    dve_op(lambda v, dn=dn: v.tensor_scalar_add(col(dn), b5[:, 0:1], EPS_DEN), (s_pe, den_mm))
                    ss["b5_free"] = last_dve
                    rec_done = dve_op(lambda v, rc=rc, dn=dn: v.reciprocal(col(rc), col(dn)), edge=True)
                    P.w("act", s_pe, num_mm)
                    P.w("act", s_dv, rec_done)
                    def mky(a, h=h, rc=rc):
                        a.activation(y_tm[:, h * 64:(h + 1) * 64], b4[:, 0:64], AF.Identity,
                                     scale=col(rc)).then_inc(s_ac.h, 1)
                    P.em("act", mky)
                    ss["b4_free"] = ac_inc()
                    ss["rec_free"][hpar] = ss["b4_free"]
                    yh_done = ss["b4_free"]
                    sgv = saug[:, h * 65:(h + 1) * 65]
                    dve_op(lambda v, sgv=sgv: v.tensor_tensor(sgv, sgv, b3[0:64, 64:129], op=ALU.add),
                           (s_pe, ds_mm), (s_dv, saug_rdy_h[h]))
                    ss["b3_free"] = last_dve
                    sgbv = saugb[:, h * 65:(h + 1) * 65]
                    saug_rdy_h[h] = dve_op(lambda v, sgv=sgv, sgbv=sgbv: v.tensor_copy(sgbv, sgv), edge=True)
                    ss["kv_war"] = ds_mm
                    ss["head_par"] ^= 1
                y_all = s_ac.n

                P.w("pe", s_ac, y_all)
                P.w("pe", s_dv, b6_evac)
                for j in range(NKH):
                    def mk(t, j=j, sp=(j == NKH - 1)):
                        ins = t.transpose(b6[:, j * 128:(j + 1) * 128], y_tm[:, j * 128:(j + 1) * 128], identb[:])
                        if sp:
                            ins.then_inc(s_pe.h, 1)
                    P.em("pe", mk)
                ytr = pe_inc()
                b6_evac = dve_op(lambda v: v.tensor_copy(y_fm[:], b6[:, 0:1024]), (s_pe, ytr))
                yfm_rdy = b6_evac
                for cs in range(2):
                    bank = b7 if cs == 0 else b2
                    bid = "b7" if cs == 0 else "b2"
                    pairs = [((lambda j=j: y_fm[:, j * 128:(j + 1) * 128]),
                              (lambda j=j, cs=cs: wpT(j, cs * 512, 512)))
                             for j in range(NKH)]
                    P.w("pe", s_dv, yfm_rdy)
                    if cs == 1:
                        P.w("pe", s_dv, ss["b2_free"])
                        P.w("pe", s_dv, proj_evac)
                    else:
                        P.w("pe", s_dv, proj_evac)
                    pj_mm = mm_group(lambda b=bank: b[:], pairs)
                    dst = y_acc[:, ctl * DM + cs * 512:ctl * DM + cs * 512 + 512]
                    proj_evac = dve_op(lambda v, dst=dst, b=bank: v.scalar_tensor_tensor(
                        dst, b[:], 1.0, dst, op0=ALU.mult, op1=ALU.add), (s_pe, pj_mm))
                    if cs == 1:
                        ss["b2_free"] = proj_evac
            return proj_evac

        ffn0_pe_end = ffn_chunk(0, pe_gate=p1_pe_end, pre_dv=(kv_bank_free[0], kv_bank_free[1]))
        P.w("sync", s_pe, ffn0_pe_end)
        for kh in range(NKH):
            def mk(s, kh=kh):
                s.dma_start(out=wqT(kh, 0, 3072), in_=wattn_bf[:, kh * 128:(kh + 1) * 128],
                            transpose=True).then_inc(d_wq.h, 16)
            P.em("sync", mk)
            d_wq.n += 16
        for kh in range(NKH):
            def mk(s, kh=kh):
                s.dma_start(out=wpT(kh, 0, 1024), in_=wp_bf[:, kh * 128:(kh + 1) * 128],
                            transpose=True).then_inc(d_wq.h, 16)
            P.em("sync", mk)
            d_wq.n += 16
        wq_full0 = d_wq.n
        scan_state["kv_war"] = p1_pe_end
        scan_state["b5_free"] = max(ffn_state["bU_free"])
        scan0_end = scan_chunk(0, wq_full0, pre_ac=max(ffn_state["bH_free"]), pre_dv=max(ffn_state["bU_free"]))
        P.w("sync", s_dv, scan0_end)
        yo0 = yout.rearrange("(tt p) h -> p tt h", p=128)[:, 0:NTT, :]
        P.em("sync", lambda s, yo0=yo0: s.dma_start(
            out=yo0, in_=y_acc[:].rearrange("p (tt h) -> p tt h", h=DM)).then_inc(d_st.h, 16))
        d_st.n += 16

        scan0_pe_end = s_pe.n
        ffn1_pe_end = ffn_chunk(1, pe_gate=scan0_pe_end,
                                pre_dv=(scan_state["qkv_bank_free"][0], scan_state["qkv_bank_free"][1],
                                        scan_state["b5_free"], scan_state["b3_free"], scan_state["b2_free"]),
                                pre_ac=(scan_state["b4_free"], scan_state["rec_free"][0], scan_state["rec_free"][1]))
        P.w("sync", s_pe, ffn1_pe_end)
        for kh in range(NKH):
            def mk(s, kh=kh):
                s.dma_start(out=wqT(kh, 0, 3072), in_=wattn_bf[:, kh * 128:(kh + 1) * 128],
                            transpose=True).then_inc(d_wq.h, 16)
            P.em("sync", mk)
            d_wq.n += 16
        for kh in range(NKH):
            def mk(s, kh=kh):
                s.dma_start(out=wpT(kh, 0, 1024), in_=wp_bf[:, kh * 128:(kh + 1) * 128],
                            transpose=True).then_inc(d_wq.h, 16)
            P.em("sync", mk)
            d_wq.n += 16
        wq_full1 = d_wq.n
        scan_state["b5_free"] = max(scan_state["b5_free"], max(ffn_state["bU_free"]))
        scan1_end = scan_chunk(1, wq_full1, pre_ac=max(ffn_state["bH_free"]), pre_dv=max(ffn_state["bU_free"]))
        P.w("sync", s_dv, scan1_end)
        yo1 = yout.rearrange("(tt p) h -> p tt h", p=128)[:, NTT:2 * NTT, :]
        P.em("sync", lambda s, yo1=yo1: s.dma_start(
            out=yo1, in_=y_acc[:].rearrange("p (tt h) -> p tt h", h=DM)).then_inc(d_st.h, 16))
        d_st.n += 16
        P.w("sync", d_st, d_st.n)

        with nc.Block() as block:
            @block.tensor
            def _(t):
                for f in P.ops["pe"]:
                    f(t)

            @block.scalar
            def _(a):
                for f in P.ops["act"]:
                    f(a)

            @block.vector
            def _(v):
                for f in P.ops["dve"]:
                    f(v)

            @block.sync
            def _(s):
                for f in P.ops["sync"]:
                    f(s)

            @block.gpsimd
            def _(g):
                for f in P.ops["gp"]:
                    f(g)

    return nc


_NC_CACHE = {}


def kernel(L0_attn_x, L0_attn_w_attn, L0_attn_w_proj, L0_ffn_gate_w, L0_ffn_w1, L0_ffn_w3, L0_ffn_w2):
    x = np.asarray(L0_attn_x)
    w_attn = np.ascontiguousarray(np.asarray(L0_attn_w_attn), dtype=np.float32)
    w_proj = np.ascontiguousarray(np.asarray(L0_attn_w_proj), dtype=np.float32)
    gate_w = np.ascontiguousarray(np.asarray(L0_ffn_gate_w), dtype=np.float32)
    w1 = np.ascontiguousarray(np.asarray(L0_ffn_w1), dtype=np.float32)
    w3 = np.ascontiguousarray(np.asarray(L0_ffn_w3), dtype=np.float32)
    w2 = np.ascontiguousarray(np.asarray(L0_ffn_w2), dtype=np.float32)

    ident = np.eye(128, dtype=ml_dtypes.bfloat16)
    maskT = np.triu(np.ones((128, 128))).astype(ml_dtypes.bfloat16)
    onesc = np.ones((128, 1), dtype=ml_dtypes.bfloat16)

    if "nc" not in _NC_CACHE:
        _NC_CACHE["nc"] = build_nc()
    nc = _NC_CACHE["nc"]

    in_maps = []
    for c in range(8):
        b, seg = divmod(c, 4)
        xs = np.ascontiguousarray(x[b, seg * TOK:(seg + 1) * TOK, :], dtype=np.float32)
        wsel = np.zeros((64, 8), np.float32)
        for r in range(8):
            rb, rseg = divmod(r, 4)
            if rb == b and rseg < seg:
                wsel[:, r] = 1.0
        in_maps.append({
            "xs": xs, "w_attn": w_attn, "w_proj": w_proj, "gate_w": gate_w,
            "w1": w1, "w3": w3, "w2": w2,
            "ident": ident, "maskT": maskT, "onesc": onesc, "wsel": wsel,
        })

    res = run_bass_kernel_spmd(nc, in_maps, core_ids=list(range(8)))
    out = np.empty((2, 8192, 1024), np.float32)
    for c in range(8):
        b, seg = divmod(c, 4)
        out[b, seg * TOK:(seg + 1) * TOK, :] = np.asarray(res.results[c]["yout"])
    return out


# version 07
# speedup vs baseline: 1.3576x; 1.0810x over previous
import contextlib
import numpy as np
import ml_dtypes
import concourse.bass as bass
import concourse.mybir as mybir
from concourse.bass_utils import run_bass_kernel_spmd

F32 = mybir.dt.float32
BF16 = mybir.dt.bfloat16
AF = mybir.ActivationFunctionType
ALU = mybir.AluOpType
AX = mybir.AxisListType

TOK = 2048
NCT = 16
DM = 1024
NKH = 8
NH = 16
HD = 64
E = 8
FF = 4096
SPAN = 512
NSP = FF // SPAN
NFJ = SPAN // 128
NSPANS = E * NSP
NCH = 2
CTOK = 1024
NTT = 8
EPS_LN = 1e-5
EPS_DEN = 1e-6


class Sem:
    def __init__(self, es, nc, name):
        self.h = es.enter_context(nc.semaphore(name))
        self.n = 0


class B:

    def __init__(self):
        self.ops = {k: [] for k in ("pe", "act", "dve", "sync", "gp")}

    def em(self, eng, fn):
        self.ops[eng].append(fn)

    def w(self, eng, sem, val):
        if val > 0:
            self.ops[eng].append(lambda e, s=sem.h, v=val: e.wait_ge(s, v))


def build_nc():
    nc = bass.Bass("TRN2", target_bir_lowering=False)
    xs = nc.dram_tensor("xs", [TOK, DM], F32, kind="ExternalInput").ap()
    w_attn = nc.dram_tensor("w_attn", [3 * DM, DM], F32, kind="ExternalInput").ap()
    w_proj = nc.dram_tensor("w_proj", [DM, DM], F32, kind="ExternalInput").ap()
    gate_w = nc.dram_tensor("gate_w", [E, DM], F32, kind="ExternalInput").ap()
    w1 = nc.dram_tensor("w1", [E, DM, FF], F32, kind="ExternalInput").ap()
    w3 = nc.dram_tensor("w3", [E, DM, FF], F32, kind="ExternalInput").ap()
    w2 = nc.dram_tensor("w2", [E, FF, DM], F32, kind="ExternalInput").ap()
    ident = nc.dram_tensor("ident", [128, 128], BF16, kind="ExternalInput").ap()
    maskT = nc.dram_tensor("maskT", [128, 128], BF16, kind="ExternalInput").ap()
    onesc = nc.dram_tensor("onesc", [128, 1], BF16, kind="ExternalInput").ap()
    wsel = nc.dram_tensor("wsel", [64, 8], F32, kind="ExternalInput").ap()
    yout = nc.dram_tensor("yout", [TOK, DM], F32, kind="ExternalOutput").ap()
    wattn_bf = nc.dram_tensor("wattn_bf", [3 * DM, DM], BF16, kind="Internal").ap()
    w1bf = nc.dram_tensor("w1bf", [E, DM, FF], BF16, kind="Internal").ap()
    w3bf = nc.dram_tensor("w3bf", [E, DM, FF], BF16, kind="Internal").ap()
    w2bf = nc.dram_tensor("w2bf", [E, FF, DM], BF16, kind="Internal").ap()
    wp_bf = nc.dram_tensor("wp_bf", [DM, DM], BF16, kind="Internal").ap()
    cc_in = nc.dram_tensor("cc_in", [64, NH * 65], F32, kind="Internal").ap()
    cc_out = nc.dram_tensor("cc_out", [8 * 64, NH * 65], F32, kind="Internal").ap()

    es = contextlib.ExitStack()
    sb = lambda n, s, d: es.enter_context(nc.sbuf_tensor(n, s, d))
    ps = lambda n, s, d: es.enter_context(nc.psum_tensor(n, s, d))
    with es:
        xn_fm = sb("xn_fm", [128, NKH * TOK], BF16)
        wreg = sb("wreg", [128, 32768], BF16)
        y_acc = sb("y_acc", [128, NTT * DM], F32)
        combs = sb("combs", [128, NCT * E], F32)
        saug = sb("saug", [64, NH * 65], F32)
        saugb = sb("saugb", [64, NH * 65], BF16)
        spart = sb("spart", [64, NH * 65], F32)
        sgath = sb("sgath", [64, NH * 65], F32)
        wselb = sb("wselb", [64, 8], F32)
        xsb = sb("xsb", [128, 2 * DM], F32)
        xn_tm = sb("xn_tm", [128, DM], BF16)
        lns = sb("lns", [128, 24], F32)
        gtmp = sb("gtmp", [128, 32], F32)
        gw8 = sb("gw8", [16, DM], BF16)
        gwT = sb("gwT", [128, NKH * E], BF16)
        identb = sb("identb", [128, 128], BF16)
        maskb = sb("maskb", [128, 128], BF16)
        onesb = sb("onesb", [128, 1], BF16)
        kp_tm = sb("kp_tm", [128, DM], BF16)
        v_tm = sb("v_tm", [128, DM], BF16)
        qp_tm = sb("qp_tm", [128, DM], BF16)
        qp_fm = sb("qp_fm", [64, NH * 128], BF16)
        kp_fm = sb("kp_fm", [64, NH * 128], BF16)
        y_tm = sb("y_tm", [128, DM], BF16)
        y_fm = sb("y_fm", [128, DM], BF16)
        tmin = sb("tmin", [128, 2 * 512], F32)
        texp = sb("texp", [128, 2 * 512], F32)
        msc = sb("msc", [128, 2 * 128], BF16)
        sil = sb("sil", [128, 2 * 512], BF16)
        gbuf = sb("gbuf", [128, 2 * 8 * 512], BF16)

        def w1b(slot, kh, fcol0, n):
            off = slot * (NKH * SPAN) + kh * SPAN + fcol0
            return wreg[:, off:off + n]

        def w3b(slot, kh, fcol0, n):
            off = 8192 + slot * (NKH * SPAN) + kh * SPAN + fcol0
            return wreg[:, off:off + n]

        def w2b(slot, kf, hcol0, n):
            off = 16384 + slot * (NFJ * DM) + kf * DM + hcol0
            return wreg[:, off:off + n]

        def wqT(kh, c0, n):
            return wreg[:, kh * 3072 + c0:kh * 3072 + c0 + n]

        def wqTA(kh, c0, n):
            return wreg[:, 16384 + kh * 2048 + (c0 - 1024):16384 + kh * 2048 + (c0 - 1024) + n]

        def wpT(kh, c0, n):
            return wreg[:, 24576 + kh * DM + c0:24576 + kh * DM + c0 + n]

        b0 = ps("b0", [128, 512], F32)
        b1 = ps("b1", [128, 512], F32)
        py = ps("py", [128, 1024], F32)
        b2 = py[:, 0:512]
        b3 = py[:, 512:1024]
        py2 = ps("py2", [128, 1024], F32)
        b4 = py2[:, 0:512]
        b5 = py2[:, 512:1024]
        b6 = ps("b6", [128, 1024], BF16)
        b7 = ps("b7", [128, 512], F32)

        d_x = [Sem(es, nc, "d_x0"), Sem(es, nc, "d_x1")]
        d_yx = Sem(es, nc, "d_yx")
        d_w = [Sem(es, nc, "d_w0"), Sem(es, nc, "d_w1")]
        d_w2 = [Sem(es, nc, "d_w2a"), Sem(es, nc, "d_w2b")]
        d_wq = Sem(es, nc, "d_wq")
        d_ms = Sem(es, nc, "d_ms")
        d_cast = Sem(es, nc, "d_cast")
        d_ct = Sem(es, nc, "d_ct")
        d_cs = Sem(es, nc, "d_cs")
        s_cc = Sem(es, nc, "s_cc")
        s_pe = Sem(es, nc, "s_pe")
        s_ac = Sem(es, nc, "s_ac")
        s_dv = Sem(es, nc, "s_dv")
        d_st = Sem(es, nc, "d_st")

        P = B()

        def pe_inc():
            s_pe.n += 1
            return s_pe.n

        def ac_inc():
            s_ac.n += 1
            return s_ac.n

        def dv_inc():
            s_dv.n += 1
            return s_dv.n

        def mm_group(bank_ap_fn, pairs, extra_first_wait=None):
            n = len(pairs)
            for i, (lf, rf) in enumerate(pairs):
                def mk(t, lf=lf, rf=rf, st=(i == 0), sp=(i == n - 1), bf=bank_ap_fn):
                    ins = t.matmul(bf(), lf(), rf(), start=st, stop=sp)
                    if sp:
                        ins.then_inc(s_pe.h, 1)
                P.em("pe", mk)
            return pe_inc()

        P.em("sync", lambda s: s.dma_start(out=identb[:], in_=ident).then_inc(d_ct.h, 16)); d_ct.n += 16
        P.em("sync", lambda s: s.dma_start(out=maskb[:], in_=maskT).then_inc(d_ct.h, 16)); d_ct.n += 16
        P.em("sync", lambda s: s.dma_start(out=onesb[:], in_=onesc).then_inc(d_ct.h, 16)); d_ct.n += 16
        P.em("sync", lambda s: s.dma_start(out=wselb[:], in_=wsel).then_inc(d_ct.h, 16)); d_ct.n += 16
        consts_rdy = d_ct.n

        P.em("gp", lambda g: g.dma_start(out=wattn_bf[:], in_=w_attn).then_inc(d_ms.h, 16)); d_ms.n += 16
        P.em("gp", lambda g: g.dma_start(out=gw8[0:8, :], in_=gate_w).then_inc(d_ms.h, 16)); d_ms.n += 16
        P.em("gp", lambda g: g.dma_start(out=wp_bf[:], in_=w_proj).then_inc(d_ms.h, 16)); d_ms.n += 16
        p0_gp = d_ms.n

        P.em("dve", lambda v: v.memset(saug[:], 0.0).then_inc(s_dv.h, 1)); dv_inc()
        P.em("dve", lambda v: v.memset(spart[:], 0.0).then_inc(s_dv.h, 1)); spart_zero = dv_inc()

        P.w("pe", d_ms, p0_gp)
        P.w("pe", d_ct, consts_rdy)
        for kh in range(NKH):
            def mk(t, kh=kh, sp=(kh == NKH - 1)):
                ins = t.transpose(b6[:, kh * 8:kh * 8 + 8], gw8[0:8, kh * 128:(kh + 1) * 128], identb[0:8, 0:8])
                if sp:
                    ins.then_inc(s_pe.h, 1)
            P.em("pe", mk)
        gwt_tr = pe_inc()
        P.w("dve", s_pe, gwt_tr)
        P.em("dve", lambda v: v.tensor_copy(gwT[:], b6[:, 0:NKH * E]).then_inc(s_dv.h, 1))
        gwt_rdy = dv_inc()

        P.w("sync", d_ms, p0_gp)
        for kh in range(NKH):
            def mk(s, kh=kh):
                s.dma_start(out=wqTA(kh, 1024, 2048), in_=wattn_bf[1024:3072, kh * 128:(kh + 1) * 128],
                            transpose=True).then_inc(d_wq.h, 16)
            P.em("sync", mk)
            d_wq.n += 16
        wq_kv_rdy = d_wq.n

        SX, SSQ, M_, EX2, T1_, VAR, LNV, RSTD, NMR = range(9)
        D_, ED, DEN0, DEN1, REC0, REC1, S1_, S2_ = range(9, 17)

        def col(i):
            return lns[:, i:i + 1]

        last_dve = 0
        exp_free = [0, 0]
        xn_evac = [0] * NCT
        xnT_done = [0] * NCT
        xn_consumed_act = [0] * NCT
        b6_evac = 0
        kv_bank_free = [0, 0]
        part_mm = 0
        kv_war = 0

        def dve_op(fn, *waits, edge=False):
            nonlocal last_dve
            for sem, val in waits:
                P.w("dve", sem, val)
            if edge:
                P.w("dve", s_dv, last_dve)
            def mk(v, fn=fn):
                fn(v).then_inc(s_dv.h, 1)
            P.em("dve", mk)
            last_dve = dv_inc()
            return last_dve

        for ct in range(NCT):
            par = ct % 2
            xt = xsb[:, par * DM:(par + 1) * DM]
            if ct >= 2:
                P.w("sync", s_ac, xn_consumed_act[ct - 2])
            def mkx(s, ct=ct, xt=xt, dx=d_x[par]):
                s.dma_start(out=xt, in_=xs[ct * 128:(ct + 1) * 128, :]).then_inc(dx.h, 16)
            P.em("sync", mkx)
            d_x[par].n += 16
            x_rdy = d_x[par].n

            sum_done = dve_op(lambda v, xt=xt: v.reduce_sum(col(SX), xt, axis=AX.X), (d_x[par], x_rdy))
            if ct >= 1:
                P.w("dve", s_pe, xnT_done[ct - 1])
            ssq_done = dve_op(lambda v, xt=xt: v.scalar_tensor_tensor(
                xn_tm[:], xt, 1.0, xt, op0=ALU.mult, op1=ALU.mult, accum_out=col(SSQ)))
            dve_op(lambda v: v.tensor_scalar_mul(col(M_), col(SX), 1.0 / DM), edge=True)
            dve_op(lambda v: v.tensor_scalar_mul(col(EX2), col(SSQ), 1.0 / DM), edge=True)
            dve_op(lambda v: v.tensor_tensor(col(T1_), col(M_), col(M_), op=ALU.mult), edge=True)
            dve_op(lambda v: v.tensor_tensor(col(VAR), col(EX2), col(T1_), op=ALU.subtract), edge=True)
            var_done = dve_op(lambda v: v.tensor_scalar_add(col(VAR), col(VAR), EPS_LN), edge=True)
            P.w("act", s_dv, var_done)
            P.em("act", lambda a: a.activation(col(LNV), col(VAR), AF.Ln).then_inc(s_ac.h, 1))
            lnv_done = ac_inc()
            P.w("act", s_ac, lnv_done)
            P.em("act", lambda a: a.activation(col(RSTD), col(LNV), AF.Exp, scale=-0.5).then_inc(s_ac.h, 1))
            rstd_done = ac_inc()
            nmr_done = dve_op(lambda v: v.scalar_tensor_tensor(
                col(NMR), col(M_), -1.0, col(RSTD), op0=ALU.mult, op1=ALU.mult), (s_ac, rstd_done), edge=True)
            P.w("act", s_dv, nmr_done)
            P.w("act", s_ac, rstd_done)
            def mkxn(a, xt=xt):
                a.activation(xn_tm[:], xt, AF.Identity, bias=col(NMR), scale=col(RSTD)).then_inc(s_ac.h, 1)
            P.em("act", mkxn)
            xn_done = ac_inc()
            xn_consumed_act[ct] = xn_done

            P.w("pe", s_ac, xn_done)
            P.w("pe", s_dv, b6_evac)
            for kh in range(NKH):
                def mk(t, kh=kh, sp=(kh == NKH - 1)):
                    ins = t.transpose(b6[:, kh * 128:(kh + 1) * 128], xn_tm[:, kh * 128:(kh + 1) * 128], identb[:])
                    if sp:
                        ins.then_inc(s_pe.h, 1)
                P.em("pe", mk)
            xnT_done[ct] = pe_inc()
            xn_view = xn_fm[:].rearrange("p (kh t) -> p kh t", t=TOK)[:, :, ct * 128:(ct + 1) * 128]
            b6v = b6[:, 0:1024].rearrange("p (kh t) -> p kh t", t=128)
            xn_evac[ct] = dve_op(lambda v, xv=xn_view, bv=b6v: v.tensor_copy(xv, bv), (s_pe, xnT_done[ct]))
            b6_evac = xn_evac[ct]

            pairs = []
            for kh in range(NKH):
                pairs.append((
                    (lambda kh=kh, ct=ct: xn_fm[:, kh * TOK + ct * 128:kh * TOK + ct * 128 + 128]),
                    (lambda kh=kh: gwT[:, kh * 8:(kh + 1) * 8]),
                ))
            P.w("pe", s_dv, xn_evac[ct])
            P.w("pe", s_dv, gwt_rdy)
            gate_mm = mm_group(lambda: b3[:, 0:8], pairs)
            gsc = gtmp[:, 0:8]
            eq1 = gtmp[:, 8:16]
            scm = gtmp[:, 16:24]
            eq2 = gtmp[:, 24:32]
            dve_op(lambda v: v.tensor_copy(gsc, b3[:, 0:8]), (s_pe, gate_mm), edge=True)
            gsc_done = last_dve
            dve_op(lambda v: v.reduce_max(col(M_), gsc, axis=AX.X), edge=True)
            dve_op(lambda v: v.tensor_scalar(eq1, gsc, col(M_), None, op0=ALU.is_equal), edge=True)
            dve_op(lambda v: v.scalar_tensor_tensor(scm, eq1, -1e30, gsc, op0=ALU.mult, op1=ALU.add), edge=True)
            dve_op(lambda v: v.reduce_max(col(T1_), scm, axis=AX.X), edge=True)
            dve_op(lambda v: v.tensor_scalar(eq2, scm, col(T1_), None, op0=ALU.is_equal), edge=True)
            d_done = dve_op(lambda v: v.tensor_tensor(col(D_), col(T1_), col(M_), op=ALU.subtract), edge=True)
            P.w("act", s_dv, d_done)
            P.em("act", lambda a: a.activation(col(ED), col(D_), AF.Exp).then_inc(s_ac.h, 1))
            ed_done = ac_inc()
            dve_op(lambda v: v.tensor_scalar_add(col(DEN0), col(ED), 1.0), (s_ac, ed_done), edge=True)
            dve_op(lambda v: v.reciprocal(col(S1_), col(DEN0)), edge=True)
            dve_op(lambda v: v.tensor_tensor(col(S2_), col(ED), col(S1_), op=ALU.mult), edge=True)
            combv = combs[:, ct * E:(ct + 1) * E]
            dve_op(lambda v, cv=combv: v.tensor_scalar(cv, eq1, col(S1_), None, op0=ALU.mult), edge=True)
            comb_done = dve_op(lambda v, cv=combv: v.scalar_tensor_tensor(
                cv, eq2, col(S2_), cv, op0=ALU.mult, op1=ALU.add), edge=True)

            kv_banks = [b0, b1]
            for sl in range(4):
                bank = kv_banks[sl % 2]
                pairs = []
                for kh in range(NKH):
                    pairs.append((
                        (lambda kh=kh, ct=ct: xn_fm[:, kh * TOK + ct * 128:kh * TOK + ct * 128 + 128]),
                        (lambda kh=kh, sl=sl: wqTA(kh, 1024 + sl * 512, 512)),
                    ))
                P.w("pe", d_wq, wq_kv_rdy)
                P.w("pe", s_dv, kv_bank_free[sl % 2])
                kv_mm = mm_group(lambda b=bank: b[:], pairs)
                if sl < 2:
                    tpar = sl % 2
                    tmn = tmin[:, tpar * 512:(tpar + 1) * 512]
                    tex = texp[:, tpar * 512:(tpar + 1) * 512]
                    mn_done = dve_op(lambda v, b=bank, t=tmn: v.tensor_scalar_min(t, b[:], 0.0),
                                     (s_pe, kv_mm), (s_ac, exp_free[tpar]))
                    P.w("act", s_dv, mn_done)
                    def mke(a, tmn=tmn, tex=tex):
                        a.activation(tex, tmn, AF.Exp).then_inc(s_ac.h, 1)
                    P.em("act", mke)
                    exp_done = ac_inc()
                    exp_free[tpar] = exp_done
                    kslice = kp_tm[:, sl * 512:(sl + 1) * 512]
                    kv_bank_free[sl % 2] = dve_op(
                        lambda v, b=bank, tex=tex, ks=kslice: v.scalar_tensor_tensor(
                            ks, b[:], 0.0, tex, op0=ALU.max, op1=ALU.add),
                        (s_ac, exp_done), (s_pe, kv_war), edge=True)
                else:
                    vslice = v_tm[:, (sl - 2) * 512:(sl - 1) * 512]
                    kv_bank_free[sl % 2] = dve_op(lambda v, b=bank, vs=vslice: v.tensor_copy(vs, b[:]),
                                                  (s_pe, kv_mm), (s_pe, kv_war))
            kv_done = last_dve
            for h in range(NH):
                P.w("pe", s_dv, kv_done)
                if part_mm > 0:
                    P.w("pe", s_dv, part_add)
                def mk1(t, h=h):
                    t.matmul(b3[0:64, 64:128], kp_tm[:, h * 64:(h + 1) * 64], v_tm[:, h * 64:(h + 1) * 64],
                             start=True, stop=True)
                P.em("pe", mk1)
                def mk2(t, h=h):
                    t.matmul(b3[0:64, 128:129], kp_tm[:, h * 64:(h + 1) * 64], onesb[:],
                             start=True, stop=True).then_inc(s_pe.h, 1)
                P.em("pe", mk2)
                part_mm = pe_inc()
                spv = spart[:, h * 65:(h + 1) * 65]
                part_add = dve_op(lambda v, spv=spv: v.tensor_tensor(spv, spv, b3[0:64, 64:129], op=ALU.add),
                                  (s_pe, part_mm), (s_dv, spart_zero))
            kv_war = part_mm
        p1_pe_end = s_pe.n
        partials_done = last_dve

        P.w("sync", s_dv, partials_done)
        P.em("sync", lambda s: s.dma_start(out=cc_in[:], in_=spart[:]).then_inc(d_cs.h, 16)); d_cs.n += 16
        P.w("gp", d_cs, d_cs.n)
        P.em("gp", lambda g: g.collective_compute(
            "AllGather", ALU.bypass, replica_groups=[[0, 1, 2, 3, 4, 5, 6, 7]],
            ins=[cc_in[:]], outs=[cc_out[:]]).then_inc(s_cc.h, 1)); s_cc.n += 1
        P.w("gp", s_cc, s_cc.n)
        P.w("dve", d_ct, consts_rdy)
        for r in range(8):
            def mkg(g, r=r):
                g.dma_start(out=sgath[:], in_=cc_out[r * 64:(r + 1) * 64, :]).then_inc(d_ms.h, 16)
            P.em("gp", mkg)
            d_ms.n += 16
            gath_rdy = d_ms.n
            acc_done = dve_op(lambda v, r=r: v.scalar_tensor_tensor(
                saug[:], sgath[:], wselb[:, r:r + 1], saug[:],
                op0=ALU.mult, op1=ALU.add), (d_ms, gath_rdy), edge=True)
            if r < 7:
                P.w("gp", s_dv, acc_done)
        saug_bf_rdy = dve_op(lambda v: v.tensor_copy(saugb[:], saug[:]), edge=True)
        saug_rdy_h = [saug_bf_rdy] * NH

        ffn_state = {
            "sil_par": 0,
            "g_pe_done": [0, 0],
            "y_evac": [0, 0],
            "g_hist": [0, 0],
            "sil_free": [0, 0],
            "bH_free": [0, 0],
            "bU_free": [0, 0],
            "fjg": 0,
            "last_g_pe": 0,
            "w13_loaded": {},
            "w2_loaded": {},
            "span_pe_done": {},
        }

        def load_w13(k, wait_pe=0):
            e_, sp_ = divmod(k, NSP)
            slot = k % 2
            dw = d_w[slot]
            if wait_pe:
                P.w("gp", s_pe, wait_pe)
            def mk(g, e_=e_, sp_=sp_, slot=slot, dw=dw):
                src1 = w1[e_].rearrange("(kh p) f -> p kh f", p=128)[:, :, sp_ * SPAN:(sp_ + 1) * SPAN]
                dst1 = wreg[:, slot * NKH * SPAN:(slot + 1) * NKH * SPAN].rearrange("p (kh f) -> p kh f", f=SPAN)
                g.dma_start(out=dst1, in_=src1).then_inc(dw.h, 16)
                src3 = w3[e_].rearrange("(kh p) f -> p kh f", p=128)[:, :, sp_ * SPAN:(sp_ + 1) * SPAN]
                dst3 = wreg[:, 8192 + slot * NKH * SPAN:8192 + (slot + 1) * NKH * SPAN].rearrange(
                    "p (kh f) -> p kh f", f=SPAN)
                g.dma_start(out=dst3, in_=src3).then_inc(dw.h, 16)
            P.em("gp", mk)
            dw.n += 32
            ffn_state["w13_loaded"][k] = dw.n

        def load_w2s(k, wait_pe=0):
            e_, sp_ = divmod(k, NSP)
            slot = k % 2
            dw = d_w2[slot]
            if wait_pe:
                P.w("gp", s_pe, wait_pe)
            def mk2(g, e_=e_, sp_=sp_, slot=slot, dw=dw):
                src2 = w2[e_].rearrange("(kf p) n -> p kf n", p=128)[:, sp_ * NFJ:(sp_ + 1) * NFJ, :]
                dst2 = wreg[:, 16384 + slot * NFJ * DM:16384 + (slot + 1) * NFJ * DM].rearrange(
                    "p (kf n) -> p kf n", n=DM)
                g.dma_start(out=dst2, in_=src2).then_inc(dw.h, 16)
            P.em("gp", mk2)
            dw.n += 16
            ffn_state["w2_loaded"][k] = dw.n

        def ffn_chunk(tc, pe_gate, pre_dv=(), pre_ac=()):
            st = ffn_state
            if tc == 1:
                P.w("sync", d_st, 16)
            xsrc = xs.rearrange("(tt p) h -> p tt h", p=128)[:, tc * NTT:(tc + 1) * NTT, :]
            P.em("sync", lambda s, xsrc=xsrc: s.dma_start(
                out=y_acc[:].rearrange("p (tt h) -> p tt h", h=DM), in_=xsrc).then_inc(d_yx.h, 16))
            d_yx.n += 16
            yx_rdy = d_yx.n

            st["w13_loaded"] = {}
            st["w2_loaded"] = {}
            st["span_pe_done"] = {}
            if tc == 0:
                load_w13(0)
                load_w13(1)
                load_w2s(0, wait_pe=pe_gate)
                load_w2s(1)
            else:
                load_w13(0, wait_pe=pe_gate)
                load_w13(1)
                load_w2s(0)
                load_w2s(1)

            bH = [b0, b4]
            bU = [b1, b5]
            for k in range(NSPANS):
                e_, sp_ = divmod(k, NSP)
                slot = k % 2
                half = k % 2
                P.w("pe", d_w[slot], st["w13_loaded"][k])
                P.w("pe", s_dv, st["y_evac"][1])
                if k == 0:
                    P.w("pe", s_dv, xn_evac[NTT * (tc + 1) - 1])
                    for v_ in pre_dv:
                        P.w("pe", s_dv, v_)
                    for v_ in pre_ac:
                        P.w("pe", s_ac, v_)
                for tg in range(2):
                    gpar = tg % 2
                    tof = tc * CTOK + tg * 512
                    for fj in range(NFJ):
                        spar = st["sil_par"]
                        bp = st["fjg"] % 2
                        P.w("pe", s_dv, st["g_hist"][bp])
                        pairs = [((lambda kh=kh, slot=slot, fj=fj: w1b(slot, kh, fj * 128, 128)),
                                  (lambda kh=kh, tof=tof: xn_fm[:, kh * TOK + tof:kh * TOK + tof + 512]))
                                 for kh in range(NKH)]
                        h_done = mm_group(lambda bp=bp: bH[bp][:], pairs)
                        P.w("act", s_pe, h_done)
                        P.w("act", s_dv, st["sil_free"][spar])
                        def mks(a, spar=spar, bp=bp):
                            a.activation(sil[:, spar * 512:(spar + 1) * 512], bH[bp][:], AF.Silu).then_inc(s_ac.h, 1)
                        P.em("act", mks)
                        st["bH_free"][bp] = ac_inc()
                        sil_done = s_ac.n
                        pairs = [((lambda kh=kh, slot=slot, fj=fj: w3b(slot, kh, fj * 128, 128)),
                                  (lambda kh=kh, tof=tof: xn_fm[:, kh * TOK + tof:kh * TOK + tof + 512]))
                                 for kh in range(NKH)]
                        u_done = mm_group(lambda bp=bp: bU[bp][:], pairs)
                        gd = gbuf[:, (gpar * 8 + half * NFJ + fj) * 512:(gpar * 8 + half * NFJ + fj) * 512 + 512]
                        g_done = dve_op(
                            lambda v, gd=gd, spar=spar, bp=bp: v.tensor_tensor(
                                gd, sil[:, spar * 512:(spar + 1) * 512], bU[bp][:], op=ALU.mult),
                            (s_ac, sil_done), (s_pe, u_done), (s_pe, st["g_pe_done"][gpar]))
                        st["g_hist"][bp] = g_done
                        st["bU_free"][bp] = g_done
                        st["sil_free"][spar] = g_done
                        st["sil_par"] ^= 1
                        st["fjg"] += 1
                if k + 2 < NSPANS:
                    load_w13(k + 2, wait_pe=s_pe.n)
                if k % 2 == 0:
                    continue
                g_rdy = s_dv.n
                P.w("pe", d_w2[0], st["w2_loaded"][k - 1])
                P.w("pe", d_w2[1], st["w2_loaded"][k])
                P.w("pe", s_dv, g_rdy)
                for tg in range(2):
                    gpar = tg % 2
                    for tt in range(4):
                        ttg = tg * 4 + tt
                        ct_g = tc * NTT + ttg
                        ypar = ttg % 2
                        ybuf = py if ypar == 0 else py2
                        P.w("pe", s_dv, st["y_evac"][ypar])
                        for ht in range(2):
                            pairs = []
                            for hf in range(2):
                                for fj in range(NFJ):
                                    pairs.append((
                                        (lambda fj=fj, gpar=gpar, tt=tt, hf=hf: gbuf[
                                            :, (gpar * 8 + hf * NFJ + fj) * 512 + tt * 128:
                                               (gpar * 8 + hf * NFJ + fj) * 512 + tt * 128 + 128]),
                                        (lambda fj=fj, hf=hf, ht=ht: w2b(hf, fj, ht * 512, 512)),
                                    ))
                            ymm = mm_group(lambda ht=ht, ybuf=ybuf: ybuf[:, ht * 512:(ht + 1) * 512], pairs)
                        dst = y_acc[:, ttg * DM:ttg * DM + DM]
                        cslice = combs[:, ct_g * E + e_:ct_g * E + e_ + 1]
                        st["y_evac"][ypar] = dve_op(
                            lambda v, dst=dst, cs=cslice, ybuf=ybuf: v.scalar_tensor_tensor(
                                dst, ybuf[:], cs, dst, op0=ALU.mult, op1=ALU.add),
                            (s_pe, ymm), (d_yx, yx_rdy), (s_dv, comb_done))
                    st["g_pe_done"][gpar] = s_pe.n
                g_end = s_pe.n
                st["span_pe_done"][k - 1] = g_end
                st["span_pe_done"][k] = g_end
                if k + 2 < NSPANS:
                    load_w2s(k + 1, wait_pe=g_end)
                    load_w2s(k + 2)
            return s_pe.n
            return s_pe.n

        scan_state = {
            "msc_free": [0, 0],
            "rec_free": [0, 0],
            "b2_free": 0,
            "b4_free": 0,
            "b5_free": 0,
            "b3_free": 0,
            "qkv_bank_free": [0, 0],
            "exp_free": [0, 0],
            "head_par": 0,
            "qp_war": 0,
            "kv_war": 0,
        }

        def scan_chunk(tc, wq_rdy, pre_ac=0, pre_dv=0):
            nonlocal b6_evac, saug_bf_rdy
            ss = scan_state
            proj_evac = 0
            for ctl in range(8):
                ct = tc * 8 + ctl
                for sl in range(2):
                    bank = [b0, b1][sl % 2]
                    pairs = [((lambda kh=kh, ct=ct: xn_fm[:, kh * TOK + ct * 128:kh * TOK + ct * 128 + 128]),
                              (lambda kh=kh, sl=sl: wqT(kh, sl * 512, 512)))
                             for kh in range(NKH)]
                    P.w("pe", d_wq, wq_rdy)
                    P.w("pe", s_dv, ss["qkv_bank_free"][sl % 2])
                    if ctl == 0 and sl == 0:
                        P.w("pe", s_ac, pre_ac)
                        P.w("pe", s_dv, pre_dv)
                    q_mm = mm_group(lambda b=bank: b[:], pairs)
                    tpar = sl % 2
                    tmn = tmin[:, tpar * 512:(tpar + 1) * 512]
                    tex = texp[:, tpar * 512:(tpar + 1) * 512]
                    mn_done = dve_op(lambda v, b=bank, t=tmn: v.tensor_scalar_min(t, b[:], 0.0),
                                     (s_pe, q_mm), (s_ac, ss["exp_free"][tpar]))
                    P.w("act", s_dv, mn_done)
                    def mke(a, tmn=tmn, tex=tex):
                        a.activation(tex, tmn, AF.Exp).then_inc(s_ac.h, 1)
                    P.em("act", mke)
                    exp_done = ac_inc()
                    ss["exp_free"][tpar] = exp_done
                    qslice = qp_tm[:, sl * 512:(sl + 1) * 512]
                    ss["qkv_bank_free"][sl % 2] = dve_op(
                        lambda v, b=bank, tex=tex, qs=qslice: v.scalar_tensor_tensor(
                            qs, b[:], 0.0, tex, op0=ALU.max, op1=ALU.add),
                        (s_ac, exp_done), (s_pe, ss["qp_war"]), edge=True)
                for sl in range(4):
                    bank = [b0, b1][sl % 2]
                    pairs = [((lambda kh=kh, ct=ct: xn_fm[:, kh * TOK + ct * 128:kh * TOK + ct * 128 + 128]),
                              (lambda kh=kh, sl=sl: wqT(kh, 1024 + sl * 512, 512)))
                             for kh in range(NKH)]
                    P.w("pe", s_dv, ss["qkv_bank_free"][sl % 2])
                    kv_mm = mm_group(lambda b=bank: b[:], pairs)
                    if sl < 2:
                        tpar = sl % 2
                        tmn = tmin[:, tpar * 512:(tpar + 1) * 512]
                        tex = texp[:, tpar * 512:(tpar + 1) * 512]
                        mn_done = dve_op(lambda v, b=bank, t=tmn: v.tensor_scalar_min(t, b[:], 0.0),
                                         (s_pe, kv_mm), (s_ac, ss["exp_free"][tpar]))
                        P.w("act", s_dv, mn_done)
                        def mke(a, tmn=tmn, tex=tex):
                            a.activation(tex, tmn, AF.Exp).then_inc(s_ac.h, 1)
                        P.em("act", mke)
                        exp_done = ac_inc()
                        ss["exp_free"][tpar] = exp_done
                        kslice = kp_tm[:, sl * 512:(sl + 1) * 512]
                        ss["qkv_bank_free"][sl % 2] = dve_op(
                            lambda v, b=bank, tex=tex, ks=kslice: v.scalar_tensor_tensor(
                                ks, b[:], 0.0, tex, op0=ALU.max, op1=ALU.add),
                            (s_ac, exp_done), (s_pe, ss["kv_war"]), edge=True)
                    else:
                        vslice = v_tm[:, (sl - 2) * 512:(sl - 1) * 512]
                        ss["qkv_bank_free"][sl % 2] = dve_op(
                            lambda v, b=bank, vs=vslice: v.tensor_copy(vs, b[:]), (s_pe, kv_mm), (s_pe, ss["kv_war"]))
                qkv_rdy = last_dve

                for (srct, dstv) in ((qp_tm, qp_fm), (kp_tm, kp_fm)):
                    for half in range(2):
                        P.w("pe", s_dv, qkv_rdy)
                        P.w("pe", s_dv, b6_evac)
                        for hh in range(8):
                            h = half * 8 + hh
                            def mk(t, srct=srct, h=h, hh=hh, sp=(hh == 7)):
                                ins = t.transpose(b6[0:64, hh * 128:(hh + 1) * 128],
                                                  srct[:, h * 64:(h + 1) * 64], identb[:])
                                if sp:
                                    ins.then_inc(s_pe.h, 1)
                            P.em("pe", mk)
                        trd = pe_inc()
                        b6_evac = dve_op(lambda v, dstv=dstv, half=half: v.tensor_copy(
                            dstv[:, half * 1024:(half + 1) * 1024], b6[0:64, 0:1024]), (s_pe, trd))
                fm_rdy = b6_evac
                ss["qp_war"] = s_pe.n

                for h in range(NH):
                    hpar = ss["head_par"]
                    qpf = qp_fm[:, h * 128:(h + 1) * 128]
                    kpf = kp_fm[:, h * 128:(h + 1) * 128]
                    P.w("pe", s_dv, fm_rdy)
                    P.w("pe", s_dv, ss["b2_free"])
                    def mksc(t, kpf=kpf, qpf=qpf):
                        t.matmul(b2[:, 0:128], kpf, qpf, start=True, stop=True).then_inc(s_pe.h, 1)
                    P.em("pe", mksc)
                    sc_mm = pe_inc()
                    mslice = msc[:, hpar * 128:(hpar + 1) * 128]
                    P.w("dve", s_pe, ss["msc_free"][hpar])
                    ss["b2_free"] = dve_op(
                        lambda v, ms=mslice: v.tensor_tensor(ms, b2[:, 0:128], maskb[:], op=ALU.mult),
                        (s_pe, sc_mm), (d_ct, consts_rdy))
                    mask_done = ss["b2_free"]
                    P.w("pe", s_dv, mask_done)
                    P.w("pe", s_ac, ss["b4_free"])
                    P.w("pe", s_dv, saug_rdy_h[h])
                    def mkn1(t, ms=mslice, h=h):
                        t.matmul(b4[:, 0:64], ms, v_tm[:, h * 64:(h + 1) * 64], start=True, stop=False)
                    P.em("pe", mkn1)
                    def mkn2(t, qpf=qpf, h=h):
                        t.matmul(b4[:, 0:64], qpf, saugb[:, h * 65:h * 65 + 64],
                                 start=False, stop=True).then_inc(s_pe.h, 1)
                    P.em("pe", mkn2)
                    num_mm = pe_inc()
                    P.w("pe", s_dv, ss["b5_free"])
                    def mkd1(t, ms=mslice):
                        t.matmul(b5[:, 0:1], ms, onesb[:], start=True, stop=False)
                    P.em("pe", mkd1)
                    def mkd2(t, qpf=qpf, h=h):
                        t.matmul(b5[:, 0:1], qpf, saugb[:, h * 65 + 64:h * 65 + 65],
                                 start=False, stop=True).then_inc(s_pe.h, 1)
                    P.em("pe", mkd2)
                    den_mm = pe_inc()
                    ss["msc_free"][hpar] = den_mm
                    P.w("pe", s_dv, ss["b3_free"])
                    def mks1(t, h=h):
                        t.matmul(b3[0:64, 64:128], kp_tm[:, h * 64:(h + 1) * 64], v_tm[:, h * 64:(h + 1) * 64],
                                 start=True, stop=True)
                    P.em("pe", mks1)
                    def mks2(t, h=h):
                        t.matmul(b3[0:64, 128:129], kp_tm[:, h * 64:(h + 1) * 64], onesb[:],
                                 start=True, stop=True).then_inc(s_pe.h, 1)
                    P.em("pe", mks2)
                    ds_mm = pe_inc()
                    rc = REC0 if hpar == 0 else REC1
                    dn = DEN0 if hpar == 0 else DEN1
                    P.w("dve", s_ac, ss["rec_free"][hpar])
                    dve_op(lambda v, dn=dn: v.tensor_scalar_add(col(dn), b5[:, 0:1], EPS_DEN), (s_pe, den_mm))
                    ss["b5_free"] = last_dve
                    rec_done = dve_op(lambda v, rc=rc, dn=dn: v.reciprocal(col(rc), col(dn)), edge=True)
                    P.w("act", s_pe, num_mm)
                    P.w("act", s_dv, rec_done)
                    def mky(a, h=h, rc=rc):
                        a.activation(y_tm[:, h * 64:(h + 1) * 64], b4[:, 0:64], AF.Identity,
                                     scale=col(rc)).then_inc(s_ac.h, 1)
                    P.em("act", mky)
                    ss["b4_free"] = ac_inc()
                    ss["rec_free"][hpar] = ss["b4_free"]
                    yh_done = ss["b4_free"]
                    sgv = saug[:, h * 65:(h + 1) * 65]
                    dve_op(lambda v, sgv=sgv: v.tensor_tensor(sgv, sgv, b3[0:64, 64:129], op=ALU.add),
                           (s_pe, ds_mm), (s_dv, saug_rdy_h[h]))
                    ss["b3_free"] = last_dve
                    sgbv = saugb[:, h * 65:(h + 1) * 65]
                    saug_rdy_h[h] = dve_op(lambda v, sgv=sgv, sgbv=sgbv: v.tensor_copy(sgbv, sgv), edge=True)
                    ss["kv_war"] = ds_mm
                    ss["head_par"] ^= 1
                y_all = s_ac.n

                P.w("pe", s_ac, y_all)
                P.w("pe", s_dv, b6_evac)
                for j in range(NKH):
                    def mk(t, j=j, sp=(j == NKH - 1)):
                        ins = t.transpose(b6[:, j * 128:(j + 1) * 128], y_tm[:, j * 128:(j + 1) * 128], identb[:])
                        if sp:
                            ins.then_inc(s_pe.h, 1)
                    P.em("pe", mk)
                ytr = pe_inc()
                b6_evac = dve_op(lambda v: v.tensor_copy(y_fm[:], b6[:, 0:1024]), (s_pe, ytr))
                yfm_rdy = b6_evac
                for cs in range(2):
                    bank = b7 if cs == 0 else b2
                    bid = "b7" if cs == 0 else "b2"
                    pairs = [((lambda j=j: y_fm[:, j * 128:(j + 1) * 128]),
                              (lambda j=j, cs=cs: wpT(j, cs * 512, 512)))
                             for j in range(NKH)]
                    P.w("pe", s_dv, yfm_rdy)
                    if cs == 1:
                        P.w("pe", s_dv, ss["b2_free"])
                        P.w("pe", s_dv, proj_evac)
                    else:
                        P.w("pe", s_dv, proj_evac)
                    pj_mm = mm_group(lambda b=bank: b[:], pairs)
                    dst = y_acc[:, ctl * DM + cs * 512:ctl * DM + cs * 512 + 512]
                    proj_evac = dve_op(lambda v, dst=dst, b=bank: v.scalar_tensor_tensor(
                        dst, b[:], 1.0, dst, op0=ALU.mult, op1=ALU.add), (s_pe, pj_mm))
                    if cs == 1:
                        ss["b2_free"] = proj_evac
            return proj_evac

        ffn0_pe_end = ffn_chunk(0, pe_gate=p1_pe_end, pre_dv=(kv_bank_free[0], kv_bank_free[1]))
        P.w("sync", s_pe, ffn0_pe_end)
        for kh in range(NKH):
            def mk(s, kh=kh):
                s.dma_start(out=wqT(kh, 0, 3072), in_=wattn_bf[:, kh * 128:(kh + 1) * 128],
                            transpose=True).then_inc(d_wq.h, 16)
            P.em("sync", mk)
            d_wq.n += 16
        for kh in range(NKH):
            def mk(s, kh=kh):
                s.dma_start(out=wpT(kh, 0, 1024), in_=wp_bf[:, kh * 128:(kh + 1) * 128],
                            transpose=True).then_inc(d_wq.h, 16)
            P.em("sync", mk)
            d_wq.n += 16
        wq_full0 = d_wq.n
        scan_state["kv_war"] = p1_pe_end
        scan_state["b5_free"] = max(max(ffn_state["bU_free"]), max(ffn_state["y_evac"]))
        scan0_end = scan_chunk(0, wq_full0, pre_ac=max(ffn_state["bH_free"]),
                               pre_dv=max(max(ffn_state["bU_free"]), max(ffn_state["y_evac"])))
        P.w("sync", s_dv, scan0_end)
        yo0 = yout.rearrange("(tt p) h -> p tt h", p=128)[:, 0:NTT, :]
        P.em("sync", lambda s, yo0=yo0: s.dma_start(
            out=yo0, in_=y_acc[:].rearrange("p (tt h) -> p tt h", h=DM)).then_inc(d_st.h, 16))
        d_st.n += 16

        scan0_pe_end = s_pe.n
        ffn1_pe_end = ffn_chunk(1, pe_gate=scan0_pe_end,
                                pre_dv=(scan_state["qkv_bank_free"][0], scan_state["qkv_bank_free"][1],
                                        scan_state["b5_free"], scan_state["b3_free"], scan_state["b2_free"]),
                                pre_ac=(scan_state["b4_free"], scan_state["rec_free"][0], scan_state["rec_free"][1]))
        P.w("sync", s_pe, ffn1_pe_end)
        for kh in range(NKH):
            def mk(s, kh=kh):
                s.dma_start(out=wqT(kh, 0, 3072), in_=wattn_bf[:, kh * 128:(kh + 1) * 128],
                            transpose=True).then_inc(d_wq.h, 16)
            P.em("sync", mk)
            d_wq.n += 16
        for kh in range(NKH):
            def mk(s, kh=kh):
                s.dma_start(out=wpT(kh, 0, 1024), in_=wp_bf[:, kh * 128:(kh + 1) * 128],
                            transpose=True).then_inc(d_wq.h, 16)
            P.em("sync", mk)
            d_wq.n += 16
        wq_full1 = d_wq.n
        scan_state["b5_free"] = max(scan_state["b5_free"], max(ffn_state["bU_free"]), max(ffn_state["y_evac"]))
        scan1_end = scan_chunk(1, wq_full1, pre_ac=max(ffn_state["bH_free"]),
                               pre_dv=max(max(ffn_state["bU_free"]), max(ffn_state["y_evac"])))
        P.w("sync", s_dv, scan1_end)
        yo1 = yout.rearrange("(tt p) h -> p tt h", p=128)[:, NTT:2 * NTT, :]
        P.em("sync", lambda s, yo1=yo1: s.dma_start(
            out=yo1, in_=y_acc[:].rearrange("p (tt h) -> p tt h", h=DM)).then_inc(d_st.h, 16))
        d_st.n += 16
        P.w("sync", d_st, d_st.n)

        with nc.Block() as block:
            @block.tensor
            def _(t):
                for f in P.ops["pe"]:
                    f(t)

            @block.scalar
            def _(a):
                for f in P.ops["act"]:
                    f(a)

            @block.vector
            def _(v):
                for f in P.ops["dve"]:
                    f(v)

            @block.sync
            def _(s):
                for f in P.ops["sync"]:
                    f(s)

            @block.gpsimd
            def _(g):
                for f in P.ops["gp"]:
                    f(g)

    return nc


_NC_CACHE = {}


def kernel(L0_attn_x, L0_attn_w_attn, L0_attn_w_proj, L0_ffn_gate_w, L0_ffn_w1, L0_ffn_w3, L0_ffn_w2):
    x = np.asarray(L0_attn_x)
    w_attn = np.ascontiguousarray(np.asarray(L0_attn_w_attn), dtype=np.float32)
    w_proj = np.ascontiguousarray(np.asarray(L0_attn_w_proj), dtype=np.float32)
    gate_w = np.ascontiguousarray(np.asarray(L0_ffn_gate_w), dtype=np.float32)
    w1 = np.ascontiguousarray(np.asarray(L0_ffn_w1), dtype=np.float32)
    w3 = np.ascontiguousarray(np.asarray(L0_ffn_w3), dtype=np.float32)
    w2 = np.ascontiguousarray(np.asarray(L0_ffn_w2), dtype=np.float32)

    ident = np.eye(128, dtype=ml_dtypes.bfloat16)
    maskT = np.triu(np.ones((128, 128))).astype(ml_dtypes.bfloat16)
    onesc = np.ones((128, 1), dtype=ml_dtypes.bfloat16)

    if "nc" not in _NC_CACHE:
        _NC_CACHE["nc"] = build_nc()
    nc = _NC_CACHE["nc"]

    in_maps = []
    for c in range(8):
        b, seg = divmod(c, 4)
        xs = np.ascontiguousarray(x[b, seg * TOK:(seg + 1) * TOK, :], dtype=np.float32)
        wsel = np.zeros((64, 8), np.float32)
        for r in range(8):
            rb, rseg = divmod(r, 4)
            if rb == b and rseg < seg:
                wsel[:, r] = 1.0
        in_maps.append({
            "xs": xs, "w_attn": w_attn, "w_proj": w_proj, "gate_w": gate_w,
            "w1": w1, "w3": w3, "w2": w2,
            "ident": ident, "maskT": maskT, "onesc": onesc, "wsel": wsel,
        })

    res = run_bass_kernel_spmd(nc, in_maps, core_ids=list(range(8)))
    out = np.empty((2, 8192, 1024), np.float32)
    for c in range(8):
        b, seg = divmod(c, 4)
        out[b, seg * TOK:(seg + 1) * TOK, :] = np.asarray(res.results[c]["yout"])
    return out


# version 08
# speedup vs baseline: 1.3937x; 1.0267x over previous
import contextlib
import numpy as np
import ml_dtypes
import concourse.bass as bass
import concourse.mybir as mybir
from concourse.bass_utils import run_bass_kernel_spmd

F32 = mybir.dt.float32
BF16 = mybir.dt.bfloat16
AF = mybir.ActivationFunctionType
ALU = mybir.AluOpType
AX = mybir.AxisListType

TOK = 2048
NCT = 16
DM = 1024
NKH = 8
NH = 16
HD = 64
E = 8
FF = 4096
SPAN = 512
NSP = FF // SPAN
NFJ = SPAN // 128
NSPANS = E * NSP
NCH = 2
CTOK = 1024
NTT = 8
EPS_LN = 1e-5
EPS_DEN = 1e-6


class Sem:
    def __init__(self, es, nc, name):
        self.h = es.enter_context(nc.semaphore(name))
        self.n = 0


class B:

    def __init__(self):
        self.ops = {k: [] for k in ("pe", "act", "dve", "sync", "gp")}

    def em(self, eng, fn):
        self.ops[eng].append(fn)

    def w(self, eng, sem, val):
        if val > 0:
            self.ops[eng].append(lambda e, s=sem.h, v=val: e.wait_ge(s, v))


def build_nc():
    nc = bass.Bass("TRN2", target_bir_lowering=False)
    xs = nc.dram_tensor("xs", [TOK, DM], F32, kind="ExternalInput").ap()
    w_attn = nc.dram_tensor("w_attn", [3 * DM, DM], F32, kind="ExternalInput").ap()
    w_proj = nc.dram_tensor("w_proj", [DM, DM], F32, kind="ExternalInput").ap()
    gate_w = nc.dram_tensor("gate_w", [E, DM], F32, kind="ExternalInput").ap()
    w1 = nc.dram_tensor("w1", [E, DM, FF], F32, kind="ExternalInput").ap()
    w3 = nc.dram_tensor("w3", [E, DM, FF], F32, kind="ExternalInput").ap()
    w2 = nc.dram_tensor("w2", [E, FF, DM], F32, kind="ExternalInput").ap()
    ident = nc.dram_tensor("ident", [128, 128], BF16, kind="ExternalInput").ap()
    maskT = nc.dram_tensor("maskT", [128, 128], BF16, kind="ExternalInput").ap()
    onesc = nc.dram_tensor("onesc", [128, 1], BF16, kind="ExternalInput").ap()
    wsel = nc.dram_tensor("wsel", [64, 8], F32, kind="ExternalInput").ap()
    yout = nc.dram_tensor("yout", [TOK, DM], F32, kind="ExternalOutput").ap()
    wattn_bf = nc.dram_tensor("wattn_bf", [3 * DM, DM], BF16, kind="Internal").ap()
    w1bf = nc.dram_tensor("w1bf", [E, DM, FF], BF16, kind="Internal").ap()
    w3bf = nc.dram_tensor("w3bf", [E, DM, FF], BF16, kind="Internal").ap()
    w2bf = nc.dram_tensor("w2bf", [E, FF, DM], BF16, kind="Internal").ap()
    wp_bf = nc.dram_tensor("wp_bf", [DM, DM], BF16, kind="Internal").ap()
    cc_in = nc.dram_tensor("cc_in", [64, NH * 65], F32, kind="Internal").ap()
    cc_out = nc.dram_tensor("cc_out", [8 * 64, NH * 65], F32, kind="Internal").ap()

    es = contextlib.ExitStack()
    sb = lambda n, s, d: es.enter_context(nc.sbuf_tensor(n, s, d))
    ps = lambda n, s, d: es.enter_context(nc.psum_tensor(n, s, d))
    with es:
        xn_fm = sb("xn_fm", [128, NKH * TOK], BF16)
        wreg = sb("wreg", [128, 32768], BF16)
        y_acc = sb("y_acc", [128, NTT * DM], F32)
        combs = sb("combs", [128, NCT * E], F32)
        saug = sb("saug", [64, NH * 65], F32)
        saugb = sb("saugb", [64, NH * 65], BF16)
        spart = sb("spart", [64, NH * 65], F32)
        sgath = sb("sgath", [64, NH * 65], F32)
        wselb = sb("wselb", [64, 8], F32)
        xsb = sb("xsb", [128, 2 * DM], F32)
        xn_tm = sb("xn_tm", [128, DM], BF16)
        lns = sb("lns", [128, 32], F32)
        gtmp = sb("gtmp", [128, 32], F32)
        gw8 = sb("gw8", [16, DM], BF16)
        gwT = sb("gwT", [128, NKH * E], BF16)
        identb = sb("identb", [128, 128], BF16)
        maskb = sb("maskb", [128, 128], BF16)
        onesb = sb("onesb", [128, 1], BF16)
        kp_tm = sb("kp_tm", [128, DM], BF16)
        v_tm = sb("v_tm", [128, DM], BF16)
        qp_tm = sb("qp_tm", [128, DM], BF16)
        qp_fm = sb("qp_fm", [64, NH * 128], BF16)
        kp_fm = sb("kp_fm", [64, NH * 128], BF16)
        y_tm = sb("y_tm", [128, DM], BF16)
        y_fm = sb("y_fm", [128, DM], BF16)
        tmin = sb("tmin", [128, 2 * 512], F32)
        texp = sb("texp", [128, 2 * 512], F32)
        msc = sb("msc", [128, 512], BF16)
        sil = sb("sil", [128, 2 * 512], BF16)
        gbuf = sb("gbuf", [128, 2 * 8 * 512], BF16)

        def w1b(slot, kh, fcol0, n):
            off = slot * (NKH * SPAN) + kh * SPAN + fcol0
            return wreg[:, off:off + n]

        def w3b(slot, kh, fcol0, n):
            off = 8192 + slot * (NKH * SPAN) + kh * SPAN + fcol0
            return wreg[:, off:off + n]

        def w2b(slot, kf, hcol0, n):
            off = 16384 + slot * (NFJ * DM) + kf * DM + hcol0
            return wreg[:, off:off + n]

        def wqT(kh, c0, n):
            return wreg[:, kh * 3072 + c0:kh * 3072 + c0 + n]

        def wqTA(kh, c0, n):
            return wreg[:, 16384 + kh * 2048 + (c0 - 1024):16384 + kh * 2048 + (c0 - 1024) + n]

        def wpT(kh, c0, n):
            return wreg[:, 24576 + kh * DM + c0:24576 + kh * DM + c0 + n]

        b0 = ps("b0", [128, 512], F32)
        b1 = ps("b1", [128, 512], F32)
        py = ps("py", [128, 1024], F32)
        b2 = py[:, 0:512]
        b3 = py[:, 512:1024]
        py2 = ps("py2", [128, 1024], F32)
        b4 = py2[:, 0:512]
        b5 = py2[:, 512:1024]
        b6 = ps("b6", [128, 1024], BF16)
        b7 = ps("b7", [128, 512], F32)

        d_x = [Sem(es, nc, "d_x0"), Sem(es, nc, "d_x1")]
        d_yx = Sem(es, nc, "d_yx")
        d_w = [Sem(es, nc, "d_w0"), Sem(es, nc, "d_w1")]
        d_w2 = [Sem(es, nc, "d_w2a"), Sem(es, nc, "d_w2b")]
        d_wq = Sem(es, nc, "d_wq")
        d_ms = Sem(es, nc, "d_ms")
        d_cast = Sem(es, nc, "d_cast")
        d_ct = Sem(es, nc, "d_ct")
        d_cs = Sem(es, nc, "d_cs")
        s_cc = Sem(es, nc, "s_cc")
        s_pe = Sem(es, nc, "s_pe")
        s_ac = Sem(es, nc, "s_ac")
        s_dv = Sem(es, nc, "s_dv")
        d_st = Sem(es, nc, "d_st")

        P = B()

        def pe_inc():
            s_pe.n += 1
            return s_pe.n

        def ac_inc():
            s_ac.n += 1
            return s_ac.n

        def dv_inc():
            s_dv.n += 1
            return s_dv.n

        def mm_group(bank_ap_fn, pairs, extra_first_wait=None):
            n = len(pairs)
            for i, (lf, rf) in enumerate(pairs):
                def mk(t, lf=lf, rf=rf, st=(i == 0), sp=(i == n - 1), bf=bank_ap_fn):
                    ins = t.matmul(bf(), lf(), rf(), start=st, stop=sp)
                    if sp:
                        ins.then_inc(s_pe.h, 1)
                P.em("pe", mk)
            return pe_inc()

        P.em("sync", lambda s: s.dma_start(out=identb[:], in_=ident).then_inc(d_ct.h, 16)); d_ct.n += 16
        P.em("sync", lambda s: s.dma_start(out=maskb[:], in_=maskT).then_inc(d_ct.h, 16)); d_ct.n += 16
        P.em("sync", lambda s: s.dma_start(out=onesb[:], in_=onesc).then_inc(d_ct.h, 16)); d_ct.n += 16
        P.em("sync", lambda s: s.dma_start(out=wselb[:], in_=wsel).then_inc(d_ct.h, 16)); d_ct.n += 16
        consts_rdy = d_ct.n

        P.em("gp", lambda g: g.dma_start(out=wattn_bf[:], in_=w_attn).then_inc(d_ms.h, 16)); d_ms.n += 16
        P.em("gp", lambda g: g.dma_start(out=gw8[0:8, :], in_=gate_w).then_inc(d_ms.h, 16)); d_ms.n += 16
        P.em("gp", lambda g: g.dma_start(out=wp_bf[:], in_=w_proj).then_inc(d_ms.h, 16)); d_ms.n += 16
        p0_gp = d_ms.n

        P.em("dve", lambda v: v.memset(saug[:], 0.0).then_inc(s_dv.h, 1)); dv_inc()
        P.em("dve", lambda v: v.memset(spart[:], 0.0).then_inc(s_dv.h, 1)); spart_zero = dv_inc()

        P.w("pe", d_ms, p0_gp)
        P.w("pe", d_ct, consts_rdy)
        for kh in range(NKH):
            def mk(t, kh=kh, sp=(kh == NKH - 1)):
                ins = t.transpose(b6[:, kh * 8:kh * 8 + 8], gw8[0:8, kh * 128:(kh + 1) * 128], identb[0:8, 0:8])
                if sp:
                    ins.then_inc(s_pe.h, 1)
            P.em("pe", mk)
        gwt_tr = pe_inc()
        P.w("dve", s_pe, gwt_tr)
        P.em("dve", lambda v: v.tensor_copy(gwT[:], b6[:, 0:NKH * E]).then_inc(s_dv.h, 1))
        gwt_rdy = dv_inc()

        P.w("sync", d_ms, p0_gp)
        for kh in range(NKH):
            def mk(s, kh=kh):
                s.dma_start(out=wqTA(kh, 1024, 2048), in_=wattn_bf[1024:3072, kh * 128:(kh + 1) * 128],
                            transpose=True).then_inc(d_wq.h, 16)
            P.em("sync", mk)
            d_wq.n += 16
        wq_kv_rdy = d_wq.n

        SX, SSQ, M_, EX2, T1_, VAR, LNV, RSTD, NMR = range(9)
        D_, ED, DEN0, DEN1, REC0, REC1, S1_, S2_ = range(9, 17)

        def col(i):
            return lns[:, i:i + 1]

        last_dve = 0
        exp_free = [0, 0]
        xn_evac = [0] * NCT
        xnT_done = [0] * NCT
        xn_consumed_act = [0] * NCT
        b6_evac = 0
        kv_bank_free = [0, 0]
        part_mm = 0
        kv_war = 0

        def dve_op(fn, *waits, edge=False):
            nonlocal last_dve
            for sem, val in waits:
                P.w("dve", sem, val)
            if edge:
                P.w("dve", s_dv, last_dve)
            def mk(v, fn=fn):
                fn(v).then_inc(s_dv.h, 1)
            P.em("dve", mk)
            last_dve = dv_inc()
            return last_dve

        for ct in range(NCT):
            par = ct % 2
            xt = xsb[:, par * DM:(par + 1) * DM]
            if ct >= 2:
                P.w("sync", s_ac, xn_consumed_act[ct - 2])
            def mkx(s, ct=ct, xt=xt, dx=d_x[par]):
                s.dma_start(out=xt, in_=xs[ct * 128:(ct + 1) * 128, :]).then_inc(dx.h, 16)
            P.em("sync", mkx)
            d_x[par].n += 16
            x_rdy = d_x[par].n

            sum_done = dve_op(lambda v, xt=xt: v.reduce_sum(col(SX), xt, axis=AX.X), (d_x[par], x_rdy))
            if ct >= 1:
                P.w("dve", s_pe, xnT_done[ct - 1])
            ssq_done = dve_op(lambda v, xt=xt: v.scalar_tensor_tensor(
                xn_tm[:], xt, 1.0, xt, op0=ALU.mult, op1=ALU.mult, accum_out=col(SSQ)))
            dve_op(lambda v: v.tensor_scalar_mul(col(M_), col(SX), 1.0 / DM), edge=True)
            dve_op(lambda v: v.tensor_scalar_mul(col(EX2), col(SSQ), 1.0 / DM), edge=True)
            dve_op(lambda v: v.tensor_tensor(col(T1_), col(M_), col(M_), op=ALU.mult), edge=True)
            dve_op(lambda v: v.tensor_tensor(col(VAR), col(EX2), col(T1_), op=ALU.subtract), edge=True)
            var_done = dve_op(lambda v: v.tensor_scalar_add(col(VAR), col(VAR), EPS_LN), edge=True)
            P.w("act", s_dv, var_done)
            P.em("act", lambda a: a.activation(col(LNV), col(VAR), AF.Ln).then_inc(s_ac.h, 1))
            lnv_done = ac_inc()
            P.w("act", s_ac, lnv_done)
            P.em("act", lambda a: a.activation(col(RSTD), col(LNV), AF.Exp, scale=-0.5).then_inc(s_ac.h, 1))
            rstd_done = ac_inc()
            nmr_done = dve_op(lambda v: v.scalar_tensor_tensor(
                col(NMR), col(M_), -1.0, col(RSTD), op0=ALU.mult, op1=ALU.mult), (s_ac, rstd_done), edge=True)
            P.w("act", s_dv, nmr_done)
            P.w("act", s_ac, rstd_done)
            def mkxn(a, xt=xt):
                a.activation(xn_tm[:], xt, AF.Identity, bias=col(NMR), scale=col(RSTD)).then_inc(s_ac.h, 1)
            P.em("act", mkxn)
            xn_done = ac_inc()
            xn_consumed_act[ct] = xn_done

            P.w("pe", s_ac, xn_done)
            P.w("pe", s_dv, b6_evac)
            for kh in range(NKH):
                def mk(t, kh=kh, sp=(kh == NKH - 1)):
                    ins = t.transpose(b6[:, kh * 128:(kh + 1) * 128], xn_tm[:, kh * 128:(kh + 1) * 128], identb[:])
                    if sp:
                        ins.then_inc(s_pe.h, 1)
                P.em("pe", mk)
            xnT_done[ct] = pe_inc()
            xn_view = xn_fm[:].rearrange("p (kh t) -> p kh t", t=TOK)[:, :, ct * 128:(ct + 1) * 128]
            b6v = b6[:, 0:1024].rearrange("p (kh t) -> p kh t", t=128)
            xn_evac[ct] = dve_op(lambda v, xv=xn_view, bv=b6v: v.tensor_copy(xv, bv), (s_pe, xnT_done[ct]))
            b6_evac = xn_evac[ct]

            pairs = []
            for kh in range(NKH):
                pairs.append((
                    (lambda kh=kh, ct=ct: xn_fm[:, kh * TOK + ct * 128:kh * TOK + ct * 128 + 128]),
                    (lambda kh=kh: gwT[:, kh * 8:(kh + 1) * 8]),
                ))
            P.w("pe", s_dv, xn_evac[ct])
            P.w("pe", s_dv, gwt_rdy)
            gate_mm = mm_group(lambda: b3[:, 0:8], pairs)
            gsc = gtmp[:, 0:8]
            eq1 = gtmp[:, 8:16]
            scm = gtmp[:, 16:24]
            eq2 = gtmp[:, 24:32]
            dve_op(lambda v: v.tensor_copy(gsc, b3[:, 0:8]), (s_pe, gate_mm), edge=True)
            gsc_done = last_dve
            dve_op(lambda v: v.reduce_max(col(M_), gsc, axis=AX.X), edge=True)
            dve_op(lambda v: v.tensor_scalar(eq1, gsc, col(M_), None, op0=ALU.is_equal), edge=True)
            dve_op(lambda v: v.scalar_tensor_tensor(scm, eq1, -1e30, gsc, op0=ALU.mult, op1=ALU.add), edge=True)
            dve_op(lambda v: v.reduce_max(col(T1_), scm, axis=AX.X), edge=True)
            dve_op(lambda v: v.tensor_scalar(eq2, scm, col(T1_), None, op0=ALU.is_equal), edge=True)
            d_done = dve_op(lambda v: v.tensor_tensor(col(D_), col(T1_), col(M_), op=ALU.subtract), edge=True)
            P.w("act", s_dv, d_done)
            P.em("act", lambda a: a.activation(col(ED), col(D_), AF.Exp).then_inc(s_ac.h, 1))
            ed_done = ac_inc()
            dve_op(lambda v: v.tensor_scalar_add(col(DEN0), col(ED), 1.0), (s_ac, ed_done), edge=True)
            dve_op(lambda v: v.reciprocal(col(S1_), col(DEN0)), edge=True)
            dve_op(lambda v: v.tensor_tensor(col(S2_), col(ED), col(S1_), op=ALU.mult), edge=True)
            combv = combs[:, ct * E:(ct + 1) * E]
            dve_op(lambda v, cv=combv: v.tensor_scalar(cv, eq1, col(S1_), None, op0=ALU.mult), edge=True)
            comb_done = dve_op(lambda v, cv=combv: v.scalar_tensor_tensor(
                cv, eq2, col(S2_), cv, op0=ALU.mult, op1=ALU.add), edge=True)

            kv_banks = [b0, b1]
            for sl in range(4):
                bank = kv_banks[sl % 2]
                pairs = []
                for kh in range(NKH):
                    pairs.append((
                        (lambda kh=kh, ct=ct: xn_fm[:, kh * TOK + ct * 128:kh * TOK + ct * 128 + 128]),
                        (lambda kh=kh, sl=sl: wqTA(kh, 1024 + sl * 512, 512)),
                    ))
                P.w("pe", d_wq, wq_kv_rdy)
                P.w("pe", s_dv, kv_bank_free[sl % 2])
                kv_mm = mm_group(lambda b=bank: b[:], pairs)
                if sl < 2:
                    tpar = sl % 2
                    tmn = tmin[:, tpar * 512:(tpar + 1) * 512]
                    tex = texp[:, tpar * 512:(tpar + 1) * 512]
                    mn_done = dve_op(lambda v, b=bank, t=tmn: v.tensor_scalar_min(t, b[:], 0.0),
                                     (s_pe, kv_mm), (s_ac, exp_free[tpar]))
                    P.w("act", s_dv, mn_done)
                    def mke(a, tmn=tmn, tex=tex):
                        a.activation(tex, tmn, AF.Exp).then_inc(s_ac.h, 1)
                    P.em("act", mke)
                    exp_done = ac_inc()
                    exp_free[tpar] = exp_done
                    kslice = kp_tm[:, sl * 512:(sl + 1) * 512]
                    kv_bank_free[sl % 2] = dve_op(
                        lambda v, b=bank, tex=tex, ks=kslice: v.scalar_tensor_tensor(
                            ks, b[:], 0.0, tex, op0=ALU.max, op1=ALU.add),
                        (s_ac, exp_done), (s_pe, kv_war), edge=True)
                else:
                    vslice = v_tm[:, (sl - 2) * 512:(sl - 1) * 512]
                    kv_bank_free[sl % 2] = dve_op(lambda v, b=bank, vs=vslice: v.tensor_copy(vs, b[:]),
                                                  (s_pe, kv_mm), (s_pe, kv_war))
            kv_done = last_dve
            for h in range(NH):
                P.w("pe", s_dv, kv_done)
                if part_mm > 0:
                    P.w("pe", s_dv, part_add)
                def mk1(t, h=h):
                    t.matmul(b3[0:64, 64:128], kp_tm[:, h * 64:(h + 1) * 64], v_tm[:, h * 64:(h + 1) * 64],
                             start=True, stop=True)
                P.em("pe", mk1)
                def mk2(t, h=h):
                    t.matmul(b3[0:64, 128:129], kp_tm[:, h * 64:(h + 1) * 64], onesb[:],
                             start=True, stop=True).then_inc(s_pe.h, 1)
                P.em("pe", mk2)
                part_mm = pe_inc()
                spv = spart[:, h * 65:(h + 1) * 65]
                part_add = dve_op(lambda v, spv=spv: v.tensor_tensor(spv, spv, b3[0:64, 64:129], op=ALU.add),
                                  (s_pe, part_mm), (s_dv, spart_zero))
            kv_war = part_mm
        p1_pe_end = s_pe.n
        partials_done = last_dve

        P.w("sync", s_dv, partials_done)
        P.em("sync", lambda s: s.dma_start(out=cc_in[:], in_=spart[:]).then_inc(d_cs.h, 16)); d_cs.n += 16
        P.w("gp", d_cs, d_cs.n)
        P.em("gp", lambda g: g.collective_compute(
            "AllGather", ALU.bypass, replica_groups=[[0, 1, 2, 3, 4, 5, 6, 7]],
            ins=[cc_in[:]], outs=[cc_out[:]]).then_inc(s_cc.h, 1)); s_cc.n += 1
        P.w("gp", s_cc, s_cc.n)
        P.w("dve", d_ct, consts_rdy)
        for r in range(8):
            def mkg(g, r=r):
                g.dma_start(out=sgath[:], in_=cc_out[r * 64:(r + 1) * 64, :]).then_inc(d_ms.h, 16)
            P.em("gp", mkg)
            d_ms.n += 16
            gath_rdy = d_ms.n
            acc_done = dve_op(lambda v, r=r: v.scalar_tensor_tensor(
                saug[:], sgath[:], wselb[:, r:r + 1], saug[:],
                op0=ALU.mult, op1=ALU.add), (d_ms, gath_rdy), edge=True)
            if r < 7:
                P.w("gp", s_dv, acc_done)
        saug_bf_rdy = dve_op(lambda v: v.tensor_copy(saugb[:], saug[:]), edge=True)

        ffn_state = {
            "sil_par": 0,
            "g_pe_done": [0, 0],
            "y_evac": [0, 0],
            "g_hist": [0, 0],
            "sil_free": [0, 0],
            "bH_free": [0, 0],
            "bU_free": [0, 0],
            "fjg": 0,
            "last_g_pe": 0,
            "w13_loaded": {},
            "w2_loaded": {},
            "span_pe_done": {},
        }

        def load_w13(k, wait_pe=0):
            e_, sp_ = divmod(k, NSP)
            slot = k % 2
            dw = d_w[slot]
            if wait_pe:
                P.w("gp", s_pe, wait_pe)
            def mk(g, e_=e_, sp_=sp_, slot=slot, dw=dw):
                src1 = w1[e_].rearrange("(kh p) f -> p kh f", p=128)[:, :, sp_ * SPAN:(sp_ + 1) * SPAN]
                dst1 = wreg[:, slot * NKH * SPAN:(slot + 1) * NKH * SPAN].rearrange("p (kh f) -> p kh f", f=SPAN)
                g.dma_start(out=dst1, in_=src1).then_inc(dw.h, 16)
                src3 = w3[e_].rearrange("(kh p) f -> p kh f", p=128)[:, :, sp_ * SPAN:(sp_ + 1) * SPAN]
                dst3 = wreg[:, 8192 + slot * NKH * SPAN:8192 + (slot + 1) * NKH * SPAN].rearrange(
                    "p (kh f) -> p kh f", f=SPAN)
                g.dma_start(out=dst3, in_=src3).then_inc(dw.h, 16)
            P.em("gp", mk)
            dw.n += 32
            ffn_state["w13_loaded"][k] = dw.n

        def load_w2s(k, wait_pe=0):
            e_, sp_ = divmod(k, NSP)
            slot = k % 2
            dw = d_w2[slot]
            if wait_pe:
                P.w("gp", s_pe, wait_pe)
            def mk2(g, e_=e_, sp_=sp_, slot=slot, dw=dw):
                src2 = w2[e_].rearrange("(kf p) n -> p kf n", p=128)[:, sp_ * NFJ:(sp_ + 1) * NFJ, :]
                dst2 = wreg[:, 16384 + slot * NFJ * DM:16384 + (slot + 1) * NFJ * DM].rearrange(
                    "p (kf n) -> p kf n", n=DM)
                g.dma_start(out=dst2, in_=src2).then_inc(dw.h, 16)
            P.em("gp", mk2)
            dw.n += 16
            ffn_state["w2_loaded"][k] = dw.n

        def ffn_chunk(tc, pe_gate, pre_dv=(), pre_ac=()):
            st = ffn_state
            if tc == 1:
                P.w("sync", d_st, 16)
            xsrc = xs.rearrange("(tt p) h -> p tt h", p=128)[:, tc * NTT:(tc + 1) * NTT, :]
            P.em("sync", lambda s, xsrc=xsrc: s.dma_start(
                out=y_acc[:].rearrange("p (tt h) -> p tt h", h=DM), in_=xsrc).then_inc(d_yx.h, 16))
            d_yx.n += 16
            yx_rdy = d_yx.n

            st["w13_loaded"] = {}
            st["w2_loaded"] = {}
            st["span_pe_done"] = {}
            if tc == 0:
                load_w13(0)
                load_w13(1)
                load_w2s(0, wait_pe=pe_gate)
                load_w2s(1)
            else:
                load_w13(0, wait_pe=pe_gate)
                load_w13(1)
                load_w2s(0)
                load_w2s(1)

            bH = [b0, b4]
            bU = [b1, b5]
            for k in range(NSPANS):
                e_, sp_ = divmod(k, NSP)
                slot = k % 2
                half = k % 2
                P.w("pe", d_w[slot], st["w13_loaded"][k])
                P.w("pe", s_dv, st["y_evac"][1])
                if k == 0:
                    P.w("pe", s_dv, xn_evac[NTT * (tc + 1) - 1])
                    for v_ in pre_dv:
                        P.w("pe", s_dv, v_)
                    for v_ in pre_ac:
                        P.w("pe", s_ac, v_)
                for tg in range(2):
                    gpar = tg % 2
                    tof = tc * CTOK + tg * 512
                    for fj in range(NFJ):
                        spar = st["sil_par"]
                        bp = st["fjg"] % 2
                        P.w("pe", s_dv, st["g_hist"][bp])
                        pairs = [((lambda kh=kh, slot=slot, fj=fj: w1b(slot, kh, fj * 128, 128)),
                                  (lambda kh=kh, tof=tof: xn_fm[:, kh * TOK + tof:kh * TOK + tof + 512]))
                                 for kh in range(NKH)]
                        h_done = mm_group(lambda bp=bp: bH[bp][:], pairs)
                        P.w("act", s_pe, h_done)
                        P.w("act", s_dv, st["sil_free"][spar])
                        def mks(a, spar=spar, bp=bp):
                            a.activation(sil[:, spar * 512:(spar + 1) * 512], bH[bp][:], AF.Silu).then_inc(s_ac.h, 1)
                        P.em("act", mks)
                        st["bH_free"][bp] = ac_inc()
                        sil_done = s_ac.n
                        pairs = [((lambda kh=kh, slot=slot, fj=fj: w3b(slot, kh, fj * 128, 128)),
                                  (lambda kh=kh, tof=tof: xn_fm[:, kh * TOK + tof:kh * TOK + tof + 512]))
                                 for kh in range(NKH)]
                        u_done = mm_group(lambda bp=bp: bU[bp][:], pairs)
                        gd = gbuf[:, (gpar * 8 + half * NFJ + fj) * 512:(gpar * 8 + half * NFJ + fj) * 512 + 512]
                        g_done = dve_op(
                            lambda v, gd=gd, spar=spar, bp=bp: v.tensor_tensor(
                                gd, sil[:, spar * 512:(spar + 1) * 512], bU[bp][:], op=ALU.mult),
                            (s_ac, sil_done), (s_pe, u_done), (s_pe, st["g_pe_done"][gpar]))
                        st["g_hist"][bp] = g_done
                        st["bU_free"][bp] = g_done
                        st["sil_free"][spar] = g_done
                        st["sil_par"] ^= 1
                        st["fjg"] += 1
                if k + 2 < NSPANS:
                    load_w13(k + 2, wait_pe=s_pe.n)
                if k % 2 == 0:
                    continue
                g_rdy = s_dv.n
                P.w("pe", d_w2[0], st["w2_loaded"][k - 1])
                P.w("pe", d_w2[1], st["w2_loaded"][k])
                P.w("pe", s_dv, g_rdy)
                for tg in range(2):
                    gpar = tg % 2
                    for tt in range(4):
                        ttg = tg * 4 + tt
                        ct_g = tc * NTT + ttg
                        ypar = ttg % 2
                        ybuf = py if ypar == 0 else py2
                        P.w("pe", s_dv, st["y_evac"][ypar])
                        for ht in range(2):
                            pairs = []
                            for hf in range(2):
                                for fj in range(NFJ):
                                    pairs.append((
                                        (lambda fj=fj, gpar=gpar, tt=tt, hf=hf: gbuf[
                                            :, (gpar * 8 + hf * NFJ + fj) * 512 + tt * 128:
                                               (gpar * 8 + hf * NFJ + fj) * 512 + tt * 128 + 128]),
                                        (lambda fj=fj, hf=hf, ht=ht: w2b(hf, fj, ht * 512, 512)),
                                    ))
                            ymm = mm_group(lambda ht=ht, ybuf=ybuf: ybuf[:, ht * 512:(ht + 1) * 512], pairs)
                        dst = y_acc[:, ttg * DM:ttg * DM + DM]
                        cslice = combs[:, ct_g * E + e_:ct_g * E + e_ + 1]
                        st["y_evac"][ypar] = dve_op(
                            lambda v, dst=dst, cs=cslice, ybuf=ybuf: v.scalar_tensor_tensor(
                                dst, ybuf[:], cs, dst, op0=ALU.mult, op1=ALU.add),
                            (s_pe, ymm), (d_yx, yx_rdy), (s_dv, comb_done))
                    st["g_pe_done"][gpar] = s_pe.n
                g_end = s_pe.n
                st["span_pe_done"][k - 1] = g_end
                st["span_pe_done"][k] = g_end
                if k + 2 < NSPANS:
                    load_w2s(k + 1, wait_pe=g_end)
                    load_w2s(k + 2)
            return s_pe.n
            return s_pe.n

        scan_state = {
            "msc_free": [0, 0],
            "rec_free": [0, 0],
            "b2_free": 0,
            "b4_free": 0,
            "b5_free": 0,
            "b3_free": 0,
            "qkv_bank_free": [0, 0],
            "exp_free": [0, 0],
            "sc_free": [0, 0],
            "nd_free": [0, 0],
            "saug_rdy_pair": [0] * (NH // 2),
            "qp_war": 0,
            "kv_war": 0,
        }

        def scan_chunk(tc, wq_rdy, pre_ac=0, pre_dv=0):
            nonlocal b6_evac, saug_bf_rdy
            ss = scan_state
            proj_evac = 0
            for ctl in range(8):
                ct = tc * 8 + ctl
                for sl in range(2):
                    bank = [b0, b1][sl % 2]
                    pairs = [((lambda kh=kh, ct=ct: xn_fm[:, kh * TOK + ct * 128:kh * TOK + ct * 128 + 128]),
                              (lambda kh=kh, sl=sl: wqT(kh, sl * 512, 512)))
                             for kh in range(NKH)]
                    P.w("pe", d_wq, wq_rdy)
                    P.w("pe", s_dv, ss["qkv_bank_free"][sl % 2])
                    if ctl == 0 and sl == 0:
                        P.w("pe", s_ac, pre_ac)
                        P.w("pe", s_dv, pre_dv)
                    q_mm = mm_group(lambda b=bank: b[:], pairs)
                    tpar = sl % 2
                    tmn = tmin[:, tpar * 512:(tpar + 1) * 512]
                    tex = texp[:, tpar * 512:(tpar + 1) * 512]
                    mn_done = dve_op(lambda v, b=bank, t=tmn: v.tensor_scalar_min(t, b[:], 0.0),
                                     (s_pe, q_mm), (s_ac, ss["exp_free"][tpar]))
                    P.w("act", s_dv, mn_done)
                    def mke(a, tmn=tmn, tex=tex):
                        a.activation(tex, tmn, AF.Exp).then_inc(s_ac.h, 1)
                    P.em("act", mke)
                    exp_done = ac_inc()
                    ss["exp_free"][tpar] = exp_done
                    qslice = qp_tm[:, sl * 512:(sl + 1) * 512]
                    ss["qkv_bank_free"][sl % 2] = dve_op(
                        lambda v, b=bank, tex=tex, qs=qslice: v.scalar_tensor_tensor(
                            qs, b[:], 0.0, tex, op0=ALU.max, op1=ALU.add),
                        (s_ac, exp_done), (s_pe, ss["qp_war"]), edge=True)
                for sl in range(4):
                    bank = [b0, b1][sl % 2]
                    pairs = [((lambda kh=kh, ct=ct: xn_fm[:, kh * TOK + ct * 128:kh * TOK + ct * 128 + 128]),
                              (lambda kh=kh, sl=sl: wqT(kh, 1024 + sl * 512, 512)))
                             for kh in range(NKH)]
                    P.w("pe", s_dv, ss["qkv_bank_free"][sl % 2])
                    kv_mm = mm_group(lambda b=bank: b[:], pairs)
                    if sl < 2:
                        tpar = sl % 2
                        tmn = tmin[:, tpar * 512:(tpar + 1) * 512]
                        tex = texp[:, tpar * 512:(tpar + 1) * 512]
                        mn_done = dve_op(lambda v, b=bank, t=tmn: v.tensor_scalar_min(t, b[:], 0.0),
                                         (s_pe, kv_mm), (s_ac, ss["exp_free"][tpar]))
                        P.w("act", s_dv, mn_done)
                        def mke(a, tmn=tmn, tex=tex):
                            a.activation(tex, tmn, AF.Exp).then_inc(s_ac.h, 1)
                        P.em("act", mke)
                        exp_done = ac_inc()
                        ss["exp_free"][tpar] = exp_done
                        kslice = kp_tm[:, sl * 512:(sl + 1) * 512]
                        ss["qkv_bank_free"][sl % 2] = dve_op(
                            lambda v, b=bank, tex=tex, ks=kslice: v.scalar_tensor_tensor(
                                ks, b[:], 0.0, tex, op0=ALU.max, op1=ALU.add),
                            (s_ac, exp_done), (s_pe, ss["kv_war"]), edge=True)
                    else:
                        vslice = v_tm[:, (sl - 2) * 512:(sl - 1) * 512]
                        ss["qkv_bank_free"][sl % 2] = dve_op(
                            lambda v, b=bank, vs=vslice: v.tensor_copy(vs, b[:]), (s_pe, kv_mm), (s_pe, ss["kv_war"]))
                qkv_rdy = last_dve

                for (srct, dstv) in ((qp_tm, qp_fm), (kp_tm, kp_fm)):
                    for half in range(2):
                        P.w("pe", s_dv, qkv_rdy)
                        P.w("pe", s_dv, b6_evac)
                        for hh in range(8):
                            h = half * 8 + hh
                            def mk(t, srct=srct, h=h, hh=hh, sp=(hh == 7)):
                                ins = t.transpose(b6[0:64, hh * 128:(hh + 1) * 128],
                                                  srct[:, h * 64:(h + 1) * 64], identb[:])
                                if sp:
                                    ins.then_inc(s_pe.h, 1)
                            P.em("pe", mk)
                        trd = pe_inc()
                        b6_evac = dve_op(lambda v, dstv=dstv, half=half: v.tensor_copy(
                            dstv[:, half * 1024:(half + 1) * 1024], b6[0:64, 0:1024]), (s_pe, trd))
                fm_rdy = b6_evac
                ss["qp_war"] = s_pe.n

                sc_banks = [py[:, 0:256], b7[:, 0:256]]
                nd_banks = [py2[:, 0:128], b0[:, 0:128]]

                def emit_sc(p):
                    bankap = sc_banks[p % 2]
                    P.w("pe", s_dv, ss["sc_free"][p % 2])
                    for i in range(2):
                        h = 2 * p + i
                        def mk(t, h=h, i=i, bankap=bankap):
                            ins = t.matmul(bankap[:, i * 128:(i + 1) * 128],
                                           kp_fm[:, h * 128:(h + 1) * 128],
                                           qp_fm[:, h * 128:(h + 1) * 128], start=True, stop=True)
                            if i == 1:
                                ins.then_inc(s_pe.h, 1)
                        P.em("pe", mk)
                    return pe_inc()

                sc_mm = {}
                sc_mm[0] = emit_sc(0)
                sc_mm[1] = emit_sc(1)
                for p in range(NH // 2):
                    ppar = p % 2
                    bankap = sc_banks[ppar]
                    mslice = msc[:, ppar * 256:(ppar + 1) * 256] if False else msc[:, 0:256] if ppar == 0 else msc[:, 256:512]
                    P.w("dve", s_pe, ss["msc_free"][ppar])
                    mask_done = dve_op(
                        lambda v, ms=mslice, bankap=bankap: v.tensor_tensor(
                            ms.rearrange("p (i c) -> p i c", c=128), bankap[:].rearrange("p (i c) -> p i c", c=128),
                            maskb[:].rearrange("p c -> p 1 c").broadcast(1, 2) if False else bankap[:].rearrange("p (i c) -> p i c", c=128), op=ALU.mult),
                        (s_pe, sc_mm[p]), (d_ct, consts_rdy)) if False else None
                    m0 = msc[:, ppar * 256:ppar * 256 + 128]
                    m1 = msc[:, ppar * 256 + 128:ppar * 256 + 256]
                    dve_op(lambda v, m0=m0, bankap=bankap: v.tensor_tensor(m0, bankap[:, 0:128], maskb[:], op=ALU.mult),
                           (s_pe, sc_mm[p]), (d_ct, consts_rdy))
                    mask_done = dve_op(lambda v, m1=m1, bankap=bankap: v.tensor_tensor(m1, bankap[:, 128:256], maskb[:], op=ALU.mult))
                    ss["b2_free"] = mask_done
                    ss["sc_free"][ppar] = mask_done
                    if p + 2 < NH // 2:
                        sc_mm[p + 2] = emit_sc(p + 2)
                    ndb = nd_banks[ppar]
                    P.w("pe", s_dv, mask_done)
                    P.w("pe", s_ac, ss["nd_free"][ppar])
                    P.w("pe", s_dv, ss["saug_rdy_pair"][p])
                    for i in range(2):
                        h = 2 * p + i
                        def mkn1(t, h=h, i=i, ppar=ppar, ndb=ndb):
                            t.matmul(ndb[:, i * 64:(i + 1) * 64],
                                     msc[:, ppar * 256 + i * 128:ppar * 256 + (i + 1) * 128],
                                     v_tm[:, h * 64:(h + 1) * 64], start=True, stop=False)
                        P.em("pe", mkn1)
                        def mkn2(t, h=h, i=i, ndb=ndb, last=(i == 1)):
                            ins = t.matmul(ndb[:, i * 64:(i + 1) * 64], qp_fm[:, h * 128:(h + 1) * 128],
                                           saugb[:, h * 65:h * 65 + 64], start=False, stop=True)
                            if last:
                                ins.then_inc(s_pe.h, 1)
                        P.em("pe", mkn2)
                    num_mm = pe_inc()
                    P.w("pe", s_dv, ss["b5_free"])
                    for i in range(2):
                        h = 2 * p + i
                        def mkd1(t, i=i, ppar=ppar):
                            t.matmul(b5[:, i:i + 1], msc[:, ppar * 256 + i * 128:ppar * 256 + (i + 1) * 128],
                                     onesb[:], start=True, stop=False)
                        P.em("pe", mkd1)
                        def mkd2(t, h=h, i=i, last=(i == 1)):
                            ins = t.matmul(b5[:, i:i + 1], qp_fm[:, h * 128:(h + 1) * 128],
                                           saugb[:, h * 65 + 64:h * 65 + 65], start=False, stop=True)
                            if last:
                                ins.then_inc(s_pe.h, 1)
                        P.em("pe", mkd2)
                    den_mm = pe_inc()
                    ss["msc_free"][ppar] = den_mm
                    P.w("pe", s_dv, ss["b3_free"])
                    for i in range(2):
                        h = 2 * p + i
                        def mks1(t, h=h, i=i):
                            t.matmul(py[0:64, 512 + 64 + i * 65:512 + 128 + i * 65],
                                     kp_tm[:, h * 64:(h + 1) * 64], v_tm[:, h * 64:(h + 1) * 64],
                                     start=True, stop=True)
                        P.em("pe", mks1)
                        def mks2(t, h=h, i=i, last=(i == 1)):
                            ins = t.matmul(py[0:64, 512 + 128 + i * 65:512 + 129 + i * 65],
                                           kp_tm[:, h * 64:(h + 1) * 64], onesb[:], start=True, stop=True)
                            if last:
                                ins.then_inc(s_pe.h, 1)
                        P.em("pe", mks2)
                    ds_mm = pe_inc()
                    rc = 21 + 2 * ppar
                    dn = 17 + 2 * ppar
                    P.w("dve", s_ac, ss["rec_free"][ppar])
                    dve_op(lambda v, dn=dn: v.tensor_scalar_add(lns[:, dn:dn + 2], b5[:, 0:2], EPS_DEN),
                           (s_pe, den_mm))
                    ss["b5_free"] = last_dve
                    rec_done = dve_op(lambda v, rc=rc, dn=dn: v.reciprocal(lns[:, rc:rc + 2], lns[:, dn:dn + 2]),
                                      edge=True)
                    for i in range(2):
                        h = 2 * p + i
                        P.w("act", s_pe, num_mm)
                        P.w("act", s_dv, rec_done)
                        def mky(a, h=h, i=i, rc=rc, ndb=ndb):
                            a.activation(y_tm[:, h * 64:(h + 1) * 64], ndb[:, i * 64:(i + 1) * 64], AF.Identity,
                                         scale=lns[:, rc + i:rc + i + 1]).then_inc(s_ac.h, 1)
                        P.em("act", mky)
                        ac_inc()
                    ss["nd_free"][ppar] = s_ac.n
                    ss["rec_free"][ppar] = s_ac.n
                    sgv = saug[:, 2 * p * 65:(2 * p + 2) * 65]
                    dve_op(lambda v, sgv=sgv: v.tensor_tensor(sgv, sgv, py[0:64, 512 + 64:512 + 194], op=ALU.add),
                           (s_pe, ds_mm), (s_dv, ss["saug_rdy_pair"][p]))
                    ss["b3_free"] = last_dve
                    sgbv = saugb[:, 2 * p * 65:(2 * p + 2) * 65]
                    ss["saug_rdy_pair"][p] = dve_op(lambda v, sgv=sgv, sgbv=sgbv: v.tensor_copy(sgbv, sgv), edge=True)
                    ss["kv_war"] = ds_mm
                y_all = s_ac.n

                P.w("pe", s_ac, y_all)
                P.w("pe", s_dv, b6_evac)
                for j in range(NKH):
                    def mk(t, j=j, sp=(j == NKH - 1)):
                        ins = t.transpose(b6[:, j * 128:(j + 1) * 128], y_tm[:, j * 128:(j + 1) * 128], identb[:])
                        if sp:
                            ins.then_inc(s_pe.h, 1)
                    P.em("pe", mk)
                ytr = pe_inc()
                b6_evac = dve_op(lambda v: v.tensor_copy(y_fm[:], b6[:, 0:1024]), (s_pe, ytr))
                yfm_rdy = b6_evac
                for cs in range(2):
                    bank = b7 if cs == 0 else b2
                    bid = "b7" if cs == 0 else "b2"
                    pairs = [((lambda j=j: y_fm[:, j * 128:(j + 1) * 128]),
                              (lambda j=j, cs=cs: wpT(j, cs * 512, 512)))
                             for j in range(NKH)]
                    P.w("pe", s_dv, yfm_rdy)
                    if cs == 1:
                        P.w("pe", s_dv, ss["b2_free"])
                        P.w("pe", s_dv, proj_evac)
                    else:
                        P.w("pe", s_dv, proj_evac)
                    pj_mm = mm_group(lambda b=bank: b[:], pairs)
                    dst = y_acc[:, ctl * DM + cs * 512:ctl * DM + cs * 512 + 512]
                    proj_evac = dve_op(lambda v, dst=dst, b=bank: v.scalar_tensor_tensor(
                        dst, b[:], 1.0, dst, op0=ALU.mult, op1=ALU.add), (s_pe, pj_mm))
                    if cs == 1:
                        ss["b2_free"] = proj_evac
            return proj_evac

        ffn0_pe_end = ffn_chunk(0, pe_gate=p1_pe_end, pre_dv=(kv_bank_free[0], kv_bank_free[1]))
        P.w("sync", s_pe, ffn0_pe_end)
        for kh in range(NKH):
            def mk(s, kh=kh):
                s.dma_start(out=wqT(kh, 0, 3072), in_=wattn_bf[:, kh * 128:(kh + 1) * 128],
                            transpose=True).then_inc(d_wq.h, 16)
            P.em("sync", mk)
            d_wq.n += 16
        for kh in range(NKH):
            def mk(s, kh=kh):
                s.dma_start(out=wpT(kh, 0, 1024), in_=wp_bf[:, kh * 128:(kh + 1) * 128],
                            transpose=True).then_inc(d_wq.h, 16)
            P.em("sync", mk)
            d_wq.n += 16
        wq_full0 = d_wq.n
        scan_state["kv_war"] = p1_pe_end
        scan_state["saug_rdy_pair"] = [saug_bf_rdy] * (NH // 2)
        scan_state["b5_free"] = max(max(ffn_state["bU_free"]), max(ffn_state["y_evac"]))
        scan0_end = scan_chunk(0, wq_full0, pre_ac=max(ffn_state["bH_free"]),
                               pre_dv=max(max(ffn_state["bU_free"]), max(ffn_state["y_evac"])))
        P.w("sync", s_dv, scan0_end)
        yo0 = yout.rearrange("(tt p) h -> p tt h", p=128)[:, 0:NTT, :]
        P.em("sync", lambda s, yo0=yo0: s.dma_start(
            out=yo0, in_=y_acc[:].rearrange("p (tt h) -> p tt h", h=DM)).then_inc(d_st.h, 16))
        d_st.n += 16

        scan0_pe_end = s_pe.n
        ffn1_pe_end = ffn_chunk(1, pe_gate=scan0_pe_end,
                                pre_dv=(scan_state["qkv_bank_free"][0], scan_state["qkv_bank_free"][1],
                                        scan_state["b5_free"], scan_state["b3_free"], scan_state["b2_free"]),
                                pre_ac=(scan_state["nd_free"][0], scan_state["nd_free"][1],
                                        scan_state["rec_free"][0], scan_state["rec_free"][1]))
        P.w("sync", s_pe, ffn1_pe_end)
        for kh in range(NKH):
            def mk(s, kh=kh):
                s.dma_start(out=wqT(kh, 0, 3072), in_=wattn_bf[:, kh * 128:(kh + 1) * 128],
                            transpose=True).then_inc(d_wq.h, 16)
            P.em("sync", mk)
            d_wq.n += 16
        for kh in range(NKH):
            def mk(s, kh=kh):
                s.dma_start(out=wpT(kh, 0, 1024), in_=wp_bf[:, kh * 128:(kh + 1) * 128],
                            transpose=True).then_inc(d_wq.h, 16)
            P.em("sync", mk)
            d_wq.n += 16
        wq_full1 = d_wq.n
        scan_state["b5_free"] = max(scan_state["b5_free"], max(ffn_state["bU_free"]), max(ffn_state["y_evac"]))
        scan1_end = scan_chunk(1, wq_full1, pre_ac=max(ffn_state["bH_free"]),
                               pre_dv=max(max(ffn_state["bU_free"]), max(ffn_state["y_evac"])))
        P.w("sync", s_dv, scan1_end)
        yo1 = yout.rearrange("(tt p) h -> p tt h", p=128)[:, NTT:2 * NTT, :]
        P.em("sync", lambda s, yo1=yo1: s.dma_start(
            out=yo1, in_=y_acc[:].rearrange("p (tt h) -> p tt h", h=DM)).then_inc(d_st.h, 16))
        d_st.n += 16
        P.w("sync", d_st, d_st.n)

        with nc.Block() as block:
            @block.tensor
            def _(t):
                for f in P.ops["pe"]:
                    f(t)

            @block.scalar
            def _(a):
                for f in P.ops["act"]:
                    f(a)

            @block.vector
            def _(v):
                for f in P.ops["dve"]:
                    f(v)

            @block.sync
            def _(s):
                for f in P.ops["sync"]:
                    f(s)

            @block.gpsimd
            def _(g):
                for f in P.ops["gp"]:
                    f(g)

    return nc


_NC_CACHE = {}


def kernel(L0_attn_x, L0_attn_w_attn, L0_attn_w_proj, L0_ffn_gate_w, L0_ffn_w1, L0_ffn_w3, L0_ffn_w2):
    x = np.asarray(L0_attn_x)
    w_attn = np.ascontiguousarray(np.asarray(L0_attn_w_attn), dtype=np.float32)
    w_proj = np.ascontiguousarray(np.asarray(L0_attn_w_proj), dtype=np.float32)
    gate_w = np.ascontiguousarray(np.asarray(L0_ffn_gate_w), dtype=np.float32)
    w1 = np.ascontiguousarray(np.asarray(L0_ffn_w1), dtype=np.float32)
    w3 = np.ascontiguousarray(np.asarray(L0_ffn_w3), dtype=np.float32)
    w2 = np.ascontiguousarray(np.asarray(L0_ffn_w2), dtype=np.float32)

    ident = np.eye(128, dtype=ml_dtypes.bfloat16)
    maskT = np.triu(np.ones((128, 128))).astype(ml_dtypes.bfloat16)
    onesc = np.ones((128, 1), dtype=ml_dtypes.bfloat16)

    if "nc" not in _NC_CACHE:
        _NC_CACHE["nc"] = build_nc()
    nc = _NC_CACHE["nc"]

    in_maps = []
    for c in range(8):
        b, seg = divmod(c, 4)
        xs = np.ascontiguousarray(x[b, seg * TOK:(seg + 1) * TOK, :], dtype=np.float32)
        wsel = np.zeros((64, 8), np.float32)
        for r in range(8):
            rb, rseg = divmod(r, 4)
            if rb == b and rseg < seg:
                wsel[:, r] = 1.0
        in_maps.append({
            "xs": xs, "w_attn": w_attn, "w_proj": w_proj, "gate_w": gate_w,
            "w1": w1, "w3": w3, "w2": w2,
            "ident": ident, "maskT": maskT, "onesc": onesc, "wsel": wsel,
        })

    res = run_bass_kernel_spmd(nc, in_maps, core_ids=list(range(8)))
    out = np.empty((2, 8192, 1024), np.float32)
    for c in range(8):
        b, seg = divmod(c, 4)
        out[b, seg * TOK:(seg + 1) * TOK, :] = np.asarray(res.results[c]["yout"])
    return out


# version 09
# speedup vs baseline: 1.4159x; 1.0159x over previous
import contextlib
import numpy as np
import ml_dtypes
import concourse.bass as bass
import concourse.mybir as mybir
from concourse.bass_utils import run_bass_kernel_spmd

F32 = mybir.dt.float32
BF16 = mybir.dt.bfloat16
AF = mybir.ActivationFunctionType
ALU = mybir.AluOpType
AX = mybir.AxisListType

TOK = 2048
NCT = 16
DM = 1024
NKH = 8
NH = 16
HD = 64
E = 8
FF = 4096
SPAN = 512
NSP = FF // SPAN
NFJ = SPAN // 128
NSPANS = E * NSP
NCH = 2
CTOK = 1024
NTT = 8
EPS_LN = 1e-5
EPS_DEN = 1e-6


class Sem:
    def __init__(self, es, nc, name):
        self.h = es.enter_context(nc.semaphore(name))
        self.n = 0


class B:

    def __init__(self):
        self.ops = {k: [] for k in ("pe", "act", "dve", "sync", "gp")}

    def em(self, eng, fn):
        self.ops[eng].append(fn)

    def w(self, eng, sem, val):
        if val > 0:
            self.ops[eng].append(lambda e, s=sem.h, v=val: e.wait_ge(s, v))


def build_nc():
    nc = bass.Bass("TRN2", target_bir_lowering=False)
    xs = nc.dram_tensor("xs", [TOK, DM], F32, kind="ExternalInput").ap()
    w_attn = nc.dram_tensor("w_attn", [3 * DM, DM], F32, kind="ExternalInput").ap()
    w_proj = nc.dram_tensor("w_proj", [DM, DM], F32, kind="ExternalInput").ap()
    gate_w = nc.dram_tensor("gate_w", [E, DM], F32, kind="ExternalInput").ap()
    w1 = nc.dram_tensor("w1", [E, DM, FF], F32, kind="ExternalInput").ap()
    w3 = nc.dram_tensor("w3", [E, DM, FF], F32, kind="ExternalInput").ap()
    w2 = nc.dram_tensor("w2", [E, FF, DM], F32, kind="ExternalInput").ap()
    ident = nc.dram_tensor("ident", [128, 128], BF16, kind="ExternalInput").ap()
    maskT = nc.dram_tensor("maskT", [128, 128], BF16, kind="ExternalInput").ap()
    onesc = nc.dram_tensor("onesc", [128, 1], BF16, kind="ExternalInput").ap()
    wsel = nc.dram_tensor("wsel", [64, 8], F32, kind="ExternalInput").ap()
    yout = nc.dram_tensor("yout", [TOK, DM], F32, kind="ExternalOutput").ap()
    wattn_bf = nc.dram_tensor("wattn_bf", [3 * DM, DM], BF16, kind="Internal").ap()
    w1bf = nc.dram_tensor("w1bf", [E, DM, FF], BF16, kind="Internal").ap()
    w3bf = nc.dram_tensor("w3bf", [E, DM, FF], BF16, kind="Internal").ap()
    w2bf = nc.dram_tensor("w2bf", [E, FF, DM], BF16, kind="Internal").ap()
    wp_bf = nc.dram_tensor("wp_bf", [DM, DM], BF16, kind="Internal").ap()
    cc_in = nc.dram_tensor("cc_in", [64, NH * 65], F32, kind="Internal").ap()
    cc_out = nc.dram_tensor("cc_out", [8 * 64, NH * 65], F32, kind="Internal").ap()

    es = contextlib.ExitStack()
    sb = lambda n, s, d: es.enter_context(nc.sbuf_tensor(n, s, d))
    ps = lambda n, s, d: es.enter_context(nc.psum_tensor(n, s, d))
    with es:
        xn_fm = sb("xn_fm", [128, NKH * TOK], BF16)
        wreg = sb("wreg", [128, 32768], BF16)
        y_acc = sb("y_acc", [128, NTT * DM], F32)
        combs = sb("combs", [128, NCT * E], F32)
        saug = sb("saug", [64, NH * 65], F32)
        saugb = sb("saugb", [64, NH * 65], BF16)
        spart = sb("spart", [64, NH * 65], F32)
        sgath = sb("sgath", [64, NH * 65], F32)
        wselb = sb("wselb", [64, 8], F32)
        xsb = sb("xsb", [128, 2 * DM], F32)
        xn_tm = sb("xn_tm", [128, DM], BF16)
        lns = sb("lns", [128, 32], F32)
        gtmp = sb("gtmp", [128, 32], F32)
        gw8 = sb("gw8", [16, DM], BF16)
        gwT = sb("gwT", [128, NKH * E], BF16)
        identb = sb("identb", [128, 128], BF16)
        maskb = sb("maskb", [128, 128], BF16)
        onesb = sb("onesb", [128, 1], BF16)
        kp_tm = sb("kp_tm", [128, DM], BF16)
        v_tm = sb("v_tm", [128, DM], BF16)
        qp_tm = sb("qp_tm", [128, DM], BF16)
        qp_fm = sb("qp_fm", [64, NH * 128], BF16)
        kp_fm = sb("kp_fm", [64, NH * 128], BF16)
        y_tm = sb("y_tm", [128, DM], BF16)
        y_fm = sb("y_fm", [128, DM], BF16)
        tmin = sb("tmin", [128, 2 * 512], F32)
        texp = sb("texp", [128, 2 * 512], F32)
        msc = sb("msc", [128, 512], BF16)
        sil = sb("sil", [128, 2 * 512], BF16)
        gbuf = sb("gbuf", [128, 2 * 8 * 512], BF16)

        def w1b(slot, kh, fcol0, n):
            off = slot * (NKH * SPAN) + kh * SPAN + fcol0
            return wreg[:, off:off + n]

        def w3b(slot, kh, fcol0, n):
            off = 8192 + slot * (NKH * SPAN) + kh * SPAN + fcol0
            return wreg[:, off:off + n]

        def w2b(slot, kf, hcol0, n):
            off = 16384 + slot * (NFJ * DM) + kf * DM + hcol0
            return wreg[:, off:off + n]

        def wqT(kh, c0, n):
            return wreg[:, kh * 3072 + c0:kh * 3072 + c0 + n]

        def wqTA(kh, c0, n):
            return wreg[:, 16384 + kh * 2048 + (c0 - 1024):16384 + kh * 2048 + (c0 - 1024) + n]

        def wpT(kh, c0, n):
            return wreg[:, 24576 + kh * DM + c0:24576 + kh * DM + c0 + n]

        b0 = ps("b0", [128, 512], F32)
        b1 = ps("b1", [128, 512], F32)
        py = ps("py", [128, 1024], F32)
        b2 = py[:, 0:512]
        b3 = py[:, 512:1024]
        py2 = ps("py2", [128, 1024], F32)
        b4 = py2[:, 0:512]
        b5 = py2[:, 512:1024]
        b6 = ps("b6", [128, 1024], BF16)
        b7 = ps("b7", [128, 512], F32)

        d_x = [Sem(es, nc, "d_x0"), Sem(es, nc, "d_x1")]
        d_yx = Sem(es, nc, "d_yx")
        d_w = [Sem(es, nc, "d_w0"), Sem(es, nc, "d_w1")]
        d_w2 = [Sem(es, nc, "d_w2a"), Sem(es, nc, "d_w2b")]
        d_wq = Sem(es, nc, "d_wq")
        d_ms = Sem(es, nc, "d_ms")
        d_cast = Sem(es, nc, "d_cast")
        d_ct = Sem(es, nc, "d_ct")
        d_cs = Sem(es, nc, "d_cs")
        s_cc = Sem(es, nc, "s_cc")
        s_pe = Sem(es, nc, "s_pe")
        s_ac = Sem(es, nc, "s_ac")
        s_dv = Sem(es, nc, "s_dv")
        d_st = Sem(es, nc, "d_st")

        P = B()

        def pe_inc():
            s_pe.n += 1
            return s_pe.n

        def ac_inc():
            s_ac.n += 1
            return s_ac.n

        def dv_inc():
            s_dv.n += 1
            return s_dv.n

        def mm_group(bank_ap_fn, pairs, extra_first_wait=None):
            n = len(pairs)
            for i, (lf, rf) in enumerate(pairs):
                def mk(t, lf=lf, rf=rf, st=(i == 0), sp=(i == n - 1), bf=bank_ap_fn):
                    ins = t.matmul(bf(), lf(), rf(), start=st, stop=sp)
                    if sp:
                        ins.then_inc(s_pe.h, 1)
                P.em("pe", mk)
            return pe_inc()

        P.em("sync", lambda s: s.dma_start(out=identb[:], in_=ident).then_inc(d_ct.h, 16)); d_ct.n += 16
        P.em("sync", lambda s: s.dma_start(out=maskb[:], in_=maskT).then_inc(d_ct.h, 16)); d_ct.n += 16
        P.em("sync", lambda s: s.dma_start(out=onesb[:], in_=onesc).then_inc(d_ct.h, 16)); d_ct.n += 16
        P.em("sync", lambda s: s.dma_start(out=wselb[:], in_=wsel).then_inc(d_ct.h, 16)); d_ct.n += 16
        consts_rdy = d_ct.n

        P.em("gp", lambda g: g.dma_start(out=wattn_bf[:], in_=w_attn).then_inc(d_ms.h, 16)); d_ms.n += 16
        P.em("gp", lambda g: g.dma_start(out=gw8[0:8, :], in_=gate_w).then_inc(d_ms.h, 16)); d_ms.n += 16
        P.em("gp", lambda g: g.dma_start(out=wp_bf[:], in_=w_proj).then_inc(d_ms.h, 16)); d_ms.n += 16
        p0_gp = d_ms.n

        P.em("dve", lambda v: v.memset(saug[:], 0.0).then_inc(s_dv.h, 1)); dv_inc()
        P.em("dve", lambda v: v.memset(spart[:], 0.0).then_inc(s_dv.h, 1)); spart_zero = dv_inc()

        P.w("pe", d_ms, p0_gp)
        P.w("pe", d_ct, consts_rdy)
        for kh in range(NKH):
            def mk(t, kh=kh, sp=(kh == NKH - 1)):
                ins = t.transpose(b6[:, kh * 8:kh * 8 + 8], gw8[0:8, kh * 128:(kh + 1) * 128], identb[0:8, 0:8])
                if sp:
                    ins.then_inc(s_pe.h, 1)
            P.em("pe", mk)
        gwt_tr = pe_inc()
        P.w("dve", s_pe, gwt_tr)
        P.em("dve", lambda v: v.tensor_copy(gwT[:], b6[:, 0:NKH * E]).then_inc(s_dv.h, 1))
        gwt_rdy = dv_inc()

        P.w("sync", d_ms, p0_gp)
        for kh in range(NKH):
            def mk(s, kh=kh):
                s.dma_start(out=wqTA(kh, 1024, 2048), in_=wattn_bf[1024:3072, kh * 128:(kh + 1) * 128],
                            transpose=True).then_inc(d_wq.h, 16)
            P.em("sync", mk)
            d_wq.n += 16
        wq_kv_rdy = d_wq.n

        SX, SSQ, M_, EX2, T1_, VAR, LNV, RSTD, NMR = range(9)
        D_, ED, DEN0, DEN1, REC0, REC1, S1_, S2_ = range(9, 17)

        def col(i):
            return lns[:, i:i + 1]

        last_dve = 0
        exp_free = [0, 0]
        xn_evac = [0] * NCT
        xnT_done = [0] * NCT
        xn_consumed_act = [0] * NCT
        b6_evac = 0
        kv_bank_free = [0, 0]
        part_mm = 0
        kv_war = 0

        def dve_op(fn, *waits, edge=False):
            nonlocal last_dve
            for sem, val in waits:
                P.w("dve", sem, val)
            if edge:
                P.w("dve", s_dv, last_dve)
            def mk(v, fn=fn):
                fn(v).then_inc(s_dv.h, 1)
            P.em("dve", mk)
            last_dve = dv_inc()
            return last_dve

        def ln_gate(ct):
            nonlocal last_dve, b6_evac
            par = ct % 2
            xt = xsb[:, par * DM:(par + 1) * DM]
            if ct >= 2:
                P.w("sync", s_ac, xn_consumed_act[ct - 2])
            def mkx(s, ct=ct, xt=xt, dx=d_x[par]):
                s.dma_start(out=xt, in_=xs[ct * 128:(ct + 1) * 128, :]).then_inc(dx.h, 16)
            P.em("sync", mkx)
            d_x[par].n += 16
            x_rdy = d_x[par].n

            sum_done = dve_op(lambda v, xt=xt: v.reduce_sum(col(SX), xt, axis=AX.X), (d_x[par], x_rdy))
            if ct >= 1:
                P.w("dve", s_pe, xnT_done[ct - 1])
            ssq_done = dve_op(lambda v, xt=xt: v.scalar_tensor_tensor(
                xn_tm[:], xt, 1.0, xt, op0=ALU.mult, op1=ALU.mult, accum_out=col(SSQ)))
            dve_op(lambda v: v.tensor_scalar_mul(col(M_), col(SX), 1.0 / DM), edge=True)
            dve_op(lambda v: v.tensor_scalar_mul(col(EX2), col(SSQ), 1.0 / DM), edge=True)
            dve_op(lambda v: v.tensor_tensor(col(T1_), col(M_), col(M_), op=ALU.mult), edge=True)
            dve_op(lambda v: v.tensor_tensor(col(VAR), col(EX2), col(T1_), op=ALU.subtract), edge=True)
            var_done = dve_op(lambda v: v.tensor_scalar_add(col(VAR), col(VAR), EPS_LN), edge=True)
            P.w("act", s_dv, var_done)
            P.em("act", lambda a: a.activation(col(LNV), col(VAR), AF.Ln).then_inc(s_ac.h, 1))
            lnv_done = ac_inc()
            P.w("act", s_ac, lnv_done)
            P.em("act", lambda a: a.activation(col(RSTD), col(LNV), AF.Exp, scale=-0.5).then_inc(s_ac.h, 1))
            rstd_done = ac_inc()
            nmr_done = dve_op(lambda v: v.scalar_tensor_tensor(
                col(NMR), col(M_), -1.0, col(RSTD), op0=ALU.mult, op1=ALU.mult), (s_ac, rstd_done), edge=True)
            P.w("act", s_dv, nmr_done)
            P.w("act", s_ac, rstd_done)
            def mkxn(a, xt=xt):
                a.activation(xn_tm[:], xt, AF.Identity, bias=col(NMR), scale=col(RSTD)).then_inc(s_ac.h, 1)
            P.em("act", mkxn)
            xn_done = ac_inc()
            xn_consumed_act[ct] = xn_done

            P.w("pe", s_ac, xn_done)
            P.w("pe", s_dv, b6_evac)
            for kh in range(NKH):
                def mk(t, kh=kh, sp=(kh == NKH - 1)):
                    ins = t.transpose(b6[:, kh * 128:(kh + 1) * 128], xn_tm[:, kh * 128:(kh + 1) * 128], identb[:])
                    if sp:
                        ins.then_inc(s_pe.h, 1)
                P.em("pe", mk)
            xnT_done[ct] = pe_inc()
            xn_view = xn_fm[:].rearrange("p (kh t) -> p kh t", t=TOK)[:, :, ct * 128:(ct + 1) * 128]
            b6v = b6[:, 0:1024].rearrange("p (kh t) -> p kh t", t=128)
            xn_evac[ct] = dve_op(lambda v, xv=xn_view, bv=b6v: v.tensor_copy(xv, bv), (s_pe, xnT_done[ct]))
            b6_evac = xn_evac[ct]

            pairs = []
            for kh in range(NKH):
                pairs.append((
                    (lambda kh=kh, ct=ct: xn_fm[:, kh * TOK + ct * 128:kh * TOK + ct * 128 + 128]),
                    (lambda kh=kh: gwT[:, kh * 8:(kh + 1) * 8]),
                ))
            P.w("pe", s_dv, xn_evac[ct])
            P.w("pe", s_dv, gwt_rdy)
            gate_mm = mm_group(lambda: b3[:, 0:8], pairs)
            gsc = gtmp[:, 0:8]
            eq1 = gtmp[:, 8:16]
            scm = gtmp[:, 16:24]
            eq2 = gtmp[:, 24:32]
            dve_op(lambda v: v.tensor_copy(gsc, b3[:, 0:8]), (s_pe, gate_mm), edge=True)
            dve_op(lambda v: v.reduce_max(col(M_), gsc, axis=AX.X), edge=True)
            dve_op(lambda v: v.tensor_scalar(eq1, gsc, col(M_), None, op0=ALU.is_equal), edge=True)
            dve_op(lambda v: v.scalar_tensor_tensor(scm, eq1, -1e30, gsc, op0=ALU.mult, op1=ALU.add), edge=True)
            dve_op(lambda v: v.reduce_max(col(T1_), scm, axis=AX.X), edge=True)
            dve_op(lambda v: v.tensor_scalar(eq2, scm, col(T1_), None, op0=ALU.is_equal), edge=True)
            d_done = dve_op(lambda v: v.tensor_tensor(col(D_), col(T1_), col(M_), op=ALU.subtract), edge=True)
            P.w("act", s_dv, d_done)
            P.em("act", lambda a: a.activation(col(ED), col(D_), AF.Exp).then_inc(s_ac.h, 1))
            ed_done = ac_inc()
            dve_op(lambda v: v.tensor_scalar_add(col(DEN0), col(ED), 1.0), (s_ac, ed_done), edge=True)
            dve_op(lambda v: v.reciprocal(col(S1_), col(DEN0)), edge=True)
            dve_op(lambda v: v.tensor_tensor(col(S2_), col(ED), col(S1_), op=ALU.mult), edge=True)
            combv = combs[:, ct * E:(ct + 1) * E]
            dve_op(lambda v, cv=combv: v.tensor_scalar(cv, eq1, col(S1_), None, op0=ALU.mult), edge=True)
            return dve_op(lambda v, cv=combv: v.scalar_tensor_tensor(
                cv, eq2, col(S2_), cv, op0=ALU.mult, op1=ALU.add), edge=True)

        def kv_part(ct):
            nonlocal last_dve, part_mm, part_add, kv_war
            kv_banks = [b0, b1]
            for sl in range(4):
                bank = kv_banks[sl % 2]
                pairs = []
                for kh in range(NKH):
                    pairs.append((
                        (lambda kh=kh, ct=ct: xn_fm[:, kh * TOK + ct * 128:kh * TOK + ct * 128 + 128]),
                        (lambda kh=kh, sl=sl: wqTA(kh, 1024 + sl * 512, 512)),
                    ))
                P.w("pe", d_wq, wq_kv_rdy)
                P.w("pe", s_dv, kv_bank_free[sl % 2])
                kv_mm = mm_group(lambda b=bank: b[:], pairs)
                if sl < 2:
                    tpar = sl % 2
                    tmn = tmin[:, tpar * 512:(tpar + 1) * 512]
                    tex = texp[:, tpar * 512:(tpar + 1) * 512]
                    mn_done = dve_op(lambda v, b=bank, t=tmn: v.tensor_scalar_min(t, b[:], 0.0),
                                     (s_pe, kv_mm), (s_ac, exp_free[tpar]))
                    P.w("act", s_dv, mn_done)
                    def mke(a, tmn=tmn, tex=tex):
                        a.activation(tex, tmn, AF.Exp).then_inc(s_ac.h, 1)
                    P.em("act", mke)
                    exp_done = ac_inc()
                    exp_free[tpar] = exp_done
                    kslice = kp_tm[:, sl * 512:(sl + 1) * 512]
                    kv_bank_free[sl % 2] = dve_op(
                        lambda v, b=bank, tex=tex, ks=kslice: v.scalar_tensor_tensor(
                            ks, b[:], 0.0, tex, op0=ALU.max, op1=ALU.add),
                        (s_ac, exp_done), (s_pe, kv_war), edge=True)
                else:
                    vslice = v_tm[:, (sl - 2) * 512:(sl - 1) * 512]
                    kv_bank_free[sl % 2] = dve_op(lambda v, b=bank, vs=vslice: v.tensor_copy(vs, b[:]),
                                                  (s_pe, kv_mm), (s_pe, kv_war))
            kv_done = last_dve
            for hb in range(4):
                P.w("pe", s_dv, kv_done)
                if part_mm > 0:
                    P.w("pe", s_dv, part_add)
                for i in range(4):
                    h = hb * 4 + i
                    def mk1(t, h=h, i=i):
                        t.matmul(b3[0:64, 64 + i * 65:128 + i * 65], kp_tm[:, h * 64:(h + 1) * 64],
                                 v_tm[:, h * 64:(h + 1) * 64], start=True, stop=True)
                    P.em("pe", mk1)
                    def mk2(t, h=h, i=i, last=(i == 3)):
                        ins = t.matmul(b3[0:64, 128 + i * 65:129 + i * 65], kp_tm[:, h * 64:(h + 1) * 64],
                                       onesb[:], start=True, stop=True)
                        if last:
                            ins.then_inc(s_pe.h, 1)
                    P.em("pe", mk2)
                part_mm = pe_inc()
                spv = spart[:, hb * 4 * 65:(hb + 1) * 4 * 65]
                part_add = dve_op(lambda v, spv=spv: v.tensor_tensor(spv, spv, b3[0:64, 64:64 + 260], op=ALU.add),
                                  (s_pe, part_mm), (s_dv, spart_zero))
            kv_war = part_mm

        part_add = 0
        comb_done = ln_gate(0)
        comb_done = ln_gate(1)
        for ct in range(NCT):
            kv_part(ct)
            if ct + 2 < NCT:
                comb_done = ln_gate(ct + 2)
        p1_pe_end = s_pe.n
        partials_done = last_dve

        P.w("sync", s_dv, partials_done)
        P.em("sync", lambda s: s.dma_start(out=cc_in[:], in_=spart[:]).then_inc(d_cs.h, 16)); d_cs.n += 16
        P.w("gp", d_cs, d_cs.n)
        P.em("gp", lambda g: g.collective_compute(
            "AllGather", ALU.bypass, replica_groups=[[0, 1, 2, 3, 4, 5, 6, 7]],
            ins=[cc_in[:]], outs=[cc_out[:]]).then_inc(s_cc.h, 1)); s_cc.n += 1
        P.w("gp", s_cc, s_cc.n)
        P.w("dve", d_ct, consts_rdy)
        for r in range(8):
            def mkg(g, r=r):
                g.dma_start(out=sgath[:], in_=cc_out[r * 64:(r + 1) * 64, :]).then_inc(d_ms.h, 16)
            P.em("gp", mkg)
            d_ms.n += 16
            gath_rdy = d_ms.n
            acc_done = dve_op(lambda v, r=r: v.scalar_tensor_tensor(
                saug[:], sgath[:], wselb[:, r:r + 1], saug[:],
                op0=ALU.mult, op1=ALU.add), (d_ms, gath_rdy), edge=True)
            if r < 7:
                P.w("gp", s_dv, acc_done)
        saug_bf_rdy = dve_op(lambda v: v.tensor_copy(saugb[:], saug[:]), edge=True)

        ffn_state = {
            "sil_par": 0,
            "g_pe_done": [0, 0],
            "y_evac": [0, 0],
            "g_hist": [0, 0],
            "sil_free": [0, 0],
            "bH_free": [0, 0],
            "bU_free": [0, 0],
            "fjg": 0,
            "last_g_pe": 0,
            "w13_loaded": {},
            "w2_loaded": {},
            "span_pe_done": {},
        }

        def load_w13(k, wait_pe=0):
            e_, sp_ = divmod(k, NSP)
            slot = k % 2
            dw = d_w[slot]
            if wait_pe:
                P.w("gp", s_pe, wait_pe)
            def mk(g, e_=e_, sp_=sp_, slot=slot, dw=dw):
                src1 = w1[e_].rearrange("(kh p) f -> p kh f", p=128)[:, :, sp_ * SPAN:(sp_ + 1) * SPAN]
                dst1 = wreg[:, slot * NKH * SPAN:(slot + 1) * NKH * SPAN].rearrange("p (kh f) -> p kh f", f=SPAN)
                g.dma_start(out=dst1, in_=src1).then_inc(dw.h, 16)
                src3 = w3[e_].rearrange("(kh p) f -> p kh f", p=128)[:, :, sp_ * SPAN:(sp_ + 1) * SPAN]
                dst3 = wreg[:, 8192 + slot * NKH * SPAN:8192 + (slot + 1) * NKH * SPAN].rearrange(
                    "p (kh f) -> p kh f", f=SPAN)
                g.dma_start(out=dst3, in_=src3).then_inc(dw.h, 16)
            P.em("gp", mk)
            dw.n += 32
            ffn_state["w13_loaded"][k] = dw.n

        def load_w2s(k, wait_pe=0):
            e_, sp_ = divmod(k, NSP)
            slot = k % 2
            dw = d_w2[slot]
            if wait_pe:
                P.w("gp", s_pe, wait_pe)
            def mk2(g, e_=e_, sp_=sp_, slot=slot, dw=dw):
                src2 = w2[e_].rearrange("(kf p) n -> p kf n", p=128)[:, sp_ * NFJ:(sp_ + 1) * NFJ, :]
                dst2 = wreg[:, 16384 + slot * NFJ * DM:16384 + (slot + 1) * NFJ * DM].rearrange(
                    "p (kf n) -> p kf n", n=DM)
                g.dma_start(out=dst2, in_=src2).then_inc(dw.h, 16)
            P.em("gp", mk2)
            dw.n += 16
            ffn_state["w2_loaded"][k] = dw.n

        def ffn_chunk(tc, pe_gate, pre_dv=(), pre_ac=()):
            st = ffn_state
            if tc == 1:
                P.w("sync", d_st, 16)
            xsrc = xs.rearrange("(tt p) h -> p tt h", p=128)[:, tc * NTT:(tc + 1) * NTT, :]
            P.em("sync", lambda s, xsrc=xsrc: s.dma_start(
                out=y_acc[:].rearrange("p (tt h) -> p tt h", h=DM), in_=xsrc).then_inc(d_yx.h, 16))
            d_yx.n += 16
            yx_rdy = d_yx.n

            st["w13_loaded"] = {}
            st["w2_loaded"] = {}
            st["span_pe_done"] = {}
            if tc == 0:
                load_w13(0)
                load_w13(1)
                load_w2s(0, wait_pe=pe_gate)
                load_w2s(1)
            else:
                load_w13(0, wait_pe=pe_gate)
                load_w13(1)
                load_w2s(0)
                load_w2s(1)

            bH = [b0, b4]
            bU = [b1, b5]
            for k in range(NSPANS):
                e_, sp_ = divmod(k, NSP)
                slot = k % 2
                half = k % 2
                P.w("pe", d_w[slot], st["w13_loaded"][k])
                P.w("pe", s_dv, st["y_evac"][1])
                if k == 0:
                    P.w("pe", s_dv, xn_evac[NTT * (tc + 1) - 1])
                    for v_ in pre_dv:
                        P.w("pe", s_dv, v_)
                    for v_ in pre_ac:
                        P.w("pe", s_ac, v_)
                for tg in range(2):
                    gpar = tg % 2
                    tof = tc * CTOK + tg * 512
                    for fj in range(NFJ):
                        spar = st["sil_par"]
                        bp = st["fjg"] % 2
                        P.w("pe", s_dv, st["g_hist"][bp])
                        pairs = [((lambda kh=kh, slot=slot, fj=fj: w1b(slot, kh, fj * 128, 128)),
                                  (lambda kh=kh, tof=tof: xn_fm[:, kh * TOK + tof:kh * TOK + tof + 512]))
                                 for kh in range(NKH)]
                        h_done = mm_group(lambda bp=bp: bH[bp][:], pairs)
                        P.w("act", s_pe, h_done)
                        P.w("act", s_dv, st["sil_free"][spar])
                        def mks(a, spar=spar, bp=bp):
                            a.activation(sil[:, spar * 512:(spar + 1) * 512], bH[bp][:], AF.Silu).then_inc(s_ac.h, 1)
                        P.em("act", mks)
                        st["bH_free"][bp] = ac_inc()
                        sil_done = s_ac.n
                        pairs = [((lambda kh=kh, slot=slot, fj=fj: w3b(slot, kh, fj * 128, 128)),
                                  (lambda kh=kh, tof=tof: xn_fm[:, kh * TOK + tof:kh * TOK + tof + 512]))
                                 for kh in range(NKH)]
                        u_done = mm_group(lambda bp=bp: bU[bp][:], pairs)
                        gd = gbuf[:, (gpar * 8 + half * NFJ + fj) * 512:(gpar * 8 + half * NFJ + fj) * 512 + 512]
                        g_done = dve_op(
                            lambda v, gd=gd, spar=spar, bp=bp: v.tensor_tensor(
                                gd, sil[:, spar * 512:(spar + 1) * 512], bU[bp][:], op=ALU.mult),
                            (s_ac, sil_done), (s_pe, u_done), (s_pe, st["g_pe_done"][gpar]))
                        st["g_hist"][bp] = g_done
                        st["bU_free"][bp] = g_done
                        st["sil_free"][spar] = g_done
                        st["sil_par"] ^= 1
                        st["fjg"] += 1
                if k + 2 < NSPANS:
                    load_w13(k + 2, wait_pe=s_pe.n)
                if k % 2 == 0:
                    continue
                g_rdy = s_dv.n
                P.w("pe", d_w2[0], st["w2_loaded"][k - 1])
                P.w("pe", d_w2[1], st["w2_loaded"][k])
                P.w("pe", s_dv, g_rdy)
                for tg in range(2):
                    gpar = tg % 2
                    for tt in range(4):
                        ttg = tg * 4 + tt
                        ct_g = tc * NTT + ttg
                        ypar = ttg % 2
                        ybuf = py if ypar == 0 else py2
                        P.w("pe", s_dv, st["y_evac"][ypar])
                        for ht in range(2):
                            pairs = []
                            for hf in range(2):
                                for fj in range(NFJ):
                                    pairs.append((
                                        (lambda fj=fj, gpar=gpar, tt=tt, hf=hf: gbuf[
                                            :, (gpar * 8 + hf * NFJ + fj) * 512 + tt * 128:
                                               (gpar * 8 + hf * NFJ + fj) * 512 + tt * 128 + 128]),
                                        (lambda fj=fj, hf=hf, ht=ht: w2b(hf, fj, ht * 512, 512)),
                                    ))
                            ymm = mm_group(lambda ht=ht, ybuf=ybuf: ybuf[:, ht * 512:(ht + 1) * 512], pairs)
                        dst = y_acc[:, ttg * DM:ttg * DM + DM]
                        cslice = combs[:, ct_g * E + e_:ct_g * E + e_ + 1]
                        st["y_evac"][ypar] = dve_op(
                            lambda v, dst=dst, cs=cslice, ybuf=ybuf: v.scalar_tensor_tensor(
                                dst, ybuf[:], cs, dst, op0=ALU.mult, op1=ALU.add),
                            (s_pe, ymm), (d_yx, yx_rdy), (s_dv, comb_done))
                    st["g_pe_done"][gpar] = s_pe.n
                g_end = s_pe.n
                st["span_pe_done"][k - 1] = g_end
                st["span_pe_done"][k] = g_end
                if k + 2 < NSPANS:
                    load_w2s(k + 1, wait_pe=g_end)
                    load_w2s(k + 2)
            return s_pe.n
            return s_pe.n

        scan_state = {
            "msc_free": [0, 0],
            "rec_free": [0, 0],
            "b2_free": 0,
            "b4_free": 0,
            "b5_free": 0,
            "b3_free": 0,
            "qkv_bank_free": [0, 0],
            "exp_free": [0, 0],
            "sc_free": [0, 0],
            "nd_free": [0, 0],
            "saug_rdy_pair": [0] * (NH // 2),
            "qp_war": 0,
            "kv_war": 0,
        }

        def scan_chunk(tc, wq_rdy, pre_ac=0, pre_dv=0):
            nonlocal b6_evac, saug_bf_rdy
            ss = scan_state
            proj_evac = 0
            for ctl in range(8):
                ct = tc * 8 + ctl
                for sl in range(2):
                    bank = [b0, b1][sl % 2]
                    pairs = [((lambda kh=kh, ct=ct: xn_fm[:, kh * TOK + ct * 128:kh * TOK + ct * 128 + 128]),
                              (lambda kh=kh, sl=sl: wqT(kh, sl * 512, 512)))
                             for kh in range(NKH)]
                    P.w("pe", d_wq, wq_rdy)
                    P.w("pe", s_dv, ss["qkv_bank_free"][sl % 2])
                    if ctl == 0 and sl == 0:
                        P.w("pe", s_ac, pre_ac)
                        P.w("pe", s_dv, pre_dv)
                    q_mm = mm_group(lambda b=bank: b[:], pairs)
                    tpar = sl % 2
                    tmn = tmin[:, tpar * 512:(tpar + 1) * 512]
                    tex = texp[:, tpar * 512:(tpar + 1) * 512]
                    mn_done = dve_op(lambda v, b=bank, t=tmn: v.tensor_scalar_min(t, b[:], 0.0),
                                     (s_pe, q_mm), (s_ac, ss["exp_free"][tpar]))
                    P.w("act", s_dv, mn_done)
                    def mke(a, tmn=tmn, tex=tex):
                        a.activation(tex, tmn, AF.Exp).then_inc(s_ac.h, 1)
                    P.em("act", mke)
                    exp_done = ac_inc()
                    ss["exp_free"][tpar] = exp_done
                    qslice = qp_tm[:, sl * 512:(sl + 1) * 512]
                    ss["qkv_bank_free"][sl % 2] = dve_op(
                        lambda v, b=bank, tex=tex, qs=qslice: v.scalar_tensor_tensor(
                            qs, b[:], 0.0, tex, op0=ALU.max, op1=ALU.add),
                        (s_ac, exp_done), (s_pe, ss["qp_war"]), edge=True)
                for sl in range(4):
                    bank = [b0, b1][sl % 2]
                    pairs = [((lambda kh=kh, ct=ct: xn_fm[:, kh * TOK + ct * 128:kh * TOK + ct * 128 + 128]),
                              (lambda kh=kh, sl=sl: wqT(kh, 1024 + sl * 512, 512)))
                             for kh in range(NKH)]
                    P.w("pe", s_dv, ss["qkv_bank_free"][sl % 2])
                    kv_mm = mm_group(lambda b=bank: b[:], pairs)
                    if sl < 2:
                        tpar = sl % 2
                        tmn = tmin[:, tpar * 512:(tpar + 1) * 512]
                        tex = texp[:, tpar * 512:(tpar + 1) * 512]
                        mn_done = dve_op(lambda v, b=bank, t=tmn: v.tensor_scalar_min(t, b[:], 0.0),
                                         (s_pe, kv_mm), (s_ac, ss["exp_free"][tpar]))
                        P.w("act", s_dv, mn_done)
                        def mke(a, tmn=tmn, tex=tex):
                            a.activation(tex, tmn, AF.Exp).then_inc(s_ac.h, 1)
                        P.em("act", mke)
                        exp_done = ac_inc()
                        ss["exp_free"][tpar] = exp_done
                        kslice = kp_tm[:, sl * 512:(sl + 1) * 512]
                        ss["qkv_bank_free"][sl % 2] = dve_op(
                            lambda v, b=bank, tex=tex, ks=kslice: v.scalar_tensor_tensor(
                                ks, b[:], 0.0, tex, op0=ALU.max, op1=ALU.add),
                            (s_ac, exp_done), (s_pe, ss["kv_war"]), edge=True)
                    else:
                        vslice = v_tm[:, (sl - 2) * 512:(sl - 1) * 512]
                        ss["qkv_bank_free"][sl % 2] = dve_op(
                            lambda v, b=bank, vs=vslice: v.tensor_copy(vs, b[:]), (s_pe, kv_mm), (s_pe, ss["kv_war"]))
                qkv_rdy = last_dve

                for (srct, dstv) in ((qp_tm, qp_fm), (kp_tm, kp_fm)):
                    for half in range(2):
                        P.w("pe", s_dv, qkv_rdy)
                        P.w("pe", s_dv, b6_evac)
                        for hh in range(8):
                            h = half * 8 + hh
                            def mk(t, srct=srct, h=h, hh=hh, sp=(hh == 7)):
                                ins = t.transpose(b6[0:64, hh * 128:(hh + 1) * 128],
                                                  srct[:, h * 64:(h + 1) * 64], identb[:])
                                if sp:
                                    ins.then_inc(s_pe.h, 1)
                            P.em("pe", mk)
                        trd = pe_inc()
                        b6_evac = dve_op(lambda v, dstv=dstv, half=half: v.tensor_copy(
                            dstv[:, half * 1024:(half + 1) * 1024], b6[0:64, 0:1024]), (s_pe, trd))
                fm_rdy = b6_evac
                ss["qp_war"] = s_pe.n

                sc_banks = [py[:, 0:256], b7[:, 0:256]]
                nd_banks = [py2[:, 0:128], b0[:, 0:128]]

                def emit_sc(p):
                    bankap = sc_banks[p % 2]
                    P.w("pe", s_dv, ss["sc_free"][p % 2])
                    for i in range(2):
                        h = 2 * p + i
                        def mk(t, h=h, i=i, bankap=bankap):
                            ins = t.matmul(bankap[:, i * 128:(i + 1) * 128],
                                           kp_fm[:, h * 128:(h + 1) * 128],
                                           qp_fm[:, h * 128:(h + 1) * 128], start=True, stop=True)
                            if i == 1:
                                ins.then_inc(s_pe.h, 1)
                        P.em("pe", mk)
                    return pe_inc()

                sc_mm = {}
                sc_mm[0] = emit_sc(0)
                sc_mm[1] = emit_sc(1)
                for p in range(NH // 2):
                    ppar = p % 2
                    bankap = sc_banks[ppar]
                    mslice = msc[:, ppar * 256:(ppar + 1) * 256] if False else msc[:, 0:256] if ppar == 0 else msc[:, 256:512]
                    P.w("dve", s_pe, ss["msc_free"][ppar])
                    mask_done = dve_op(
                        lambda v, ms=mslice, bankap=bankap: v.tensor_tensor(
                            ms.rearrange("p (i c) -> p i c", c=128), bankap[:].rearrange("p (i c) -> p i c", c=128),
                            maskb[:].rearrange("p c -> p 1 c").broadcast(1, 2) if False else bankap[:].rearrange("p (i c) -> p i c", c=128), op=ALU.mult),
                        (s_pe, sc_mm[p]), (d_ct, consts_rdy)) if False else None
                    m0 = msc[:, ppar * 256:ppar * 256 + 128]
                    m1 = msc[:, ppar * 256 + 128:ppar * 256 + 256]
                    dve_op(lambda v, m0=m0, bankap=bankap: v.tensor_tensor(m0, bankap[:, 0:128], maskb[:], op=ALU.mult),
                           (s_pe, sc_mm[p]), (d_ct, consts_rdy))
                    mask_done = dve_op(lambda v, m1=m1, bankap=bankap: v.tensor_tensor(m1, bankap[:, 128:256], maskb[:], op=ALU.mult))
                    ss["b2_free"] = mask_done
                    ss["sc_free"][ppar] = mask_done
                    if p + 2 < NH // 2:
                        sc_mm[p + 2] = emit_sc(p + 2)
                    ndb = nd_banks[ppar]
                    P.w("pe", s_dv, mask_done)
                    P.w("pe", s_ac, ss["nd_free"][ppar])
                    P.w("pe", s_dv, ss["saug_rdy_pair"][p])
                    for i in range(2):
                        h = 2 * p + i
                        def mkn1(t, h=h, i=i, ppar=ppar, ndb=ndb):
                            t.matmul(ndb[:, i * 64:(i + 1) * 64],
                                     msc[:, ppar * 256 + i * 128:ppar * 256 + (i + 1) * 128],
                                     v_tm[:, h * 64:(h + 1) * 64], start=True, stop=False)
                        P.em("pe", mkn1)
                        def mkn2(t, h=h, i=i, ndb=ndb, last=(i == 1)):
                            ins = t.matmul(ndb[:, i * 64:(i + 1) * 64], qp_fm[:, h * 128:(h + 1) * 128],
                                           saugb[:, h * 65:h * 65 + 64], start=False, stop=True)
                            if last:
                                ins.then_inc(s_pe.h, 1)
                        P.em("pe", mkn2)
                    num_mm = pe_inc()
                    P.w("pe", s_dv, ss["b5_free"])
                    for i in range(2):
                        h = 2 * p + i
                        def mkd1(t, i=i, ppar=ppar):
                            t.matmul(b5[:, i:i + 1], msc[:, ppar * 256 + i * 128:ppar * 256 + (i + 1) * 128],
                                     onesb[:], start=True, stop=False)
                        P.em("pe", mkd1)
                        def mkd2(t, h=h, i=i, last=(i == 1)):
                            ins = t.matmul(b5[:, i:i + 1], qp_fm[:, h * 128:(h + 1) * 128],
                                           saugb[:, h * 65 + 64:h * 65 + 65], start=False, stop=True)
                            if last:
                                ins.then_inc(s_pe.h, 1)
                        P.em("pe", mkd2)
                    den_mm = pe_inc()
                    ss["msc_free"][ppar] = den_mm
                    P.w("pe", s_dv, ss["b3_free"])
                    for i in range(2):
                        h = 2 * p + i
                        def mks1(t, h=h, i=i):
                            t.matmul(py[0:64, 512 + 64 + i * 65:512 + 128 + i * 65],
                                     kp_tm[:, h * 64:(h + 1) * 64], v_tm[:, h * 64:(h + 1) * 64],
                                     start=True, stop=True)
                        P.em("pe", mks1)
                        def mks2(t, h=h, i=i, last=(i == 1)):
                            ins = t.matmul(py[0:64, 512 + 128 + i * 65:512 + 129 + i * 65],
                                           kp_tm[:, h * 64:(h + 1) * 64], onesb[:], start=True, stop=True)
                            if last:
                                ins.then_inc(s_pe.h, 1)
                        P.em("pe", mks2)
                    ds_mm = pe_inc()
                    rc = 21 + 2 * ppar
                    dn = 17 + 2 * ppar
                    P.w("dve", s_ac, ss["rec_free"][ppar])
                    dve_op(lambda v, dn=dn: v.tensor_scalar_add(lns[:, dn:dn + 2], b5[:, 0:2], EPS_DEN),
                           (s_pe, den_mm))
                    ss["b5_free"] = last_dve
                    rec_done = dve_op(lambda v, rc=rc, dn=dn: v.reciprocal(lns[:, rc:rc + 2], lns[:, dn:dn + 2]),
                                      edge=True)
                    for i in range(2):
                        h = 2 * p + i
                        P.w("act", s_pe, num_mm)
                        P.w("act", s_dv, rec_done)
                        def mky(a, h=h, i=i, rc=rc, ndb=ndb):
                            a.activation(y_tm[:, h * 64:(h + 1) * 64], ndb[:, i * 64:(i + 1) * 64], AF.Identity,
                                         scale=lns[:, rc + i:rc + i + 1]).then_inc(s_ac.h, 1)
                        P.em("act", mky)
                        ac_inc()
                    ss["nd_free"][ppar] = s_ac.n
                    ss["rec_free"][ppar] = s_ac.n
                    sgv = saug[:, 2 * p * 65:(2 * p + 2) * 65]
                    dve_op(lambda v, sgv=sgv: v.tensor_tensor(sgv, sgv, py[0:64, 512 + 64:512 + 194], op=ALU.add),
                           (s_pe, ds_mm), (s_dv, ss["saug_rdy_pair"][p]))
                    ss["b3_free"] = last_dve
                    sgbv = saugb[:, 2 * p * 65:(2 * p + 2) * 65]
                    ss["saug_rdy_pair"][p] = dve_op(lambda v, sgv=sgv, sgbv=sgbv: v.tensor_copy(sgbv, sgv), edge=True)
                    ss["kv_war"] = ds_mm
                y_all = s_ac.n

                P.w("pe", s_ac, y_all)
                P.w("pe", s_dv, b6_evac)
                for j in range(NKH):
                    def mk(t, j=j, sp=(j == NKH - 1)):
                        ins = t.transpose(b6[:, j * 128:(j + 1) * 128], y_tm[:, j * 128:(j + 1) * 128], identb[:])
                        if sp:
                            ins.then_inc(s_pe.h, 1)
                    P.em("pe", mk)
                ytr = pe_inc()
                b6_evac = dve_op(lambda v: v.tensor_copy(y_fm[:], b6[:, 0:1024]), (s_pe, ytr))
                yfm_rdy = b6_evac
                for cs in range(2):
                    bank = b7 if cs == 0 else b2
                    bid = "b7" if cs == 0 else "b2"
                    pairs = [((lambda j=j: y_fm[:, j * 128:(j + 1) * 128]),
                              (lambda j=j, cs=cs: wpT(j, cs * 512, 512)))
                             for j in range(NKH)]
                    P.w("pe", s_dv, yfm_rdy)
                    if cs == 1:
                        P.w("pe", s_dv, ss["b2_free"])
                        P.w("pe", s_dv, proj_evac)
                    else:
                        P.w("pe", s_dv, proj_evac)
                    pj_mm = mm_group(lambda b=bank: b[:], pairs)
                    dst = y_acc[:, ctl * DM + cs * 512:ctl * DM + cs * 512 + 512]
                    proj_evac = dve_op(lambda v, dst=dst, b=bank: v.scalar_tensor_tensor(
                        dst, b[:], 1.0, dst, op0=ALU.mult, op1=ALU.add), (s_pe, pj_mm))
                    if cs == 1:
                        ss["b2_free"] = proj_evac
            return proj_evac

        ffn0_pe_end = ffn_chunk(0, pe_gate=p1_pe_end, pre_dv=(kv_bank_free[0], kv_bank_free[1]))
        P.w("sync", s_pe, ffn0_pe_end)
        for kh in range(NKH):
            def mk(s, kh=kh):
                s.dma_start(out=wqT(kh, 0, 3072), in_=wattn_bf[:, kh * 128:(kh + 1) * 128],
                            transpose=True).then_inc(d_wq.h, 16)
            P.em("sync", mk)
            d_wq.n += 16
        for kh in range(NKH):
            def mk(s, kh=kh):
                s.dma_start(out=wpT(kh, 0, 1024), in_=wp_bf[:, kh * 128:(kh + 1) * 128],
                            transpose=True).then_inc(d_wq.h, 16)
            P.em("sync", mk)
            d_wq.n += 16
        wq_full0 = d_wq.n
        scan_state["kv_war"] = p1_pe_end
        scan_state["saug_rdy_pair"] = [saug_bf_rdy] * (NH // 2)
        scan_state["b5_free"] = max(max(ffn_state["bU_free"]), max(ffn_state["y_evac"]))
        scan0_end = scan_chunk(0, wq_full0, pre_ac=max(ffn_state["bH_free"]),
                               pre_dv=max(max(ffn_state["bU_free"]), max(ffn_state["y_evac"])))
        P.w("sync", s_dv, scan0_end)
        yo0 = yout.rearrange("(tt p) h -> p tt h", p=128)[:, 0:NTT, :]
        P.em("sync", lambda s, yo0=yo0: s.dma_start(
            out=yo0, in_=y_acc[:].rearrange("p (tt h) -> p tt h", h=DM)).then_inc(d_st.h, 16))
        d_st.n += 16

        scan0_pe_end = s_pe.n
        ffn1_pe_end = ffn_chunk(1, pe_gate=scan0_pe_end,
                                pre_dv=(scan_state["qkv_bank_free"][0], scan_state["qkv_bank_free"][1],
                                        scan_state["b5_free"], scan_state["b3_free"], scan_state["b2_free"]),
                                pre_ac=(scan_state["nd_free"][0], scan_state["nd_free"][1],
                                        scan_state["rec_free"][0], scan_state["rec_free"][1]))
        P.w("sync", s_pe, ffn1_pe_end)
        for kh in range(NKH):
            def mk(s, kh=kh):
                s.dma_start(out=wqT(kh, 0, 3072), in_=wattn_bf[:, kh * 128:(kh + 1) * 128],
                            transpose=True).then_inc(d_wq.h, 16)
            P.em("sync", mk)
            d_wq.n += 16
        for kh in range(NKH):
            def mk(s, kh=kh):
                s.dma_start(out=wpT(kh, 0, 1024), in_=wp_bf[:, kh * 128:(kh + 1) * 128],
                            transpose=True).then_inc(d_wq.h, 16)
            P.em("sync", mk)
            d_wq.n += 16
        wq_full1 = d_wq.n
        scan_state["b5_free"] = max(scan_state["b5_free"], max(ffn_state["bU_free"]), max(ffn_state["y_evac"]))
        scan1_end = scan_chunk(1, wq_full1, pre_ac=max(ffn_state["bH_free"]),
                               pre_dv=max(max(ffn_state["bU_free"]), max(ffn_state["y_evac"])))
        P.w("sync", s_dv, scan1_end)
        yo1 = yout.rearrange("(tt p) h -> p tt h", p=128)[:, NTT:2 * NTT, :]
        P.em("sync", lambda s, yo1=yo1: s.dma_start(
            out=yo1, in_=y_acc[:].rearrange("p (tt h) -> p tt h", h=DM)).then_inc(d_st.h, 16))
        d_st.n += 16
        P.w("sync", d_st, d_st.n)

        with nc.Block() as block:
            @block.tensor
            def _(t):
                for f in P.ops["pe"]:
                    f(t)

            @block.scalar
            def _(a):
                for f in P.ops["act"]:
                    f(a)

            @block.vector
            def _(v):
                for f in P.ops["dve"]:
                    f(v)

            @block.sync
            def _(s):
                for f in P.ops["sync"]:
                    f(s)

            @block.gpsimd
            def _(g):
                for f in P.ops["gp"]:
                    f(g)

    return nc


_NC_CACHE = {}


def kernel(L0_attn_x, L0_attn_w_attn, L0_attn_w_proj, L0_ffn_gate_w, L0_ffn_w1, L0_ffn_w3, L0_ffn_w2):
    x = np.asarray(L0_attn_x)
    w_attn = np.ascontiguousarray(np.asarray(L0_attn_w_attn), dtype=np.float32)
    w_proj = np.ascontiguousarray(np.asarray(L0_attn_w_proj), dtype=np.float32)
    gate_w = np.ascontiguousarray(np.asarray(L0_ffn_gate_w), dtype=np.float32)
    w1 = np.ascontiguousarray(np.asarray(L0_ffn_w1), dtype=np.float32)
    w3 = np.ascontiguousarray(np.asarray(L0_ffn_w3), dtype=np.float32)
    w2 = np.ascontiguousarray(np.asarray(L0_ffn_w2), dtype=np.float32)

    ident = np.eye(128, dtype=ml_dtypes.bfloat16)
    maskT = np.triu(np.ones((128, 128))).astype(ml_dtypes.bfloat16)
    onesc = np.ones((128, 1), dtype=ml_dtypes.bfloat16)

    if "nc" not in _NC_CACHE:
        _NC_CACHE["nc"] = build_nc()
    nc = _NC_CACHE["nc"]

    in_maps = []
    for c in range(8):
        b, seg = divmod(c, 4)
        xs = np.ascontiguousarray(x[b, seg * TOK:(seg + 1) * TOK, :], dtype=np.float32)
        wsel = np.zeros((64, 8), np.float32)
        for r in range(8):
            rb, rseg = divmod(r, 4)
            if rb == b and rseg < seg:
                wsel[:, r] = 1.0
        in_maps.append({
            "xs": xs, "w_attn": w_attn, "w_proj": w_proj, "gate_w": gate_w,
            "w1": w1, "w3": w3, "w2": w2,
            "ident": ident, "maskT": maskT, "onesc": onesc, "wsel": wsel,
        })

    res = run_bass_kernel_spmd(nc, in_maps, core_ids=list(range(8)))
    out = np.empty((2, 8192, 1024), np.float32)
    for c in range(8):
        b, seg = divmod(c, 4)
        out[b, seg * TOK:(seg + 1) * TOK, :] = np.asarray(res.results[c]["yout"])
    return out


# version 10
# speedup vs baseline: 1.4280x; 1.0085x over previous
import contextlib
import numpy as np
import ml_dtypes
import concourse.bass as bass
import concourse.mybir as mybir
from concourse.bass_utils import run_bass_kernel_spmd

F32 = mybir.dt.float32
BF16 = mybir.dt.bfloat16
AF = mybir.ActivationFunctionType
ALU = mybir.AluOpType
AX = mybir.AxisListType

TOK = 2048
NCT = 16
DM = 1024
NKH = 8
NH = 16
HD = 64
E = 8
FF = 4096
SPAN = 512
NSP = FF // SPAN
NFJ = SPAN // 128
NSPANS = E * NSP
NCH = 2
CTOK = 1024
NTT = 8
EPS_LN = 1e-5
EPS_DEN = 1e-6


class Sem:
    def __init__(self, es, nc, name):
        self.h = es.enter_context(nc.semaphore(name))
        self.n = 0


class B:

    def __init__(self):
        self.ops = {k: [] for k in ("pe", "act", "dve", "sync", "gp")}

    def em(self, eng, fn):
        self.ops[eng].append(fn)

    def w(self, eng, sem, val):
        if val > 0:
            self.ops[eng].append(lambda e, s=sem.h, v=val: e.wait_ge(s, v))


def build_nc():
    nc = bass.Bass("TRN2", target_bir_lowering=False)
    xs = nc.dram_tensor("xs", [TOK, DM], F32, kind="ExternalInput").ap()
    w_attn = nc.dram_tensor("w_attn", [3 * DM, DM], F32, kind="ExternalInput").ap()
    w_proj = nc.dram_tensor("w_proj", [DM, DM], F32, kind="ExternalInput").ap()
    gate_w = nc.dram_tensor("gate_w", [E, DM], F32, kind="ExternalInput").ap()
    w1 = nc.dram_tensor("w1", [E, DM, FF], F32, kind="ExternalInput").ap()
    w3 = nc.dram_tensor("w3", [E, DM, FF], F32, kind="ExternalInput").ap()
    w2 = nc.dram_tensor("w2", [E, FF, DM], F32, kind="ExternalInput").ap()
    ident = nc.dram_tensor("ident", [128, 128], BF16, kind="ExternalInput").ap()
    maskT = nc.dram_tensor("maskT", [128, 128], BF16, kind="ExternalInput").ap()
    onesc = nc.dram_tensor("onesc", [128, 1], BF16, kind="ExternalInput").ap()
    wsel = nc.dram_tensor("wsel", [64, 8], F32, kind="ExternalInput").ap()
    yout = nc.dram_tensor("yout", [TOK, DM], F32, kind="ExternalOutput").ap()
    wattn_bf = nc.dram_tensor("wattn_bf", [3 * DM, DM], BF16, kind="Internal").ap()
    w1bf = nc.dram_tensor("w1bf", [E, DM, FF], BF16, kind="Internal").ap()
    w3bf = nc.dram_tensor("w3bf", [E, DM, FF], BF16, kind="Internal").ap()
    w2bf = nc.dram_tensor("w2bf", [E, FF, DM], BF16, kind="Internal").ap()
    wp_bf = nc.dram_tensor("wp_bf", [DM, DM], BF16, kind="Internal").ap()
    cc_in = nc.dram_tensor("cc_in", [64, NH * 65], F32, kind="Internal").ap()
    cc_out = nc.dram_tensor("cc_out", [8 * 64, NH * 65], F32, kind="Internal").ap()

    es = contextlib.ExitStack()
    sb = lambda n, s, d: es.enter_context(nc.sbuf_tensor(n, s, d))
    ps = lambda n, s, d: es.enter_context(nc.psum_tensor(n, s, d))
    with es:
        xn_fm = sb("xn_fm", [128, NKH * TOK], BF16)
        wreg = sb("wreg", [128, 32768], BF16)
        y_acc = sb("y_acc", [128, NTT * DM], F32)
        combs = sb("combs", [128, NCT * E], F32)
        saug = sb("saug", [64, NH * 65], F32)
        saugb = sb("saugb", [64, NH * 65], BF16)
        spart = sb("spart", [64, NH * 65], F32)
        sgath = sb("sgath", [64, NH * 65], F32)
        wselb = sb("wselb", [64, 8], F32)
        xsb = sb("xsb", [128, 2 * DM], F32)
        xn_tm = sb("xn_tm", [128, DM], BF16)
        lns = sb("lns", [128, 32], F32)
        gtmp = sb("gtmp", [128, 32], F32)
        gw8 = sb("gw8", [16, DM], BF16)
        gwT = sb("gwT", [128, NKH * E], BF16)
        identb = sb("identb", [128, 128], BF16)
        maskb = sb("maskb", [128, 128], BF16)
        onesb = sb("onesb", [128, 1], BF16)
        kp_tm = sb("kp_tm", [128, DM], BF16)
        v_tm = sb("v_tm", [128, DM], BF16)
        qp_tm = sb("qp_tm", [128, DM], BF16)
        qp_fm = sb("qp_fm", [64, NH * 128], BF16)
        kp_fm = sb("kp_fm", [64, NH * 128], BF16)
        y_tm = sb("y_tm", [128, DM], BF16)
        y_fm = sb("y_fm", [128, DM], BF16)
        tmin = sb("tmin", [128, 2 * 512], F32)
        texp = sb("texp", [128, 2 * 512], F32)
        msc = sb("msc", [128, 512], BF16)
        sil = sb("sil", [128, 2 * 512], BF16)
        gbuf = sb("gbuf", [128, 2 * 8 * 512], BF16)

        def w1b(slot, kh, fcol0, n):
            off = slot * (NKH * SPAN) + kh * SPAN + fcol0
            return wreg[:, off:off + n]

        def w3b(slot, kh, fcol0, n):
            off = 8192 + slot * (NKH * SPAN) + kh * SPAN + fcol0
            return wreg[:, off:off + n]

        def w2b(slot, kf, hcol0, n):
            off = 16384 + slot * (NFJ * DM) + kf * DM + hcol0
            return wreg[:, off:off + n]

        def wqT(kh, c0, n):
            return wreg[:, kh * 3072 + c0:kh * 3072 + c0 + n]

        def wqTA(kh, c0, n):
            return wreg[:, 16384 + kh * 2048 + (c0 - 1024):16384 + kh * 2048 + (c0 - 1024) + n]

        def wpT(kh, c0, n):
            return wreg[:, 24576 + kh * DM + c0:24576 + kh * DM + c0 + n]

        b0 = ps("b0", [128, 512], F32)
        b1 = ps("b1", [128, 512], F32)
        py = ps("py", [128, 1024], F32)
        b2 = py[:, 0:512]
        b3 = py[:, 512:1024]
        py2 = ps("py2", [128, 1024], F32)
        b4 = py2[:, 0:512]
        b5 = py2[:, 512:1024]
        b6 = ps("b6", [128, 1024], BF16)
        b7 = ps("b7", [128, 512], F32)

        d_x = [Sem(es, nc, "d_x0"), Sem(es, nc, "d_x1")]
        d_yx = Sem(es, nc, "d_yx")
        d_w = [Sem(es, nc, "d_w0"), Sem(es, nc, "d_w1")]
        d_w2 = [Sem(es, nc, "d_w2a"), Sem(es, nc, "d_w2b")]
        d_wq = Sem(es, nc, "d_wq")
        d_ms = Sem(es, nc, "d_ms")
        d_cast = Sem(es, nc, "d_cast")
        d_ct = Sem(es, nc, "d_ct")
        d_cs = Sem(es, nc, "d_cs")
        s_cc = Sem(es, nc, "s_cc")
        s_pe = Sem(es, nc, "s_pe")
        s_ac = Sem(es, nc, "s_ac")
        s_dv = Sem(es, nc, "s_dv")
        d_st = Sem(es, nc, "d_st")

        P = B()

        def pe_inc():
            s_pe.n += 1
            return s_pe.n

        def ac_inc():
            s_ac.n += 1
            return s_ac.n

        def dv_inc():
            s_dv.n += 1
            return s_dv.n

        def mm_group(bank_ap_fn, pairs, extra_first_wait=None):
            n = len(pairs)
            for i, (lf, rf) in enumerate(pairs):
                def mk(t, lf=lf, rf=rf, st=(i == 0), sp=(i == n - 1), bf=bank_ap_fn):
                    ins = t.matmul(bf(), lf(), rf(), start=st, stop=sp)
                    if sp:
                        ins.then_inc(s_pe.h, 1)
                P.em("pe", mk)
            return pe_inc()

        P.em("sync", lambda s: s.dma_start(out=identb[:], in_=ident).then_inc(d_ct.h, 16)); d_ct.n += 16
        P.em("sync", lambda s: s.dma_start(out=maskb[:], in_=maskT).then_inc(d_ct.h, 16)); d_ct.n += 16
        P.em("sync", lambda s: s.dma_start(out=onesb[:], in_=onesc).then_inc(d_ct.h, 16)); d_ct.n += 16
        P.em("sync", lambda s: s.dma_start(out=wselb[:], in_=wsel).then_inc(d_ct.h, 16)); d_ct.n += 16
        consts_rdy = d_ct.n

        P.em("gp", lambda g: g.dma_start(out=wattn_bf[:], in_=w_attn).then_inc(d_ms.h, 16)); d_ms.n += 16
        P.em("gp", lambda g: g.dma_start(out=gw8[0:8, :], in_=gate_w).then_inc(d_ms.h, 16)); d_ms.n += 16
        P.em("gp", lambda g: g.dma_start(out=wp_bf[:], in_=w_proj).then_inc(d_ms.h, 16)); d_ms.n += 16
        p0_gp = d_ms.n

        P.em("dve", lambda v: v.memset(saug[:], 0.0).then_inc(s_dv.h, 1)); dv_inc()
        P.em("dve", lambda v: v.memset(spart[:], 0.0).then_inc(s_dv.h, 1)); spart_zero = dv_inc()

        P.w("pe", d_ms, p0_gp)
        P.w("pe", d_ct, consts_rdy)
        for kh in range(NKH):
            def mk(t, kh=kh, sp=(kh == NKH - 1)):
                ins = t.transpose(b6[:, kh * 8:kh * 8 + 8], gw8[0:8, kh * 128:(kh + 1) * 128], identb[0:8, 0:8])
                if sp:
                    ins.then_inc(s_pe.h, 1)
            P.em("pe", mk)
        gwt_tr = pe_inc()
        P.w("dve", s_pe, gwt_tr)
        P.em("dve", lambda v: v.tensor_copy(gwT[:], b6[:, 0:NKH * E]).then_inc(s_dv.h, 1))
        gwt_rdy = dv_inc()

        P.w("sync", d_ms, p0_gp)
        for kh in range(NKH):
            def mk(s, kh=kh):
                s.dma_start(out=wqTA(kh, 1024, 2048), in_=wattn_bf[1024:3072, kh * 128:(kh + 1) * 128],
                            transpose=True).then_inc(d_wq.h, 16)
            P.em("sync", mk)
            d_wq.n += 16
        wq_kv_rdy = d_wq.n

        SX, SSQ, M_, EX2, T1_, VAR, LNV, RSTD, NMR = range(9)
        D_, ED, DEN0, DEN1, REC0, REC1, S1_, S2_ = range(9, 17)

        def col(i):
            return lns[:, i:i + 1]

        last_dve = 0
        exp_free = [0, 0]
        xn_evac = [0] * NCT
        xnT_done = [0] * NCT
        xn_consumed_act = [0] * NCT
        b6_evac = 0
        kv_bank_free = [0, 0]
        part_mm = 0
        kv_war = 0

        def dve_op(fn, *waits, edge=False):
            nonlocal last_dve
            for sem, val in waits:
                P.w("dve", sem, val)
            if edge:
                P.w("dve", s_dv, last_dve)
            def mk(v, fn=fn):
                fn(v).then_inc(s_dv.h, 1)
            P.em("dve", mk)
            last_dve = dv_inc()
            return last_dve

        def ln_gate(ct):
            nonlocal last_dve, b6_evac
            par = ct % 2
            xt = xsb[:, par * DM:(par + 1) * DM]
            if ct >= 2:
                P.w("sync", s_ac, xn_consumed_act[ct - 2])
            def mkx(s, ct=ct, xt=xt, dx=d_x[par]):
                s.dma_start(out=xt, in_=xs[ct * 128:(ct + 1) * 128, :]).then_inc(dx.h, 16)
            P.em("sync", mkx)
            d_x[par].n += 16
            x_rdy = d_x[par].n

            sum_done = dve_op(lambda v, xt=xt: v.reduce_sum(col(SX), xt, axis=AX.X), (d_x[par], x_rdy))
            if ct >= 1:
                P.w("dve", s_pe, xnT_done[ct - 1])
            ssq_done = dve_op(lambda v, xt=xt: v.scalar_tensor_tensor(
                xn_tm[:], xt, 1.0, xt, op0=ALU.mult, op1=ALU.mult, accum_out=col(SSQ)))
            dve_op(lambda v: v.tensor_scalar_mul(col(M_), col(SX), 1.0 / DM), edge=True)
            dve_op(lambda v: v.tensor_scalar_mul(col(EX2), col(SSQ), 1.0 / DM), edge=True)
            dve_op(lambda v: v.tensor_tensor(col(T1_), col(M_), col(M_), op=ALU.mult), edge=True)
            dve_op(lambda v: v.tensor_tensor(col(VAR), col(EX2), col(T1_), op=ALU.subtract), edge=True)
            var_done = dve_op(lambda v: v.tensor_scalar_add(col(VAR), col(VAR), EPS_LN), edge=True)
            P.w("act", s_dv, var_done)
            P.em("act", lambda a: a.activation(col(LNV), col(VAR), AF.Ln).then_inc(s_ac.h, 1))
            lnv_done = ac_inc()
            P.w("act", s_ac, lnv_done)
            P.em("act", lambda a: a.activation(col(RSTD), col(LNV), AF.Exp, scale=-0.5).then_inc(s_ac.h, 1))
            rstd_done = ac_inc()
            nmr_done = dve_op(lambda v: v.scalar_tensor_tensor(
                col(NMR), col(M_), -1.0, col(RSTD), op0=ALU.mult, op1=ALU.mult), (s_ac, rstd_done), edge=True)
            P.w("act", s_dv, nmr_done)
            P.w("act", s_ac, rstd_done)
            def mkxn(a, xt=xt):
                a.activation(xn_tm[:], xt, AF.Identity, bias=col(NMR), scale=col(RSTD)).then_inc(s_ac.h, 1)
            P.em("act", mkxn)
            xn_done = ac_inc()
            xn_consumed_act[ct] = xn_done

            P.w("pe", s_ac, xn_done)
            P.w("pe", s_dv, b6_evac)
            for kh in range(NKH):
                def mk(t, kh=kh, sp=(kh == NKH - 1)):
                    ins = t.transpose(b6[:, kh * 128:(kh + 1) * 128], xn_tm[:, kh * 128:(kh + 1) * 128], identb[:])
                    if sp:
                        ins.then_inc(s_pe.h, 1)
                P.em("pe", mk)
            xnT_done[ct] = pe_inc()
            xn_view = xn_fm[:].rearrange("p (kh t) -> p kh t", t=TOK)[:, :, ct * 128:(ct + 1) * 128]
            b6v = b6[:, 0:1024].rearrange("p (kh t) -> p kh t", t=128)
            xn_evac[ct] = dve_op(lambda v, xv=xn_view, bv=b6v: v.tensor_copy(xv, bv), (s_pe, xnT_done[ct]))
            b6_evac = xn_evac[ct]

            pairs = []
            for kh in range(NKH):
                pairs.append((
                    (lambda kh=kh, ct=ct: xn_fm[:, kh * TOK + ct * 128:kh * TOK + ct * 128 + 128]),
                    (lambda kh=kh: gwT[:, kh * 8:(kh + 1) * 8]),
                ))
            P.w("pe", s_dv, xn_evac[ct])
            P.w("pe", s_dv, gwt_rdy)
            gate_mm = mm_group(lambda: b3[:, 0:8], pairs)
            gsc = gtmp[:, 0:8]
            eq1 = gtmp[:, 8:16]
            scm = gtmp[:, 16:24]
            eq2 = gtmp[:, 24:32]
            dve_op(lambda v: v.tensor_copy(gsc, b3[:, 0:8]), (s_pe, gate_mm), edge=True)
            dve_op(lambda v: v.reduce_max(col(M_), gsc, axis=AX.X), edge=True)
            dve_op(lambda v: v.tensor_scalar(eq1, gsc, col(M_), None, op0=ALU.is_equal), edge=True)
            dve_op(lambda v: v.scalar_tensor_tensor(scm, eq1, -1e30, gsc, op0=ALU.mult, op1=ALU.add), edge=True)
            dve_op(lambda v: v.reduce_max(col(T1_), scm, axis=AX.X), edge=True)
            dve_op(lambda v: v.tensor_scalar(eq2, scm, col(T1_), None, op0=ALU.is_equal), edge=True)
            d_done = dve_op(lambda v: v.tensor_tensor(col(D_), col(T1_), col(M_), op=ALU.subtract), edge=True)
            P.w("act", s_dv, d_done)
            P.em("act", lambda a: a.activation(col(ED), col(D_), AF.Exp).then_inc(s_ac.h, 1))
            ed_done = ac_inc()
            dve_op(lambda v: v.tensor_scalar_add(col(DEN0), col(ED), 1.0), (s_ac, ed_done), edge=True)
            dve_op(lambda v: v.reciprocal(col(S1_), col(DEN0)), edge=True)
            dve_op(lambda v: v.tensor_tensor(col(S2_), col(ED), col(S1_), op=ALU.mult), edge=True)
            combv = combs[:, ct * E:(ct + 1) * E]
            dve_op(lambda v, cv=combv: v.tensor_scalar(cv, eq1, col(S1_), None, op0=ALU.mult), edge=True)
            return dve_op(lambda v, cv=combv: v.scalar_tensor_tensor(
                cv, eq2, col(S2_), cv, op0=ALU.mult, op1=ALU.add), edge=True)

        def kv_part(ct):
            nonlocal last_dve, part_mm, part_add, kv_war
            kv_banks = [b0, b1]
            for sl in range(4):
                bank = kv_banks[sl % 2]
                pairs = []
                for kh in range(NKH):
                    pairs.append((
                        (lambda kh=kh, ct=ct: xn_fm[:, kh * TOK + ct * 128:kh * TOK + ct * 128 + 128]),
                        (lambda kh=kh, sl=sl: wqTA(kh, 1024 + sl * 512, 512)),
                    ))
                P.w("pe", d_wq, wq_kv_rdy)
                P.w("pe", s_dv, kv_bank_free[sl % 2])
                kv_mm = mm_group(lambda b=bank: b[:], pairs)
                if sl < 2:
                    tpar = sl % 2
                    tmn = tmin[:, tpar * 512:(tpar + 1) * 512]
                    tex = texp[:, tpar * 512:(tpar + 1) * 512]
                    mn_done = dve_op(lambda v, b=bank, t=tmn: v.tensor_scalar_min(t, b[:], 0.0),
                                     (s_pe, kv_mm), (s_ac, exp_free[tpar]))
                    P.w("act", s_dv, mn_done)
                    def mke(a, tmn=tmn, tex=tex):
                        a.activation(tex, tmn, AF.Exp).then_inc(s_ac.h, 1)
                    P.em("act", mke)
                    exp_done = ac_inc()
                    exp_free[tpar] = exp_done
                    kslice = kp_tm[:, sl * 512:(sl + 1) * 512]
                    kv_bank_free[sl % 2] = dve_op(
                        lambda v, b=bank, tex=tex, ks=kslice: v.scalar_tensor_tensor(
                            ks, b[:], 0.0, tex, op0=ALU.max, op1=ALU.add),
                        (s_ac, exp_done), (s_pe, kv_war), edge=True)
                else:
                    vslice = v_tm[:, (sl - 2) * 512:(sl - 1) * 512]
                    kv_bank_free[sl % 2] = dve_op(lambda v, b=bank, vs=vslice: v.tensor_copy(vs, b[:]),
                                                  (s_pe, kv_mm), (s_pe, kv_war))
            kv_done = last_dve
            for hb in range(4):
                P.w("pe", s_dv, kv_done)
                if part_mm > 0:
                    P.w("pe", s_dv, part_add)
                for i in range(4):
                    h = hb * 4 + i
                    def mk1(t, h=h, i=i):
                        t.matmul(b3[0:64, 64 + i * 65:128 + i * 65], kp_tm[:, h * 64:(h + 1) * 64],
                                 v_tm[:, h * 64:(h + 1) * 64], start=True, stop=True)
                    P.em("pe", mk1)
                    def mk2(t, h=h, i=i, last=(i == 3)):
                        ins = t.matmul(b3[0:64, 128 + i * 65:129 + i * 65], kp_tm[:, h * 64:(h + 1) * 64],
                                       onesb[:], start=True, stop=True)
                        if last:
                            ins.then_inc(s_pe.h, 1)
                    P.em("pe", mk2)
                part_mm = pe_inc()
                spv = spart[:, hb * 4 * 65:(hb + 1) * 4 * 65]
                part_add = dve_op(lambda v, spv=spv: v.tensor_tensor(spv, spv, b3[0:64, 64:64 + 260], op=ALU.add),
                                  (s_pe, part_mm), (s_dv, spart_zero))
            kv_war = part_mm

        part_add = 0
        comb_done = ln_gate(0)
        comb_done = ln_gate(1)
        for ct in range(NCT):
            kv_part(ct)
            if ct + 2 < NCT:
                comb_done = ln_gate(ct + 2)
        p1_pe_end = s_pe.n
        partials_done = last_dve

        P.w("sync", s_dv, partials_done)
        P.em("sync", lambda s: s.dma_start(out=cc_in[:], in_=spart[:]).then_inc(d_cs.h, 16)); d_cs.n += 16
        P.w("gp", d_cs, d_cs.n)
        P.em("gp", lambda g: g.collective_compute(
            "AllGather", ALU.bypass, replica_groups=[[0, 1, 2, 3, 4, 5, 6, 7]],
            ins=[cc_in[:]], outs=[cc_out[:]]).then_inc(s_cc.h, 1)); s_cc.n += 1
        P.w("gp", s_cc, s_cc.n)
        P.w("dve", d_ct, consts_rdy)
        for r in range(8):
            def mkg(g, r=r):
                g.dma_start(out=sgath[:], in_=cc_out[r * 64:(r + 1) * 64, :]).then_inc(d_ms.h, 16)
            P.em("gp", mkg)
            d_ms.n += 16
            gath_rdy = d_ms.n
            acc_done = dve_op(lambda v, r=r: v.scalar_tensor_tensor(
                saug[:], sgath[:], wselb[:, r:r + 1], saug[:],
                op0=ALU.mult, op1=ALU.add), (d_ms, gath_rdy), edge=True)
            if r < 7:
                P.w("gp", s_dv, acc_done)
        saug_bf_rdy = dve_op(lambda v: v.tensor_copy(saugb[:], saug[:]), edge=True)

        ffn_state = {
            "sil_par": 0,
            "g_pe_done": [0, 0],
            "y_evac": [0, 0],
            "g_hist": [0, 0],
            "sil_free": [0, 0],
            "bH_free": [0, 0],
            "bU_free": [0, 0],
            "fjg": 0,
            "last_g_pe": 0,
            "w13_loaded": {},
            "w2_loaded": {},
            "span_pe_done": {},
        }

        def load_w13(k, wait_pe=0):
            e_, sp_ = divmod(k, NSP)
            slot = k % 2
            dw = d_w[slot]
            if wait_pe:
                P.w("gp", s_pe, wait_pe)
            def mk(g, e_=e_, sp_=sp_, slot=slot, dw=dw):
                src1 = w1[e_].rearrange("(kh p) f -> p kh f", p=128)[:, :, sp_ * SPAN:(sp_ + 1) * SPAN]
                dst1 = wreg[:, slot * NKH * SPAN:(slot + 1) * NKH * SPAN].rearrange("p (kh f) -> p kh f", f=SPAN)
                g.dma_start(out=dst1, in_=src1).then_inc(dw.h, 16)
                src3 = w3[e_].rearrange("(kh p) f -> p kh f", p=128)[:, :, sp_ * SPAN:(sp_ + 1) * SPAN]
                dst3 = wreg[:, 8192 + slot * NKH * SPAN:8192 + (slot + 1) * NKH * SPAN].rearrange(
                    "p (kh f) -> p kh f", f=SPAN)
                g.dma_start(out=dst3, in_=src3).then_inc(dw.h, 16)
            P.em("gp", mk)
            dw.n += 32
            ffn_state["w13_loaded"][k] = dw.n

        def load_w2s(k, wait_pe=0):
            e_, sp_ = divmod(k, NSP)
            slot = k % 2
            dw = d_w2[slot]
            if wait_pe:
                P.w("gp", s_pe, wait_pe)
            def mk2(g, e_=e_, sp_=sp_, slot=slot, dw=dw):
                src2 = w2[e_].rearrange("(kf p) n -> p kf n", p=128)[:, sp_ * NFJ:(sp_ + 1) * NFJ, :]
                dst2 = wreg[:, 16384 + slot * NFJ * DM:16384 + (slot + 1) * NFJ * DM].rearrange(
                    "p (kf n) -> p kf n", n=DM)
                g.dma_start(out=dst2, in_=src2).then_inc(dw.h, 16)
            P.em("gp", mk2)
            dw.n += 16
            ffn_state["w2_loaded"][k] = dw.n

        def ffn_chunk(tc, pe_gate, pre_dv=(), pre_ac=()):
            st = ffn_state
            if tc == 1:
                P.w("sync", d_st, 16)
            xsrc = xs.rearrange("(tt p) h -> p tt h", p=128)[:, tc * NTT:(tc + 1) * NTT, :]
            P.em("sync", lambda s, xsrc=xsrc: s.dma_start(
                out=y_acc[:].rearrange("p (tt h) -> p tt h", h=DM), in_=xsrc).then_inc(d_yx.h, 16))
            d_yx.n += 16
            yx_rdy = d_yx.n

            st["w13_loaded"] = {}
            st["w2_loaded"] = {}
            st["span_pe_done"] = {}
            if tc == 0:
                load_w13(0)
                load_w13(1)
                load_w2s(0, wait_pe=pe_gate)
                load_w2s(1)
            else:
                load_w13(0, wait_pe=pe_gate)
                load_w13(1)
                load_w2s(0)
                load_w2s(1)

            bH = [b0, b4]
            bU = [b1, b5]
            for k in range(NSPANS):
                e_, sp_ = divmod(k, NSP)
                slot = k % 2
                half = k % 2
                P.w("pe", d_w[slot], st["w13_loaded"][k])
                P.w("pe", s_dv, st["y_evac"][1])
                if k == 0:
                    P.w("pe", s_dv, xn_evac[NTT * (tc + 1) - 1])
                    for v_ in pre_dv:
                        P.w("pe", s_dv, v_)
                    for v_ in pre_ac:
                        P.w("pe", s_ac, v_)
                for tg in range(2):
                    gpar = tg % 2
                    tof = tc * CTOK + tg * 512
                    for fj in range(NFJ):
                        spar = st["sil_par"]
                        bp = st["fjg"] % 2
                        P.w("pe", s_dv, st["g_hist"][bp])
                        pairs = [((lambda kh=kh, slot=slot, fj=fj: w1b(slot, kh, fj * 128, 128)),
                                  (lambda kh=kh, tof=tof: xn_fm[:, kh * TOK + tof:kh * TOK + tof + 512]))
                                 for kh in range(NKH)]
                        h_done = mm_group(lambda bp=bp: bH[bp][:], pairs)
                        P.w("act", s_pe, h_done)
                        P.w("act", s_dv, st["sil_free"][spar])
                        def mks(a, spar=spar, bp=bp):
                            a.activation(sil[:, spar * 512:(spar + 1) * 512], bH[bp][:], AF.Silu).then_inc(s_ac.h, 1)
                        P.em("act", mks)
                        st["bH_free"][bp] = ac_inc()
                        sil_done = s_ac.n
                        pairs = [((lambda kh=kh, slot=slot, fj=fj: w3b(slot, kh, fj * 128, 128)),
                                  (lambda kh=kh, tof=tof: xn_fm[:, kh * TOK + tof:kh * TOK + tof + 512]))
                                 for kh in range(NKH)]
                        u_done = mm_group(lambda bp=bp: bU[bp][:], pairs)
                        gd = gbuf[:, (gpar * 8 + half * NFJ + fj) * 512:(gpar * 8 + half * NFJ + fj) * 512 + 512]
                        g_done = dve_op(
                            lambda v, gd=gd, spar=spar, bp=bp: v.tensor_tensor(
                                gd, sil[:, spar * 512:(spar + 1) * 512], bU[bp][:], op=ALU.mult),
                            (s_ac, sil_done), (s_pe, u_done), (s_pe, st["g_pe_done"][gpar]))
                        st["g_hist"][bp] = g_done
                        st["bU_free"][bp] = g_done
                        st["sil_free"][spar] = g_done
                        st["sil_par"] ^= 1
                        st["fjg"] += 1
                if k + 2 < NSPANS:
                    load_w13(k + 2, wait_pe=s_pe.n)
                if k % 2 == 0:
                    continue
                g_rdy = s_dv.n
                P.w("pe", d_w2[0], st["w2_loaded"][k - 1])
                P.w("pe", d_w2[1], st["w2_loaded"][k])
                P.w("pe", s_dv, g_rdy)
                for tg in range(2):
                    gpar = tg % 2
                    for tt in range(4):
                        ttg = tg * 4 + tt
                        ct_g = tc * NTT + ttg
                        ypar = ttg % 2
                        ybuf = py if ypar == 0 else py2
                        P.w("pe", s_dv, st["y_evac"][ypar])
                        for ht in range(2):
                            pairs = []
                            for hf in range(2):
                                for fj in range(NFJ):
                                    pairs.append((
                                        (lambda fj=fj, gpar=gpar, tt=tt, hf=hf: gbuf[
                                            :, (gpar * 8 + hf * NFJ + fj) * 512 + tt * 128:
                                               (gpar * 8 + hf * NFJ + fj) * 512 + tt * 128 + 128]),
                                        (lambda fj=fj, hf=hf, ht=ht: w2b(hf, fj, ht * 512, 512)),
                                    ))
                            ymm = mm_group(lambda ht=ht, ybuf=ybuf: ybuf[:, ht * 512:(ht + 1) * 512], pairs)
                        dst = y_acc[:, ttg * DM:ttg * DM + DM]
                        cslice = combs[:, ct_g * E + e_:ct_g * E + e_ + 1]
                        st["y_evac"][ypar] = dve_op(
                            lambda v, dst=dst, cs=cslice, ybuf=ybuf: v.scalar_tensor_tensor(
                                dst, ybuf[:], cs, dst, op0=ALU.mult, op1=ALU.add),
                            (s_pe, ymm), (d_yx, yx_rdy), (s_dv, comb_done))
                    st["g_pe_done"][gpar] = s_pe.n
                g_end = s_pe.n
                st["span_pe_done"][k - 1] = g_end
                st["span_pe_done"][k] = g_end
                if k + 2 < NSPANS:
                    load_w2s(k + 1, wait_pe=g_end)
                    load_w2s(k + 2)
            return s_pe.n
            return s_pe.n

        scan_state = {
            "msc_free": [0, 0],
            "rec_free": [0, 0],
            "b2_free": 0,
            "b4_free": 0,
            "b5_free": 0,
            "b3_free": 0,
            "qkv_bank_free": [0, 0],
            "exp_free": [0, 0],
            "sc_free": [0, 0],
            "nd_free": [0, 0],
            "saug_rdy_pair": [0] * (NH // 2),
            "qp_war": 0,
            "kv_war": 0,
        }

        def scan_chunk(tc, wq_rdy, pre_ac=0, pre_dv=0):
            nonlocal b6_evac, saug_bf_rdy
            ss = scan_state
            proj_evac = 0
            for ctl in range(8):
                ct = tc * 8 + ctl
                for sl in range(2):
                    bank = [b0, b1][sl % 2]
                    pairs = [((lambda kh=kh, ct=ct: xn_fm[:, kh * TOK + ct * 128:kh * TOK + ct * 128 + 128]),
                              (lambda kh=kh, sl=sl: wqT(kh, sl * 512, 512)))
                             for kh in range(NKH)]
                    P.w("pe", d_wq, wq_rdy)
                    P.w("pe", s_dv, ss["qkv_bank_free"][sl % 2])
                    if ctl == 0 and sl == 0:
                        P.w("pe", s_ac, pre_ac)
                        P.w("pe", s_dv, pre_dv)
                    q_mm = mm_group(lambda b=bank: b[:], pairs)
                    tpar = sl % 2
                    tmn = tmin[:, tpar * 512:(tpar + 1) * 512]
                    tex = texp[:, tpar * 512:(tpar + 1) * 512]
                    mn_done = dve_op(lambda v, b=bank, t=tmn: v.tensor_scalar_min(t, b[:], 0.0),
                                     (s_pe, q_mm), (s_ac, ss["exp_free"][tpar]))
                    P.w("act", s_dv, mn_done)
                    def mke(a, tmn=tmn, tex=tex):
                        a.activation(tex, tmn, AF.Exp).then_inc(s_ac.h, 1)
                    P.em("act", mke)
                    exp_done = ac_inc()
                    ss["exp_free"][tpar] = exp_done
                    qslice = qp_tm[:, sl * 512:(sl + 1) * 512]
                    ss["qkv_bank_free"][sl % 2] = dve_op(
                        lambda v, b=bank, tex=tex, qs=qslice: v.scalar_tensor_tensor(
                            qs, b[:], 0.0, tex, op0=ALU.max, op1=ALU.add),
                        (s_ac, exp_done), (s_pe, ss["qp_war"]), edge=True)
                for sl in range(4):
                    bank = [b0, b1][sl % 2]
                    pairs = [((lambda kh=kh, ct=ct: xn_fm[:, kh * TOK + ct * 128:kh * TOK + ct * 128 + 128]),
                              (lambda kh=kh, sl=sl: wqT(kh, 1024 + sl * 512, 512)))
                             for kh in range(NKH)]
                    P.w("pe", s_dv, ss["qkv_bank_free"][sl % 2])
                    kv_mm = mm_group(lambda b=bank: b[:], pairs)
                    if sl < 2:
                        tpar = sl % 2
                        tmn = tmin[:, tpar * 512:(tpar + 1) * 512]
                        tex = texp[:, tpar * 512:(tpar + 1) * 512]
                        mn_done = dve_op(lambda v, b=bank, t=tmn: v.tensor_scalar_min(t, b[:], 0.0),
                                         (s_pe, kv_mm), (s_ac, ss["exp_free"][tpar]))
                        P.w("act", s_dv, mn_done)
                        def mke(a, tmn=tmn, tex=tex):
                            a.activation(tex, tmn, AF.Exp).then_inc(s_ac.h, 1)
                        P.em("act", mke)
                        exp_done = ac_inc()
                        ss["exp_free"][tpar] = exp_done
                        kslice = kp_tm[:, sl * 512:(sl + 1) * 512]
                        ss["qkv_bank_free"][sl % 2] = dve_op(
                            lambda v, b=bank, tex=tex, ks=kslice: v.scalar_tensor_tensor(
                                ks, b[:], 0.0, tex, op0=ALU.max, op1=ALU.add),
                            (s_ac, exp_done), (s_pe, ss["kv_war"]), edge=True)
                    else:
                        vslice = v_tm[:, (sl - 2) * 512:(sl - 1) * 512]
                        ss["qkv_bank_free"][sl % 2] = dve_op(
                            lambda v, b=bank, vs=vslice: v.tensor_copy(vs, b[:]), (s_pe, kv_mm), (s_pe, ss["kv_war"]))
                qkv_rdy = last_dve

                for (srct, dstv) in ((qp_tm, qp_fm), (kp_tm, kp_fm)):
                    for half in range(2):
                        P.w("pe", s_dv, qkv_rdy)
                        P.w("pe", s_dv, b6_evac)
                        for hh in range(8):
                            h = half * 8 + hh
                            def mk(t, srct=srct, h=h, hh=hh, sp=(hh == 7)):
                                ins = t.transpose(b6[0:64, hh * 128:(hh + 1) * 128],
                                                  srct[:, h * 64:(h + 1) * 64], identb[:])
                                if sp:
                                    ins.then_inc(s_pe.h, 1)
                            P.em("pe", mk)
                        trd = pe_inc()
                        b6_evac = dve_op(lambda v, dstv=dstv, half=half: v.tensor_copy(
                            dstv[:, half * 1024:(half + 1) * 1024], b6[0:64, 0:1024]), (s_pe, trd))
                fm_rdy = b6_evac
                ss["qp_war"] = s_pe.n

                sc_banks = [py[:, 0:256], b7[:, 0:256]]
                nd_banks = [py2[:, 0:128], b0[:, 0:128]]

                def emit_sc(p):
                    bankap = sc_banks[p % 2]
                    P.w("pe", s_dv, ss["sc_free"][p % 2])
                    for i in range(2):
                        h = 2 * p + i
                        def mk(t, h=h, i=i, bankap=bankap):
                            ins = t.matmul(bankap[:, i * 128:(i + 1) * 128],
                                           kp_fm[:, h * 128:(h + 1) * 128],
                                           qp_fm[:, h * 128:(h + 1) * 128], start=True, stop=True)
                            if i == 1:
                                ins.then_inc(s_pe.h, 1)
                        P.em("pe", mk)
                    return pe_inc()

                sc_mm = {}
                sc_mm[0] = emit_sc(0)
                sc_mm[1] = emit_sc(1)

                def emit_mask(p):
                    ppar = p % 2
                    bankap = sc_banks[ppar]
                    m0 = msc[:, ppar * 256:ppar * 256 + 128]
                    m1 = msc[:, ppar * 256 + 128:ppar * 256 + 256]
                    P.w("dve", s_pe, ss["msc_free"][ppar])
                    dve_op(lambda v, m0=m0, bankap=bankap: v.tensor_tensor(m0, bankap[:, 0:128], maskb[:], op=ALU.mult),
                           (s_pe, sc_mm[p]), (d_ct, consts_rdy))
                    md = dve_op(lambda v, m1=m1, bankap=bankap: v.tensor_tensor(m1, bankap[:, 128:256], maskb[:], op=ALU.mult))
                    ss["sc_free"][ppar] = md
                    ss["b2_free"] = md
                    return md

                mask_done = {0: emit_mask(0)}
                for p in range(NH // 2):
                    ppar = p % 2
                    ndb = nd_banks[ppar]
                    P.w("pe", s_dv, mask_done[p])
                    P.w("pe", s_ac, ss["nd_free"][ppar])
                    P.w("pe", s_dv, ss["saug_rdy_pair"][p])
                    for i in range(2):
                        h = 2 * p + i
                        def mkn1(t, h=h, i=i, ppar=ppar, ndb=ndb):
                            t.matmul(ndb[:, i * 64:(i + 1) * 64],
                                     msc[:, ppar * 256 + i * 128:ppar * 256 + (i + 1) * 128],
                                     v_tm[:, h * 64:(h + 1) * 64], start=True, stop=False)
                        P.em("pe", mkn1)
                        def mkn2(t, h=h, i=i, ndb=ndb, last=(i == 1)):
                            ins = t.matmul(ndb[:, i * 64:(i + 1) * 64], qp_fm[:, h * 128:(h + 1) * 128],
                                           saugb[:, h * 65:h * 65 + 64], start=False, stop=True)
                            if last:
                                ins.then_inc(s_pe.h, 1)
                        P.em("pe", mkn2)
                    num_mm = pe_inc()
                    P.w("pe", s_dv, ss["b5_free"])
                    for i in range(2):
                        h = 2 * p + i
                        def mkd1(t, i=i, ppar=ppar):
                            t.matmul(b5[:, i:i + 1], msc[:, ppar * 256 + i * 128:ppar * 256 + (i + 1) * 128],
                                     onesb[:], start=True, stop=False)
                        P.em("pe", mkd1)
                        def mkd2(t, h=h, i=i, last=(i == 1)):
                            ins = t.matmul(b5[:, i:i + 1], qp_fm[:, h * 128:(h + 1) * 128],
                                           saugb[:, h * 65 + 64:h * 65 + 65], start=False, stop=True)
                            if last:
                                ins.then_inc(s_pe.h, 1)
                        P.em("pe", mkd2)
                    den_mm = pe_inc()
                    ss["msc_free"][ppar] = den_mm
                    P.w("pe", s_dv, ss["b3_free"])
                    for i in range(2):
                        h = 2 * p + i
                        def mks1(t, h=h, i=i):
                            t.matmul(py[0:64, 512 + 64 + i * 65:512 + 128 + i * 65],
                                     kp_tm[:, h * 64:(h + 1) * 64], v_tm[:, h * 64:(h + 1) * 64],
                                     start=True, stop=True)
                        P.em("pe", mks1)
                        def mks2(t, h=h, i=i, last=(i == 1)):
                            ins = t.matmul(py[0:64, 512 + 128 + i * 65:512 + 129 + i * 65],
                                           kp_tm[:, h * 64:(h + 1) * 64], onesb[:], start=True, stop=True)
                            if last:
                                ins.then_inc(s_pe.h, 1)
                        P.em("pe", mks2)
                    ds_mm = pe_inc()
                    if p + 2 < NH // 2:
                        sc_mm[p + 2] = emit_sc(p + 2)
                    rc = 21 + 2 * ppar
                    dn = 17 + 2 * ppar
                    P.w("dve", s_ac, ss["rec_free"][ppar])
                    dve_op(lambda v, dn=dn: v.tensor_scalar_add(lns[:, dn:dn + 2], b5[:, 0:2], EPS_DEN),
                           (s_pe, den_mm))
                    ss["b5_free"] = last_dve
                    rec_done = dve_op(lambda v, rc=rc, dn=dn: v.reciprocal(lns[:, rc:rc + 2], lns[:, dn:dn + 2]),
                                      edge=True)
                    if p + 1 < NH // 2:
                        mask_done[p + 1] = emit_mask(p + 1)
                    for i in range(2):
                        h = 2 * p + i
                        P.w("act", s_pe, num_mm)
                        P.w("act", s_dv, rec_done)
                        def mky(a, h=h, i=i, rc=rc, ndb=ndb):
                            a.activation(y_tm[:, h * 64:(h + 1) * 64], ndb[:, i * 64:(i + 1) * 64], AF.Identity,
                                         scale=lns[:, rc + i:rc + i + 1]).then_inc(s_ac.h, 1)
                        P.em("act", mky)
                        ac_inc()
                    ss["nd_free"][ppar] = s_ac.n
                    ss["rec_free"][ppar] = s_ac.n
                    sgv = saug[:, 2 * p * 65:(2 * p + 2) * 65]
                    dve_op(lambda v, sgv=sgv: v.tensor_tensor(sgv, sgv, py[0:64, 512 + 64:512 + 194], op=ALU.add),
                           (s_pe, ds_mm), (s_dv, ss["saug_rdy_pair"][p]))
                    ss["b3_free"] = last_dve
                    sgbv = saugb[:, 2 * p * 65:(2 * p + 2) * 65]
                    ss["saug_rdy_pair"][p] = dve_op(lambda v, sgv=sgv, sgbv=sgbv: v.tensor_copy(sgbv, sgv), edge=True)
                    ss["kv_war"] = ds_mm
                y_all = s_ac.n

                P.w("pe", s_ac, y_all)
                P.w("pe", s_dv, b6_evac)
                for j in range(NKH):
                    def mk(t, j=j, sp=(j == NKH - 1)):
                        ins = t.transpose(b6[:, j * 128:(j + 1) * 128], y_tm[:, j * 128:(j + 1) * 128], identb[:])
                        if sp:
                            ins.then_inc(s_pe.h, 1)
                    P.em("pe", mk)
                ytr = pe_inc()
                b6_evac = dve_op(lambda v: v.tensor_copy(y_fm[:], b6[:, 0:1024]), (s_pe, ytr))
                yfm_rdy = b6_evac
                for cs in range(2):
                    bank = b7 if cs == 0 else b2
                    bid = "b7" if cs == 0 else "b2"
                    pairs = [((lambda j=j: y_fm[:, j * 128:(j + 1) * 128]),
                              (lambda j=j, cs=cs: wpT(j, cs * 512, 512)))
                             for j in range(NKH)]
                    P.w("pe", s_dv, yfm_rdy)
                    if cs == 1:
                        P.w("pe", s_dv, ss["b2_free"])
                        P.w("pe", s_dv, proj_evac)
                    else:
                        P.w("pe", s_dv, proj_evac)
                    pj_mm = mm_group(lambda b=bank: b[:], pairs)
                    dst = y_acc[:, ctl * DM + cs * 512:ctl * DM + cs * 512 + 512]
                    proj_evac = dve_op(lambda v, dst=dst, b=bank: v.scalar_tensor_tensor(
                        dst, b[:], 1.0, dst, op0=ALU.mult, op1=ALU.add), (s_pe, pj_mm))
                    if cs == 1:
                        ss["b2_free"] = proj_evac
            return proj_evac

        ffn0_pe_end = ffn_chunk(0, pe_gate=p1_pe_end, pre_dv=(kv_bank_free[0], kv_bank_free[1]))
        P.w("sync", s_pe, ffn0_pe_end)
        for kh in range(NKH):
            def mk(s, kh=kh):
                s.dma_start(out=wqT(kh, 0, 3072), in_=wattn_bf[:, kh * 128:(kh + 1) * 128],
                            transpose=True).then_inc(d_wq.h, 16)
            P.em("sync", mk)
            d_wq.n += 16
        for kh in range(NKH):
            def mk(s, kh=kh):
                s.dma_start(out=wpT(kh, 0, 1024), in_=wp_bf[:, kh * 128:(kh + 1) * 128],
                            transpose=True).then_inc(d_wq.h, 16)
            P.em("sync", mk)
            d_wq.n += 16
        wq_full0 = d_wq.n
        scan_state["kv_war"] = p1_pe_end
        scan_state["saug_rdy_pair"] = [saug_bf_rdy] * (NH // 2)
        scan_state["b5_free"] = max(max(ffn_state["bU_free"]), max(ffn_state["y_evac"]))
        scan0_end = scan_chunk(0, wq_full0, pre_ac=max(ffn_state["bH_free"]),
                               pre_dv=max(max(ffn_state["bU_free"]), max(ffn_state["y_evac"])))
        P.w("sync", s_dv, scan0_end)
        yo0 = yout.rearrange("(tt p) h -> p tt h", p=128)[:, 0:NTT, :]
        P.em("sync", lambda s, yo0=yo0: s.dma_start(
            out=yo0, in_=y_acc[:].rearrange("p (tt h) -> p tt h", h=DM)).then_inc(d_st.h, 16))
        d_st.n += 16

        scan0_pe_end = s_pe.n
        ffn1_pe_end = ffn_chunk(1, pe_gate=scan0_pe_end,
                                pre_dv=(scan_state["qkv_bank_free"][0], scan_state["qkv_bank_free"][1],
                                        scan_state["b5_free"], scan_state["b3_free"], scan_state["b2_free"]),
                                pre_ac=(scan_state["nd_free"][0], scan_state["nd_free"][1],
                                        scan_state["rec_free"][0], scan_state["rec_free"][1]))
        P.w("sync", s_pe, ffn1_pe_end)
        for kh in range(NKH):
            def mk(s, kh=kh):
                s.dma_start(out=wqT(kh, 0, 3072), in_=wattn_bf[:, kh * 128:(kh + 1) * 128],
                            transpose=True).then_inc(d_wq.h, 16)
            P.em("sync", mk)
            d_wq.n += 16
        for kh in range(NKH):
            def mk(s, kh=kh):
                s.dma_start(out=wpT(kh, 0, 1024), in_=wp_bf[:, kh * 128:(kh + 1) * 128],
                            transpose=True).then_inc(d_wq.h, 16)
            P.em("sync", mk)
            d_wq.n += 16
        wq_full1 = d_wq.n
        scan_state["b5_free"] = max(scan_state["b5_free"], max(ffn_state["bU_free"]), max(ffn_state["y_evac"]))
        scan1_end = scan_chunk(1, wq_full1, pre_ac=max(ffn_state["bH_free"]),
                               pre_dv=max(max(ffn_state["bU_free"]), max(ffn_state["y_evac"])))
        P.w("sync", s_dv, scan1_end)
        yo1 = yout.rearrange("(tt p) h -> p tt h", p=128)[:, NTT:2 * NTT, :]
        P.em("sync", lambda s, yo1=yo1: s.dma_start(
            out=yo1, in_=y_acc[:].rearrange("p (tt h) -> p tt h", h=DM)).then_inc(d_st.h, 16))
        d_st.n += 16
        P.w("sync", d_st, d_st.n)

        with nc.Block() as block:
            @block.tensor
            def _(t):
                for f in P.ops["pe"]:
                    f(t)

            @block.scalar
            def _(a):
                for f in P.ops["act"]:
                    f(a)

            @block.vector
            def _(v):
                for f in P.ops["dve"]:
                    f(v)

            @block.sync
            def _(s):
                for f in P.ops["sync"]:
                    f(s)

            @block.gpsimd
            def _(g):
                for f in P.ops["gp"]:
                    f(g)

    return nc


_NC_CACHE = {}


def kernel(L0_attn_x, L0_attn_w_attn, L0_attn_w_proj, L0_ffn_gate_w, L0_ffn_w1, L0_ffn_w3, L0_ffn_w2):
    x = np.asarray(L0_attn_x)
    w_attn = np.ascontiguousarray(np.asarray(L0_attn_w_attn), dtype=np.float32)
    w_proj = np.ascontiguousarray(np.asarray(L0_attn_w_proj), dtype=np.float32)
    gate_w = np.ascontiguousarray(np.asarray(L0_ffn_gate_w), dtype=np.float32)
    w1 = np.ascontiguousarray(np.asarray(L0_ffn_w1), dtype=np.float32)
    w3 = np.ascontiguousarray(np.asarray(L0_ffn_w3), dtype=np.float32)
    w2 = np.ascontiguousarray(np.asarray(L0_ffn_w2), dtype=np.float32)

    ident = np.eye(128, dtype=ml_dtypes.bfloat16)
    maskT = np.triu(np.ones((128, 128))).astype(ml_dtypes.bfloat16)
    onesc = np.ones((128, 1), dtype=ml_dtypes.bfloat16)

    if "nc" not in _NC_CACHE:
        _NC_CACHE["nc"] = build_nc()
    nc = _NC_CACHE["nc"]

    in_maps = []
    for c in range(8):
        b, seg = divmod(c, 4)
        xs = np.ascontiguousarray(x[b, seg * TOK:(seg + 1) * TOK, :], dtype=np.float32)
        wsel = np.zeros((64, 8), np.float32)
        for r in range(8):
            rb, rseg = divmod(r, 4)
            if rb == b and rseg < seg:
                wsel[:, r] = 1.0
        in_maps.append({
            "xs": xs, "w_attn": w_attn, "w_proj": w_proj, "gate_w": gate_w,
            "w1": w1, "w3": w3, "w2": w2,
            "ident": ident, "maskT": maskT, "onesc": onesc, "wsel": wsel,
        })

    res = run_bass_kernel_spmd(nc, in_maps, core_ids=list(range(8)))
    out = np.empty((2, 8192, 1024), np.float32)
    for c in range(8):
        b, seg = divmod(c, 4)
        out[b, seg * TOK:(seg + 1) * TOK, :] = np.asarray(res.results[c]["yout"])
    return out


# version 12
# speedup vs baseline: 1.4401x; 1.0085x over previous
import contextlib
import numpy as np
import ml_dtypes
import concourse.bass as bass
import concourse.mybir as mybir
from concourse.bass_utils import run_bass_kernel_spmd

F32 = mybir.dt.float32
BF16 = mybir.dt.bfloat16
AF = mybir.ActivationFunctionType
ALU = mybir.AluOpType
AX = mybir.AxisListType

TOK = 2048
NCT = 16
DM = 1024
NKH = 8
NH = 16
HD = 64
E = 8
FF = 4096
SPAN = 512
NSP = FF // SPAN
NFJ = SPAN // 128
NSPANS = E * NSP
NCH = 2
CTOK = 1024
NTT = 8
EPS_LN = 1e-5
EPS_DEN = 1e-6


class Sem:
    def __init__(self, es, nc, name):
        self.h = es.enter_context(nc.semaphore(name))
        self.n = 0


class B:

    def __init__(self):
        self.ops = {k: [] for k in ("pe", "act", "dve", "sync", "gp")}

    def em(self, eng, fn):
        self.ops[eng].append(fn)

    def w(self, eng, sem, val):
        if val > 0:
            self.ops[eng].append(lambda e, s=sem.h, v=val: e.wait_ge(s, v))


def build_nc():
    nc = bass.Bass("TRN2", target_bir_lowering=False)
    xs = nc.dram_tensor("xs", [TOK, DM], F32, kind="ExternalInput").ap()
    w_attn = nc.dram_tensor("w_attn", [3 * DM, DM], F32, kind="ExternalInput").ap()
    w_proj = nc.dram_tensor("w_proj", [DM, DM], F32, kind="ExternalInput").ap()
    gate_w = nc.dram_tensor("gate_w", [E, DM], F32, kind="ExternalInput").ap()
    w1 = nc.dram_tensor("w1", [E, DM, FF], F32, kind="ExternalInput").ap()
    w3 = nc.dram_tensor("w3", [E, DM, FF], F32, kind="ExternalInput").ap()
    w2 = nc.dram_tensor("w2", [E, FF, DM], F32, kind="ExternalInput").ap()
    ident = nc.dram_tensor("ident", [128, 128], BF16, kind="ExternalInput").ap()
    maskT = nc.dram_tensor("maskT", [128, 128], BF16, kind="ExternalInput").ap()
    onesc = nc.dram_tensor("onesc", [128, 1], BF16, kind="ExternalInput").ap()
    wsel = nc.dram_tensor("wsel", [64, 8], F32, kind="ExternalInput").ap()
    yout = nc.dram_tensor("yout", [TOK, DM], F32, kind="ExternalOutput").ap()
    wattn_bf = nc.dram_tensor("wattn_bf", [3 * DM, DM], BF16, kind="Internal").ap()
    w1bf = nc.dram_tensor("w1bf", [E, DM, FF], BF16, kind="Internal").ap()
    w3bf = nc.dram_tensor("w3bf", [E, DM, FF], BF16, kind="Internal").ap()
    w2bf = nc.dram_tensor("w2bf", [E, FF, DM], BF16, kind="Internal").ap()
    wp_bf = nc.dram_tensor("wp_bf", [DM, DM], BF16, kind="Internal").ap()
    cc_in = nc.dram_tensor("cc_in", [64, NH * 65], F32, kind="Internal").ap()
    cc_out = nc.dram_tensor("cc_out", [8 * 64, NH * 65], F32, kind="Internal").ap()

    es = contextlib.ExitStack()
    sb = lambda n, s, d: es.enter_context(nc.sbuf_tensor(n, s, d))
    ps = lambda n, s, d: es.enter_context(nc.psum_tensor(n, s, d))
    with es:
        xn_fm = sb("xn_fm", [128, NKH * TOK], BF16)
        wreg = sb("wreg", [128, 32768], BF16)
        y_acc = sb("y_acc", [128, NTT * DM], F32)
        combs = sb("combs", [128, NCT * E], F32)
        saug = sb("saug", [64, NH * 65], F32)
        saugb = sb("saugb", [64, NH * 65], BF16)
        spart = sb("spart", [64, NH * 65], F32)
        sgath = sb("sgath", [64, NH * 65], F32)
        wselb = sb("wselb", [64, 8], F32)
        xsb = sb("xsb", [128, 2 * DM], F32)
        xn_tm = sb("xn_tm", [128, DM], BF16)
        lns = sb("lns", [128, 32], F32)
        gtmp = sb("gtmp", [128, 32], F32)
        gw8 = sb("gw8", [16, DM], BF16)
        gwT = sb("gwT", [128, NKH * E], BF16)
        identb = sb("identb", [128, 128], BF16)
        maskb = sb("maskb", [128, 128], BF16)
        onesb = sb("onesb", [128, 1], BF16)
        kp_tm = sb("kp_tm", [128, DM], BF16)
        v_tm = sb("v_tm", [128, DM], BF16)
        qp_tm = sb("qp_tm", [128, DM], BF16)
        qp_fm = sb("qp_fm", [64, NH * 128], BF16)
        kp_fm = sb("kp_fm", [64, NH * 128], BF16)
        y_tm = sb("y_tm", [128, DM], BF16)
        y_fm = sb("y_fm", [128, DM], BF16)
        tmin = sb("tmin", [128, 2 * 512], F32)
        texp = sb("texp", [128, 2 * 512], F32)
        msc = sb("msc", [128, 512], BF16)
        sil = sb("sil", [128, 2 * 512], BF16)
        gbuf = sb("gbuf", [128, 2 * 8 * 512], BF16)

        def w1b(slot, kh, fcol0, n):
            off = slot * (NKH * SPAN) + kh * SPAN + fcol0
            return wreg[:, off:off + n]

        def w3b(slot, kh, fcol0, n):
            off = 8192 + slot * (NKH * SPAN) + kh * SPAN + fcol0
            return wreg[:, off:off + n]

        def w2b(slot, kf, hcol0, n):
            off = 16384 + slot * (NFJ * DM) + kf * DM + hcol0
            return wreg[:, off:off + n]

        def wqT(kh, c0, n):
            return wreg[:, kh * 3072 + c0:kh * 3072 + c0 + n]

        def wqTA(kh, c0, n):
            return wreg[:, 16384 + kh * 2048 + (c0 - 1024):16384 + kh * 2048 + (c0 - 1024) + n]

        def wpT(kh, c0, n):
            return wreg[:, 24576 + kh * DM + c0:24576 + kh * DM + c0 + n]

        b0 = ps("b0", [128, 512], F32)
        b1 = ps("b1", [128, 512], F32)
        py = ps("py", [128, 1024], F32)
        b2 = py[:, 0:512]
        b3 = py[:, 512:1024]
        py2 = ps("py2", [128, 1024], F32)
        b4 = py2[:, 0:512]
        b5 = py2[:, 512:1024]
        b6 = ps("b6", [128, 1024], BF16)
        b7 = ps("b7", [128, 512], F32)

        d_x = [Sem(es, nc, "d_x0"), Sem(es, nc, "d_x1")]
        d_yx = Sem(es, nc, "d_yx")
        d_w = [Sem(es, nc, "d_w0"), Sem(es, nc, "d_w1")]
        d_w2 = [Sem(es, nc, "d_w2a"), Sem(es, nc, "d_w2b")]
        d_wq = Sem(es, nc, "d_wq")
        d_ms = Sem(es, nc, "d_ms")
        d_cast = Sem(es, nc, "d_cast")
        d_ct = Sem(es, nc, "d_ct")
        d_cs = Sem(es, nc, "d_cs")
        s_cc = Sem(es, nc, "s_cc")
        s_pe = Sem(es, nc, "s_pe")
        s_ac = Sem(es, nc, "s_ac")
        s_dv = Sem(es, nc, "s_dv")
        d_st = Sem(es, nc, "d_st")

        P = B()

        def pe_inc():
            s_pe.n += 1
            return s_pe.n

        def ac_inc():
            s_ac.n += 1
            return s_ac.n

        def dv_inc():
            s_dv.n += 1
            return s_dv.n

        def mm_group(bank_ap_fn, pairs, extra_first_wait=None):
            n = len(pairs)
            for i, (lf, rf) in enumerate(pairs):
                def mk(t, lf=lf, rf=rf, st=(i == 0), sp=(i == n - 1), bf=bank_ap_fn):
                    ins = t.matmul(bf(), lf(), rf(), start=st, stop=sp)
                    if sp:
                        ins.then_inc(s_pe.h, 1)
                P.em("pe", mk)
            return pe_inc()

        P.em("sync", lambda s: s.dma_start(out=identb[:], in_=ident).then_inc(d_ct.h, 16)); d_ct.n += 16
        P.em("sync", lambda s: s.dma_start(out=maskb[:], in_=maskT).then_inc(d_ct.h, 16)); d_ct.n += 16
        P.em("sync", lambda s: s.dma_start(out=onesb[:], in_=onesc).then_inc(d_ct.h, 16)); d_ct.n += 16
        P.em("sync", lambda s: s.dma_start(out=wselb[:], in_=wsel).then_inc(d_ct.h, 16)); d_ct.n += 16
        consts_rdy = d_ct.n

        P.em("gp", lambda g: g.dma_start(out=wattn_bf[:], in_=w_attn).then_inc(d_ms.h, 16)); d_ms.n += 16
        P.em("gp", lambda g: g.dma_start(out=gw8[0:8, :], in_=gate_w).then_inc(d_ms.h, 16)); d_ms.n += 16
        P.em("gp", lambda g: g.dma_start(out=wp_bf[:], in_=w_proj).then_inc(d_ms.h, 16)); d_ms.n += 16
        p0_gp = d_ms.n

        P.em("dve", lambda v: v.memset(saug[:], 0.0).then_inc(s_dv.h, 1)); dv_inc()
        P.em("dve", lambda v: v.memset(spart[:], 0.0).then_inc(s_dv.h, 1)); spart_zero = dv_inc()

        P.w("pe", d_ms, p0_gp)
        P.w("pe", d_ct, consts_rdy)
        for kh in range(NKH):
            def mk(t, kh=kh, sp=(kh == NKH - 1)):
                ins = t.transpose(b6[:, kh * 8:kh * 8 + 8], gw8[0:8, kh * 128:(kh + 1) * 128], identb[0:8, 0:8])
                if sp:
                    ins.then_inc(s_pe.h, 1)
            P.em("pe", mk)
        gwt_tr = pe_inc()
        P.w("dve", s_pe, gwt_tr)
        P.em("dve", lambda v: v.tensor_copy(gwT[:], b6[:, 0:NKH * E]).then_inc(s_dv.h, 1))
        gwt_rdy = dv_inc()

        P.w("sync", d_ms, p0_gp)
        for kh in range(NKH):
            def mk(s, kh=kh):
                s.dma_start(out=wqTA(kh, 1024, 2048), in_=wattn_bf[1024:3072, kh * 128:(kh + 1) * 128],
                            transpose=True).then_inc(d_wq.h, 16)
            P.em("sync", mk)
            d_wq.n += 16
        wq_kv_rdy = d_wq.n

        SX, SSQ, M_, EX2, T1_, VAR, LNV, RSTD, NMR = range(9)
        D_, ED, DEN0, DEN1, REC0, REC1, S1_, S2_ = range(9, 17)

        def col(i):
            return lns[:, i:i + 1]

        last_dve = 0
        exp_free = [0, 0]
        xn_evac = [0] * NCT
        xnT_done = [0] * NCT
        xn_consumed_act = [0] * NCT
        b6_evac = 0
        kv_bank_free = [0, 0]
        part_mm = 0
        kv_war = 0

        def dve_op(fn, *waits, edge=False):
            nonlocal last_dve
            for sem, val in waits:
                P.w("dve", sem, val)
            if edge:
                P.w("dve", s_dv, last_dve)
            def mk(v, fn=fn):
                fn(v).then_inc(s_dv.h, 1)
            P.em("dve", mk)
            last_dve = dv_inc()
            return last_dve

        def ln_gate(ct):
            nonlocal last_dve, b6_evac
            par = ct % 2
            xt = xsb[:, par * DM:(par + 1) * DM]
            if ct >= 2:
                P.w("sync", s_ac, xn_consumed_act[ct - 2])
            def mkx(s, ct=ct, xt=xt, dx=d_x[par]):
                s.dma_start(out=xt, in_=xs[ct * 128:(ct + 1) * 128, :]).then_inc(dx.h, 16)
            P.em("sync", mkx)
            d_x[par].n += 16
            x_rdy = d_x[par].n

            sum_done = dve_op(lambda v, xt=xt: v.reduce_sum(col(SX), xt, axis=AX.X), (d_x[par], x_rdy))
            if ct >= 1:
                P.w("dve", s_pe, xnT_done[ct - 1])
            ssq_done = dve_op(lambda v, xt=xt: v.scalar_tensor_tensor(
                xn_tm[:], xt, 1.0, xt, op0=ALU.mult, op1=ALU.mult, accum_out=col(SSQ)))
            dve_op(lambda v: v.tensor_scalar_mul(col(M_), col(SX), 1.0 / DM), edge=True)
            dve_op(lambda v: v.tensor_scalar_mul(col(EX2), col(SSQ), 1.0 / DM), edge=True)
            dve_op(lambda v: v.tensor_tensor(col(T1_), col(M_), col(M_), op=ALU.mult), edge=True)
            dve_op(lambda v: v.tensor_tensor(col(VAR), col(EX2), col(T1_), op=ALU.subtract), edge=True)
            var_done = dve_op(lambda v: v.tensor_scalar_add(col(VAR), col(VAR), EPS_LN), edge=True)
            P.w("act", s_dv, var_done)
            P.em("act", lambda a: a.activation(col(LNV), col(VAR), AF.Ln).then_inc(s_ac.h, 1))
            lnv_done = ac_inc()
            P.w("act", s_ac, lnv_done)
            P.em("act", lambda a: a.activation(col(RSTD), col(LNV), AF.Exp, scale=-0.5).then_inc(s_ac.h, 1))
            rstd_done = ac_inc()
            nmr_done = dve_op(lambda v: v.scalar_tensor_tensor(
                col(NMR), col(M_), -1.0, col(RSTD), op0=ALU.mult, op1=ALU.mult), (s_ac, rstd_done), edge=True)
            P.w("act", s_dv, nmr_done)
            P.w("act", s_ac, rstd_done)
            def mkxn(a, xt=xt):
                a.activation(xn_tm[:], xt, AF.Identity, bias=col(NMR), scale=col(RSTD)).then_inc(s_ac.h, 1)
            P.em("act", mkxn)
            xn_done = ac_inc()
            xn_consumed_act[ct] = xn_done

            P.w("pe", s_ac, xn_done)
            P.w("pe", s_dv, b6_evac)
            for kh in range(NKH):
                def mk(t, kh=kh, sp=(kh == NKH - 1)):
                    ins = t.transpose(b6[:, kh * 128:(kh + 1) * 128], xn_tm[:, kh * 128:(kh + 1) * 128], identb[:])
                    if sp:
                        ins.then_inc(s_pe.h, 1)
                P.em("pe", mk)
            xnT_done[ct] = pe_inc()
            xn_view = xn_fm[:].rearrange("p (kh t) -> p kh t", t=TOK)[:, :, ct * 128:(ct + 1) * 128]
            b6v = b6[:, 0:1024].rearrange("p (kh t) -> p kh t", t=128)
            xn_evac[ct] = dve_op(lambda v, xv=xn_view, bv=b6v: v.tensor_copy(xv, bv), (s_pe, xnT_done[ct]))
            b6_evac = xn_evac[ct]

            pairs = []
            for kh in range(NKH):
                pairs.append((
                    (lambda kh=kh, ct=ct: xn_fm[:, kh * TOK + ct * 128:kh * TOK + ct * 128 + 128]),
                    (lambda kh=kh: gwT[:, kh * 8:(kh + 1) * 8]),
                ))
            P.w("pe", s_dv, xn_evac[ct])
            P.w("pe", s_dv, gwt_rdy)
            gate_mm = mm_group(lambda: b3[:, 0:8], pairs)
            gsc = gtmp[:, 0:8]
            eq1 = gtmp[:, 8:16]
            scm = gtmp[:, 16:24]
            eq2 = gtmp[:, 24:32]
            dve_op(lambda v: v.tensor_copy(gsc, b3[:, 0:8]), (s_pe, gate_mm), edge=True)
            dve_op(lambda v: v.reduce_max(col(M_), gsc, axis=AX.X), edge=True)
            dve_op(lambda v: v.tensor_scalar(eq1, gsc, col(M_), None, op0=ALU.is_equal), edge=True)
            dve_op(lambda v: v.scalar_tensor_tensor(scm, eq1, -1e30, gsc, op0=ALU.mult, op1=ALU.add), edge=True)
            dve_op(lambda v: v.reduce_max(col(T1_), scm, axis=AX.X), edge=True)
            dve_op(lambda v: v.tensor_scalar(eq2, scm, col(T1_), None, op0=ALU.is_equal), edge=True)
            d_done = dve_op(lambda v: v.tensor_tensor(col(D_), col(T1_), col(M_), op=ALU.subtract), edge=True)
            P.w("act", s_dv, d_done)
            P.em("act", lambda a: a.activation(col(ED), col(D_), AF.Exp).then_inc(s_ac.h, 1))
            ed_done = ac_inc()
            dve_op(lambda v: v.tensor_scalar_add(col(DEN0), col(ED), 1.0), (s_ac, ed_done), edge=True)
            dve_op(lambda v: v.reciprocal(col(S1_), col(DEN0)), edge=True)
            dve_op(lambda v: v.tensor_tensor(col(S2_), col(ED), col(S1_), op=ALU.mult), edge=True)
            combv = combs[:, ct * E:(ct + 1) * E]
            dve_op(lambda v, cv=combv: v.tensor_scalar(cv, eq1, col(S1_), None, op0=ALU.mult), edge=True)
            return dve_op(lambda v, cv=combv: v.scalar_tensor_tensor(
                cv, eq2, col(S2_), cv, op0=ALU.mult, op1=ALU.add), edge=True)

        def kv_part(ct):
            nonlocal last_dve, part_mm, part_add, kv_war
            kv_banks = [b0, b1]
            for sl in range(4):
                bank = kv_banks[sl % 2]
                pairs = []
                for kh in range(NKH):
                    pairs.append((
                        (lambda kh=kh, ct=ct: xn_fm[:, kh * TOK + ct * 128:kh * TOK + ct * 128 + 128]),
                        (lambda kh=kh, sl=sl: wqTA(kh, 1024 + sl * 512, 512)),
                    ))
                P.w("pe", d_wq, wq_kv_rdy)
                P.w("pe", s_dv, kv_bank_free[sl % 2])
                kv_mm = mm_group(lambda b=bank: b[:], pairs)
                if sl < 2:
                    tpar = sl % 2
                    tmn = tmin[:, tpar * 512:(tpar + 1) * 512]
                    tex = texp[:, tpar * 512:(tpar + 1) * 512]
                    mn_done = dve_op(lambda v, b=bank, t=tmn: v.tensor_scalar_min(t, b[:], 0.0),
                                     (s_pe, kv_mm), (s_ac, exp_free[tpar]))
                    P.w("act", s_dv, mn_done)
                    def mke(a, tmn=tmn, tex=tex):
                        a.activation(tex, tmn, AF.Exp).then_inc(s_ac.h, 1)
                    P.em("act", mke)
                    exp_done = ac_inc()
                    exp_free[tpar] = exp_done
                    kslice = kp_tm[:, sl * 512:(sl + 1) * 512]
                    kv_bank_free[sl % 2] = dve_op(
                        lambda v, b=bank, tex=tex, ks=kslice: v.scalar_tensor_tensor(
                            ks, b[:], 0.0, tex, op0=ALU.max, op1=ALU.add),
                        (s_ac, exp_done), (s_pe, kv_war), edge=True)
                else:
                    vslice = v_tm[:, (sl - 2) * 512:(sl - 1) * 512]
                    kv_bank_free[sl % 2] = dve_op(lambda v, b=bank, vs=vslice: v.tensor_copy(vs, b[:]),
                                                  (s_pe, kv_mm), (s_pe, kv_war))
            kv_done = last_dve
            for hb in range(4):
                P.w("pe", s_dv, kv_done)
                if part_mm > 0:
                    P.w("pe", s_dv, part_add)
                for i in range(4):
                    h = hb * 4 + i
                    def mk1(t, h=h, i=i):
                        t.matmul(b3[0:64, 64 + i * 65:128 + i * 65], kp_tm[:, h * 64:(h + 1) * 64],
                                 v_tm[:, h * 64:(h + 1) * 64], start=True, stop=True)
                    P.em("pe", mk1)
                    def mk2(t, h=h, i=i, last=(i == 3)):
                        ins = t.matmul(b3[0:64, 128 + i * 65:129 + i * 65], kp_tm[:, h * 64:(h + 1) * 64],
                                       onesb[:], start=True, stop=True)
                        if last:
                            ins.then_inc(s_pe.h, 1)
                    P.em("pe", mk2)
                part_mm = pe_inc()
                spv = spart[:, hb * 4 * 65:(hb + 1) * 4 * 65]
                part_add = dve_op(lambda v, spv=spv: v.tensor_tensor(spv, spv, b3[0:64, 64:64 + 260], op=ALU.add),
                                  (s_pe, part_mm), (s_dv, spart_zero))
            kv_war = part_mm

        part_add = 0
        comb_done = ln_gate(0)
        comb_done = ln_gate(1)
        for ct in range(NCT):
            kv_part(ct)
            if ct + 2 < NCT:
                comb_done = ln_gate(ct + 2)
        p1_pe_end = s_pe.n
        partials_done = last_dve

        P.w("sync", s_dv, partials_done)
        P.em("sync", lambda s: s.dma_start(out=cc_in[:], in_=spart[:]).then_inc(d_cs.h, 16)); d_cs.n += 16
        P.w("gp", d_cs, d_cs.n)
        P.em("gp", lambda g: g.collective_compute(
            "AllGather", ALU.bypass, replica_groups=[[0, 1, 2, 3, 4, 5, 6, 7]],
            ins=[cc_in[:]], outs=[cc_out[:]]).then_inc(s_cc.h, 1)); s_cc.n += 1
        P.w("gp", s_cc, s_cc.n)
        P.w("dve", d_ct, consts_rdy)
        for r in range(8):
            def mkg(g, r=r):
                g.dma_start(out=sgath[:], in_=cc_out[r * 64:(r + 1) * 64, :]).then_inc(d_ms.h, 16)
            P.em("gp", mkg)
            d_ms.n += 16
            gath_rdy = d_ms.n
            acc_done = dve_op(lambda v, r=r: v.scalar_tensor_tensor(
                saug[:], sgath[:], wselb[:, r:r + 1], saug[:],
                op0=ALU.mult, op1=ALU.add), (d_ms, gath_rdy), edge=True)
            if r < 7:
                P.w("gp", s_dv, acc_done)
        saug_bf_rdy = dve_op(lambda v: v.tensor_copy(saugb[:], saug[:]), edge=True)

        ffn_state = {
            "sil_par": 0,
            "g_pe_done": [0, 0],
            "y_evac": [0, 0],
            "g_hist": [0, 0],
            "sil_free": [0, 0],
            "bH_free": [0, 0],
            "bU_free": [0, 0],
            "fjg": 0,
            "last_g_pe": 0,
            "w13_loaded": {},
            "w2_loaded": {},
            "span_pe_done": {},
        }

        def load_w13(k, wait_pe=0):
            e_, sp_ = divmod(k, NSP)
            slot = k % 2
            dw = d_w[slot]
            if wait_pe:
                P.w("gp", s_pe, wait_pe)
            def mk(g, e_=e_, sp_=sp_, slot=slot, dw=dw):
                src1 = w1[e_].rearrange("(kh p) f -> p kh f", p=128)[:, :, sp_ * SPAN:(sp_ + 1) * SPAN]
                dst1 = wreg[:, slot * NKH * SPAN:(slot + 1) * NKH * SPAN].rearrange("p (kh f) -> p kh f", f=SPAN)
                g.dma_start(out=dst1, in_=src1).then_inc(dw.h, 16)
                src3 = w3[e_].rearrange("(kh p) f -> p kh f", p=128)[:, :, sp_ * SPAN:(sp_ + 1) * SPAN]
                dst3 = wreg[:, 8192 + slot * NKH * SPAN:8192 + (slot + 1) * NKH * SPAN].rearrange(
                    "p (kh f) -> p kh f", f=SPAN)
                g.dma_start(out=dst3, in_=src3).then_inc(dw.h, 16)
            P.em("gp", mk)
            dw.n += 32
            ffn_state["w13_loaded"][k] = dw.n

        def load_w2s(k, wait_pe=0):
            e_, sp_ = divmod(k, NSP)
            slot = k % 2
            dw = d_w2[slot]
            if wait_pe:
                P.w("gp", s_pe, wait_pe)
            def mk2(g, e_=e_, sp_=sp_, slot=slot, dw=dw):
                src2 = w2[e_].rearrange("(kf p) n -> p kf n", p=128)[:, sp_ * NFJ:(sp_ + 1) * NFJ, :]
                dst2 = wreg[:, 16384 + slot * NFJ * DM:16384 + (slot + 1) * NFJ * DM].rearrange(
                    "p (kf n) -> p kf n", n=DM)
                g.dma_start(out=dst2, in_=src2).then_inc(dw.h, 16)
            P.em("gp", mk2)
            dw.n += 16
            ffn_state["w2_loaded"][k] = dw.n

        def ffn_chunk(tc, pe_gate, pre_dv=(), pre_ac=()):
            st = ffn_state
            if tc == 1:
                P.w("sync", d_st, 16)
            xsrc = xs.rearrange("(tt p) h -> p tt h", p=128)[:, tc * NTT:(tc + 1) * NTT, :]
            P.em("sync", lambda s, xsrc=xsrc: s.dma_start(
                out=y_acc[:].rearrange("p (tt h) -> p tt h", h=DM), in_=xsrc).then_inc(d_yx.h, 16))
            d_yx.n += 16
            yx_rdy = d_yx.n

            st["w13_loaded"] = {}
            st["w2_loaded"] = {}
            st["span_pe_done"] = {}
            if tc == 0:
                load_w13(0)
                load_w13(1)
                load_w2s(0, wait_pe=pe_gate)
                load_w2s(1)
            else:
                load_w13(0, wait_pe=pe_gate)
                load_w13(1)
                load_w2s(0)
                load_w2s(1)

            bH = [b0, b4]
            bU = [b1, b5]
            for k in range(NSPANS):
                e_, sp_ = divmod(k, NSP)
                slot = k % 2
                half = k % 2
                P.w("pe", d_w[slot], st["w13_loaded"][k])
                P.w("pe", s_dv, st["y_evac"][1])
                if k == 0:
                    P.w("pe", s_dv, xn_evac[NTT * (tc + 1) - 1])
                    for v_ in pre_dv:
                        P.w("pe", s_dv, v_)
                    for v_ in pre_ac:
                        P.w("pe", s_ac, v_)
                for tg in range(2):
                    gpar = tg % 2
                    tof = tc * CTOK + tg * 512
                    for fj in range(NFJ):
                        spar = st["sil_par"]
                        bp = st["fjg"] % 2
                        P.w("pe", s_dv, st["g_hist"][bp])
                        pairs = [((lambda kh=kh, slot=slot, fj=fj: w1b(slot, kh, fj * 128, 128)),
                                  (lambda kh=kh, tof=tof: xn_fm[:, kh * TOK + tof:kh * TOK + tof + 512]))
                                 for kh in range(NKH)]
                        h_done = mm_group(lambda bp=bp: bH[bp][:], pairs)
                        P.w("act", s_pe, h_done)
                        P.w("act", s_dv, st["sil_free"][spar])
                        def mks(a, spar=spar, bp=bp):
                            a.activation(sil[:, spar * 512:(spar + 1) * 512], bH[bp][:], AF.Silu).then_inc(s_ac.h, 1)
                        P.em("act", mks)
                        st["bH_free"][bp] = ac_inc()
                        sil_done = s_ac.n
                        pairs = [((lambda kh=kh, slot=slot, fj=fj: w3b(slot, kh, fj * 128, 128)),
                                  (lambda kh=kh, tof=tof: xn_fm[:, kh * TOK + tof:kh * TOK + tof + 512]))
                                 for kh in range(NKH)]
                        u_done = mm_group(lambda bp=bp: bU[bp][:], pairs)
                        gd = gbuf[:, (gpar * 8 + half * NFJ + fj) * 512:(gpar * 8 + half * NFJ + fj) * 512 + 512]
                        g_done = dve_op(
                            lambda v, gd=gd, spar=spar, bp=bp: v.tensor_tensor(
                                gd, sil[:, spar * 512:(spar + 1) * 512], bU[bp][:], op=ALU.mult),
                            (s_ac, sil_done), (s_pe, u_done), (s_pe, st["g_pe_done"][gpar]))
                        st["g_hist"][bp] = g_done
                        st["bU_free"][bp] = g_done
                        st["sil_free"][spar] = g_done
                        st["sil_par"] ^= 1
                        st["fjg"] += 1
                if k + 2 < NSPANS:
                    load_w13(k + 2, wait_pe=s_pe.n)
                if k % 2 == 0:
                    continue
                g_rdy = s_dv.n
                P.w("pe", d_w2[0], st["w2_loaded"][k - 1])
                P.w("pe", d_w2[1], st["w2_loaded"][k])
                P.w("pe", s_dv, g_rdy)
                for tg in range(2):
                    gpar = tg % 2
                    for tt in range(4):
                        ttg = tg * 4 + tt
                        ct_g = tc * NTT + ttg
                        ypar = ttg % 2
                        ybuf = py if ypar == 0 else py2
                        P.w("pe", s_dv, st["y_evac"][ypar])
                        for ht in range(2):
                            pairs = []
                            for hf in range(2):
                                for fj in range(NFJ):
                                    pairs.append((
                                        (lambda fj=fj, gpar=gpar, tt=tt, hf=hf: gbuf[
                                            :, (gpar * 8 + hf * NFJ + fj) * 512 + tt * 128:
                                               (gpar * 8 + hf * NFJ + fj) * 512 + tt * 128 + 128]),
                                        (lambda fj=fj, hf=hf, ht=ht: w2b(hf, fj, ht * 512, 512)),
                                    ))
                            ymm = mm_group(lambda ht=ht, ybuf=ybuf: ybuf[:, ht * 512:(ht + 1) * 512], pairs)
                        dst = y_acc[:, ttg * DM:ttg * DM + DM]
                        cslice = combs[:, ct_g * E + e_:ct_g * E + e_ + 1]
                        st["y_evac"][ypar] = dve_op(
                            lambda v, dst=dst, cs=cslice, ybuf=ybuf: v.scalar_tensor_tensor(
                                dst, ybuf[:], cs, dst, op0=ALU.mult, op1=ALU.add),
                            (s_pe, ymm), (d_yx, yx_rdy), (s_dv, comb_done))
                    st["g_pe_done"][gpar] = s_pe.n
                g_end = s_pe.n
                st["span_pe_done"][k - 1] = g_end
                st["span_pe_done"][k] = g_end
                if k + 2 < NSPANS:
                    load_w2s(k + 1, wait_pe=g_end)
                    load_w2s(k + 2)
            return s_pe.n
            return s_pe.n

        scan_state = {
            "msc_free": [0, 0],
            "rec_free": [0, 0],
            "b2_free": 0,
            "b4_free": 0,
            "b5_free": 0,
            "b3_free": 0,
            "qkv_bank_free": [0, 0],
            "exp_free": [0, 0],
            "sc_free": [0, 0],
            "nd_free": [0, 0],
            "saug_rdy_pair": [0] * (NH // 2),
            "qp_war": 0,
            "kv_war": 0,
        }

        def scan_chunk(tc, wq_rdy, pre_ac=0, pre_dv=0):
            nonlocal b6_evac, saug_bf_rdy
            ss = scan_state
            proj_evac = 0
            for ctl in range(8):
                ct = tc * 8 + ctl
                for sl in range(2):
                    bank = [b0, b1][sl % 2]
                    pairs = [((lambda kh=kh, ct=ct: xn_fm[:, kh * TOK + ct * 128:kh * TOK + ct * 128 + 128]),
                              (lambda kh=kh, sl=sl: wqT(kh, sl * 512, 512)))
                             for kh in range(NKH)]
                    P.w("pe", d_wq, wq_rdy)
                    P.w("pe", s_dv, ss["qkv_bank_free"][sl % 2])
                    if ctl == 0 and sl == 0:
                        P.w("pe", s_ac, pre_ac)
                        P.w("pe", s_dv, pre_dv)
                    q_mm = mm_group(lambda b=bank: b[:], pairs)
                    tpar = sl % 2
                    tmn = tmin[:, tpar * 512:(tpar + 1) * 512]
                    tex = texp[:, tpar * 512:(tpar + 1) * 512]
                    mn_done = dve_op(lambda v, b=bank, t=tmn: v.tensor_scalar_min(t, b[:], 0.0),
                                     (s_pe, q_mm), (s_ac, ss["exp_free"][tpar]))
                    P.w("act", s_dv, mn_done)
                    def mke(a, tmn=tmn, tex=tex):
                        a.activation(tex, tmn, AF.Exp).then_inc(s_ac.h, 1)
                    P.em("act", mke)
                    exp_done = ac_inc()
                    ss["exp_free"][tpar] = exp_done
                    qslice = qp_tm[:, sl * 512:(sl + 1) * 512]
                    ss["qkv_bank_free"][sl % 2] = dve_op(
                        lambda v, b=bank, tex=tex, qs=qslice: v.scalar_tensor_tensor(
                            qs, b[:], 0.0, tex, op0=ALU.max, op1=ALU.add),
                        (s_ac, exp_done), (s_pe, ss["qp_war"]), edge=True)
                for sl in range(4):
                    bank = [b0, b1][sl % 2]
                    pairs = [((lambda kh=kh, ct=ct: xn_fm[:, kh * TOK + ct * 128:kh * TOK + ct * 128 + 128]),
                              (lambda kh=kh, sl=sl: wqT(kh, 1024 + sl * 512, 512)))
                             for kh in range(NKH)]
                    P.w("pe", s_dv, ss["qkv_bank_free"][sl % 2])
                    kv_mm = mm_group(lambda b=bank: b[:], pairs)
                    if sl < 2:
                        tpar = sl % 2
                        tmn = tmin[:, tpar * 512:(tpar + 1) * 512]
                        tex = texp[:, tpar * 512:(tpar + 1) * 512]
                        mn_done = dve_op(lambda v, b=bank, t=tmn: v.tensor_scalar_min(t, b[:], 0.0),
                                         (s_pe, kv_mm), (s_ac, ss["exp_free"][tpar]))
                        P.w("act", s_dv, mn_done)
                        def mke(a, tmn=tmn, tex=tex):
                            a.activation(tex, tmn, AF.Exp).then_inc(s_ac.h, 1)
                        P.em("act", mke)
                        exp_done = ac_inc()
                        ss["exp_free"][tpar] = exp_done
                        kslice = kp_tm[:, sl * 512:(sl + 1) * 512]
                        ss["qkv_bank_free"][sl % 2] = dve_op(
                            lambda v, b=bank, tex=tex, ks=kslice: v.scalar_tensor_tensor(
                                ks, b[:], 0.0, tex, op0=ALU.max, op1=ALU.add),
                            (s_ac, exp_done), (s_pe, ss["kv_war"]), edge=True)
                    else:
                        vslice = v_tm[:, (sl - 2) * 512:(sl - 1) * 512]
                        ss["qkv_bank_free"][sl % 2] = dve_op(
                            lambda v, b=bank, vs=vslice: v.tensor_copy(vs, b[:]), (s_pe, kv_mm), (s_pe, ss["kv_war"]))
                qkv_rdy = last_dve

                def tr_batch(srct, dstv, half):
                    nonlocal b6_evac
                    P.w("pe", s_dv, qkv_rdy)
                    P.w("pe", s_dv, b6_evac)
                    for hh in range(8):
                        h = half * 8 + hh
                        def mk(t, srct=srct, h=h, hh=hh, sp=(hh == 7)):
                            ins = t.transpose(b6[0:64, hh * 128:(hh + 1) * 128],
                                              srct[:, h * 64:(h + 1) * 64], identb[:])
                            if sp:
                                ins.then_inc(s_pe.h, 1)
                        P.em("pe", mk)
                    trd = pe_inc()
                    b6_evac = dve_op(lambda v, dstv=dstv, half=half: v.tensor_copy(
                        dstv[:, half * 1024:(half + 1) * 1024], b6[0:64, 0:1024]), (s_pe, trd))
                    return b6_evac

                tr_batch(qp_tm, qp_fm, 0)
                fm_rdy0 = tr_batch(kp_tm, kp_fm, 0)
                fm_pending = [(qp_tm, qp_fm, 1), (kp_tm, kp_fm, 1)]
                fm_rdy = {0: fm_rdy0, 1: None}
                ss["qp_war"] = s_pe.n

                sc_banks = [py[:, 0:256], b7[:, 0:256]]
                nd_banks = [py2[:, 0:128], b0[:, 0:128]]

                def emit_sc(p):
                    bankap = sc_banks[p % 2]
                    P.w("pe", s_dv, ss["sc_free"][p % 2])
                    P.w("pe", s_dv, fm_rdy[1] if p >= 4 else fm_rdy[0])
                    for i in range(2):
                        h = 2 * p + i
                        def mk(t, h=h, i=i, bankap=bankap):
                            ins = t.matmul(bankap[:, i * 128:(i + 1) * 128],
                                           kp_fm[:, h * 128:(h + 1) * 128],
                                           qp_fm[:, h * 128:(h + 1) * 128], start=True, stop=True)
                            if i == 1:
                                ins.then_inc(s_pe.h, 1)
                        P.em("pe", mk)
                    return pe_inc()

                sc_mm = {}
                sc_mm[0] = emit_sc(0)
                sc_mm[1] = emit_sc(1)

                def emit_mask(p):
                    ppar = p % 2
                    bankap = sc_banks[ppar]
                    m0 = msc[:, ppar * 256:ppar * 256 + 128]
                    m1 = msc[:, ppar * 256 + 128:ppar * 256 + 256]
                    P.w("dve", s_pe, ss["msc_free"][ppar])
                    dve_op(lambda v, m0=m0, bankap=bankap: v.tensor_tensor(m0, bankap[:, 0:128], maskb[:], op=ALU.mult),
                           (s_pe, sc_mm[p]), (d_ct, consts_rdy))
                    md = dve_op(lambda v, m1=m1, bankap=bankap: v.tensor_tensor(m1, bankap[:, 128:256], maskb[:], op=ALU.mult))
                    ss["sc_free"][ppar] = md
                    ss["b2_free"] = md
                    return md

                mask_done = {0: emit_mask(0)}
                for p in range(NH // 2):
                    ppar = p % 2
                    ndb = nd_banks[ppar]
                    if p in (0, 1) and fm_pending:
                        srct, dstv, half = fm_pending.pop(0)
                        fm_rdy[1] = tr_batch(srct, dstv, half)
                        ss["qp_war"] = s_pe.n
                    P.w("pe", s_dv, mask_done[p])
                    P.w("pe", s_ac, ss["nd_free"][ppar])
                    P.w("pe", s_dv, ss["saug_rdy_pair"][p])
                    for i in range(2):
                        h = 2 * p + i
                        def mkn1(t, h=h, i=i, ppar=ppar, ndb=ndb):
                            t.matmul(ndb[:, i * 64:(i + 1) * 64],
                                     msc[:, ppar * 256 + i * 128:ppar * 256 + (i + 1) * 128],
                                     v_tm[:, h * 64:(h + 1) * 64], start=True, stop=False)
                        P.em("pe", mkn1)
                        def mkn2(t, h=h, i=i, ndb=ndb, last=(i == 1)):
                            ins = t.matmul(ndb[:, i * 64:(i + 1) * 64], qp_fm[:, h * 128:(h + 1) * 128],
                                           saugb[:, h * 65:h * 65 + 64], start=False, stop=True)
                            if last:
                                ins.then_inc(s_pe.h, 1)
                        P.em("pe", mkn2)
                    num_mm = pe_inc()
                    P.w("pe", s_dv, ss["b5_free"])
                    for i in range(2):
                        h = 2 * p + i
                        def mkd1(t, i=i, ppar=ppar):
                            t.matmul(b5[:, i:i + 1], msc[:, ppar * 256 + i * 128:ppar * 256 + (i + 1) * 128],
                                     onesb[:], start=True, stop=False)
                        P.em("pe", mkd1)
                        def mkd2(t, h=h, i=i, last=(i == 1)):
                            ins = t.matmul(b5[:, i:i + 1], qp_fm[:, h * 128:(h + 1) * 128],
                                           saugb[:, h * 65 + 64:h * 65 + 65], start=False, stop=True)
                            if last:
                                ins.then_inc(s_pe.h, 1)
                        P.em("pe", mkd2)
                    den_mm = pe_inc()
                    ss["msc_free"][ppar] = den_mm
                    P.w("pe", s_dv, ss["b3_free"])
                    for i in range(2):
                        h = 2 * p + i
                        def mks1(t, h=h, i=i):
                            t.matmul(py[0:64, 512 + 64 + i * 65:512 + 128 + i * 65],
                                     kp_tm[:, h * 64:(h + 1) * 64], v_tm[:, h * 64:(h + 1) * 64],
                                     start=True, stop=True)
                        P.em("pe", mks1)
                        def mks2(t, h=h, i=i, last=(i == 1)):
                            ins = t.matmul(py[0:64, 512 + 128 + i * 65:512 + 129 + i * 65],
                                           kp_tm[:, h * 64:(h + 1) * 64], onesb[:], start=True, stop=True)
                            if last:
                                ins.then_inc(s_pe.h, 1)
                        P.em("pe", mks2)
                    ds_mm = pe_inc()
                    if p + 2 < NH // 2:
                        sc_mm[p + 2] = emit_sc(p + 2)
                    rc = 21 + 2 * ppar
                    dn = 17 + 2 * ppar
                    P.w("dve", s_ac, ss["rec_free"][ppar])
                    dve_op(lambda v, dn=dn: v.tensor_scalar_add(lns[:, dn:dn + 2], b5[:, 0:2], EPS_DEN),
                           (s_pe, den_mm))
                    ss["b5_free"] = last_dve
                    rec_done = dve_op(lambda v, rc=rc, dn=dn: v.reciprocal(lns[:, rc:rc + 2], lns[:, dn:dn + 2]),
                                      edge=True)
                    if p + 1 < NH // 2:
                        mask_done[p + 1] = emit_mask(p + 1)
                    for i in range(2):
                        h = 2 * p + i
                        P.w("act", s_pe, num_mm)
                        P.w("act", s_dv, rec_done)
                        def mky(a, h=h, i=i, rc=rc, ndb=ndb):
                            a.activation(y_tm[:, h * 64:(h + 1) * 64], ndb[:, i * 64:(i + 1) * 64], AF.Identity,
                                         scale=lns[:, rc + i:rc + i + 1]).then_inc(s_ac.h, 1)
                        P.em("act", mky)
                        ac_inc()
                    ss["nd_free"][ppar] = s_ac.n
                    ss["rec_free"][ppar] = s_ac.n
                    sgv = saug[:, 2 * p * 65:(2 * p + 2) * 65]
                    dve_op(lambda v, sgv=sgv: v.tensor_tensor(sgv, sgv, py[0:64, 512 + 64:512 + 194], op=ALU.add),
                           (s_pe, ds_mm), (s_dv, ss["saug_rdy_pair"][p]))
                    ss["b3_free"] = last_dve
                    sgbv = saugb[:, 2 * p * 65:(2 * p + 2) * 65]
                    ss["saug_rdy_pair"][p] = dve_op(lambda v, sgv=sgv, sgbv=sgbv: v.tensor_copy(sgbv, sgv), edge=True)
                    ss["kv_war"] = ds_mm
                y_all = s_ac.n

                P.w("pe", s_ac, y_all)
                P.w("pe", s_dv, b6_evac)
                for j in range(NKH):
                    def mk(t, j=j, sp=(j == NKH - 1)):
                        ins = t.transpose(b6[:, j * 128:(j + 1) * 128], y_tm[:, j * 128:(j + 1) * 128], identb[:])
                        if sp:
                            ins.then_inc(s_pe.h, 1)
                    P.em("pe", mk)
                ytr = pe_inc()
                b6_evac = dve_op(lambda v: v.tensor_copy(y_fm[:], b6[:, 0:1024]), (s_pe, ytr))
                yfm_rdy = b6_evac
                for cs in range(2):
                    bank = b7 if cs == 0 else b2
                    bid = "b7" if cs == 0 else "b2"
                    pairs = [((lambda j=j: y_fm[:, j * 128:(j + 1) * 128]),
                              (lambda j=j, cs=cs: wpT(j, cs * 512, 512)))
                             for j in range(NKH)]
                    P.w("pe", s_dv, yfm_rdy)
                    if cs == 1:
                        P.w("pe", s_dv, ss["b2_free"])
                        P.w("pe", s_dv, proj_evac)
                    else:
                        P.w("pe", s_dv, proj_evac)
                    pj_mm = mm_group(lambda b=bank: b[:], pairs)
                    dst = y_acc[:, ctl * DM + cs * 512:ctl * DM + cs * 512 + 512]
                    proj_evac = dve_op(lambda v, dst=dst, b=bank: v.scalar_tensor_tensor(
                        dst, b[:], 1.0, dst, op0=ALU.mult, op1=ALU.add), (s_pe, pj_mm))
                    if cs == 1:
                        ss["b2_free"] = proj_evac
            return proj_evac

        ffn0_pe_end = ffn_chunk(0, pe_gate=p1_pe_end, pre_dv=(kv_bank_free[0], kv_bank_free[1]))
        P.w("sync", s_pe, ffn0_pe_end)
        for kh in range(NKH):
            def mk(s, kh=kh):
                s.dma_start(out=wqT(kh, 0, 3072), in_=wattn_bf[:, kh * 128:(kh + 1) * 128],
                            transpose=True).then_inc(d_wq.h, 16)
            P.em("sync", mk)
            d_wq.n += 16
        for kh in range(NKH):
            def mk(s, kh=kh):
                s.dma_start(out=wpT(kh, 0, 1024), in_=wp_bf[:, kh * 128:(kh + 1) * 128],
                            transpose=True).then_inc(d_wq.h, 16)
            P.em("sync", mk)
            d_wq.n += 16
        wq_full0 = d_wq.n
        scan_state["kv_war"] = p1_pe_end
        scan_state["saug_rdy_pair"] = [saug_bf_rdy] * (NH // 2)
        scan_state["b5_free"] = max(max(ffn_state["bU_free"]), max(ffn_state["y_evac"]))
        scan0_end = scan_chunk(0, wq_full0, pre_ac=max(ffn_state["bH_free"]),
                               pre_dv=max(max(ffn_state["bU_free"]), max(ffn_state["y_evac"])))
        P.w("sync", s_dv, scan0_end)
        yo0 = yout.rearrange("(tt p) h -> p tt h", p=128)[:, 0:NTT, :]
        P.em("sync", lambda s, yo0=yo0: s.dma_start(
            out=yo0, in_=y_acc[:].rearrange("p (tt h) -> p tt h", h=DM)).then_inc(d_st.h, 16))
        d_st.n += 16

        scan0_pe_end = s_pe.n
        ffn1_pe_end = ffn_chunk(1, pe_gate=scan0_pe_end,
                                pre_dv=(scan_state["qkv_bank_free"][0], scan_state["qkv_bank_free"][1],
                                        scan_state["b5_free"], scan_state["b3_free"], scan_state["b2_free"]),
                                pre_ac=(scan_state["nd_free"][0], scan_state["nd_free"][1],
                                        scan_state["rec_free"][0], scan_state["rec_free"][1]))
        P.w("sync", s_pe, ffn1_pe_end)
        for kh in range(NKH):
            def mk(s, kh=kh):
                s.dma_start(out=wqT(kh, 0, 3072), in_=wattn_bf[:, kh * 128:(kh + 1) * 128],
                            transpose=True).then_inc(d_wq.h, 16)
            P.em("sync", mk)
            d_wq.n += 16
        for kh in range(NKH):
            def mk(s, kh=kh):
                s.dma_start(out=wpT(kh, 0, 1024), in_=wp_bf[:, kh * 128:(kh + 1) * 128],
                            transpose=True).then_inc(d_wq.h, 16)
            P.em("sync", mk)
            d_wq.n += 16
        wq_full1 = d_wq.n
        scan_state["b5_free"] = max(scan_state["b5_free"], max(ffn_state["bU_free"]), max(ffn_state["y_evac"]))
        scan1_end = scan_chunk(1, wq_full1, pre_ac=max(ffn_state["bH_free"]),
                               pre_dv=max(max(ffn_state["bU_free"]), max(ffn_state["y_evac"])))
        P.w("sync", s_dv, scan1_end)
        yo1 = yout.rearrange("(tt p) h -> p tt h", p=128)[:, NTT:2 * NTT, :]
        P.em("sync", lambda s, yo1=yo1: s.dma_start(
            out=yo1, in_=y_acc[:].rearrange("p (tt h) -> p tt h", h=DM)).then_inc(d_st.h, 16))
        d_st.n += 16
        P.w("sync", d_st, d_st.n)

        with nc.Block() as block:
            @block.tensor
            def _(t):
                for f in P.ops["pe"]:
                    f(t)

            @block.scalar
            def _(a):
                for f in P.ops["act"]:
                    f(a)

            @block.vector
            def _(v):
                for f in P.ops["dve"]:
                    f(v)

            @block.sync
            def _(s):
                for f in P.ops["sync"]:
                    f(s)

            @block.gpsimd
            def _(g):
                for f in P.ops["gp"]:
                    f(g)

    return nc


_NC_CACHE = {}


def kernel(L0_attn_x, L0_attn_w_attn, L0_attn_w_proj, L0_ffn_gate_w, L0_ffn_w1, L0_ffn_w3, L0_ffn_w2):
    x = np.asarray(L0_attn_x)
    w_attn = np.ascontiguousarray(np.asarray(L0_attn_w_attn), dtype=np.float32)
    w_proj = np.ascontiguousarray(np.asarray(L0_attn_w_proj), dtype=np.float32)
    gate_w = np.ascontiguousarray(np.asarray(L0_ffn_gate_w), dtype=np.float32)
    w1 = np.ascontiguousarray(np.asarray(L0_ffn_w1), dtype=np.float32)
    w3 = np.ascontiguousarray(np.asarray(L0_ffn_w3), dtype=np.float32)
    w2 = np.ascontiguousarray(np.asarray(L0_ffn_w2), dtype=np.float32)

    ident = np.eye(128, dtype=ml_dtypes.bfloat16)
    maskT = np.triu(np.ones((128, 128))).astype(ml_dtypes.bfloat16)
    onesc = np.ones((128, 1), dtype=ml_dtypes.bfloat16)

    if "nc" not in _NC_CACHE:
        _NC_CACHE["nc"] = build_nc()
    nc = _NC_CACHE["nc"]

    in_maps = []
    for c in range(8):
        b, seg = divmod(c, 4)
        xs = np.ascontiguousarray(x[b, seg * TOK:(seg + 1) * TOK, :], dtype=np.float32)
        wsel = np.zeros((64, 8), np.float32)
        for r in range(8):
            rb, rseg = divmod(r, 4)
            if rb == b and rseg < seg:
                wsel[:, r] = 1.0
        in_maps.append({
            "xs": xs, "w_attn": w_attn, "w_proj": w_proj, "gate_w": gate_w,
            "w1": w1, "w3": w3, "w2": w2,
            "ident": ident, "maskT": maskT, "onesc": onesc, "wsel": wsel,
        })

    res = run_bass_kernel_spmd(nc, in_maps, core_ids=list(range(8)))
    out = np.empty((2, 8192, 1024), np.float32)
    for c in range(8):
        b, seg = divmod(c, 4)
        out[b, seg * TOK:(seg + 1) * TOK, :] = np.asarray(res.results[c]["yout"])
    return out
